# version 01
# baseline (speedup 1.0000x reference)
import contextlib
import numpy as np
import ml_dtypes

import concourse.bass as bass
import concourse.mybir as mybir
from concourse.bass_utils import run_bass_kernel_spmd

F32 = mybir.dt.float32
BF16 = mybir.dt.bfloat16
I32 = mybir.dt.int32
AF = mybir.ActivationFunctionType
ALU = mybir.AluOpType
AX = mybir.AxisListType

NCORES = 8
B = 2
S = 8192
T = B * S
D = 1024
TPC = T // NCORES
BF = ml_dtypes.bfloat16

ENGINES = ("tensor", "vector", "scalar", "gpsimd", "sync")
STRICT = True


class _Op:
    __slots__ = ("engine", "fn", "reads", "writes", "is_dma", "idx", "deps")

    def __init__(self, engine, fn, reads, writes, is_dma):
        self.engine = engine
        self.fn = fn
        self.reads = reads
        self.writes = writes
        self.is_dma = is_dma
        self.idx = None
        self.deps = None


class Pgm:
    def __init__(self, nc, strict=False):
        self.nc = nc
        self.ops = []
        self.strict = strict
        self.last_writer = {}
        self.readers = {}

    def op(self, engine, fn, reads=(), writes=()):
        self._add(_Op(engine, fn, tuple(reads), tuple(writes), False))

    def dma(self, engine, out_ap, in_ap, reads=(), writes=(), **kw):
        def fn(eng, out_ap=out_ap, in_ap=in_ap, kw=kw):
            return eng.dma_start(out=out_ap, in_=in_ap, **kw)
        self._add(_Op(engine, fn, tuple(reads), tuple(writes), True))

    def _add(self, op):
        deps = []
        for r in op.reads:
            w = self.last_writer.get(r)
            if w is not None:
                deps.append(w)
        for w_ in op.writes:
            w = self.last_writer.get(w_)
            if w is not None:
                deps.append(w)
            for r in self.readers.get(w_, ()):
                deps.append(r)
        op.deps = deps
        for r in op.reads:
            self.readers.setdefault(r, []).append(op)
        for w_ in op.writes:
            self.last_writer[w_] = op
            self.readers[w_] = []
        self.ops.append(op)

    def emit(self):
        nc = self.nc
        counts = {e: 0 for e in ENGINES}
        dcounts = {e: 0 for e in ENGINES}
        for op in self.ops:
            if op.is_dma:
                op.idx = dcounts[op.engine]
                dcounts[op.engine] += 1
            else:
                op.idx = counts[op.engine]
                counts[op.engine] += 1
        stack = contextlib.ExitStack()
        csem = {}
        dsem = {}
        for e in ENGINES:
            if counts[e]:
                csem[e] = stack.enter_context(nc.semaphore(f"c_{e}"))
            if dcounts[e]:
                dsem[e] = stack.enter_context(nc.semaphore(f"d_{e}"))
        ops_by_engine = {e: [op for op in self.ops if op.engine == e] for e in ENGINES}
        tail_eng = "sync" if ops_by_engine["sync"] else (
            "gpsimd" if ops_by_engine["gpsimd"] else "vector")
        with stack:
            with nc.Block() as block:
                for e in ENGINES:
                    if not ops_by_engine[e] and e != tail_eng:
                        continue

                    def section(eng, e=e):
                        hwm_c = {x: 0 for x in ENGINES}
                        hwm_d = {x: 0 for x in ENGINES}
                        for op in ops_by_engine[e]:
                            need_c = {}
                            need_d = {}
                            for dep in op.deps:
                                if dep.is_dma:
                                    v = 16 * (dep.idx + 1)
                                    if v > need_d.get(dep.engine, 0):
                                        need_d[dep.engine] = v
                                else:
                                    if dep.engine == e and not self.strict:
                                        continue
                                    v = dep.idx + 1
                                    if v > need_c.get(dep.engine, 0):
                                        need_c[dep.engine] = v
                            for x, v in need_c.items():
                                if v > hwm_c[x]:
                                    eng.wait_ge(csem[x], v)
                                    hwm_c[x] = v
                            for x, v in need_d.items():
                                if v > hwm_d[x]:
                                    eng.wait_ge(dsem[x], v)
                                    hwm_d[x] = v
                            inst = op.fn(eng)
                            if op.is_dma:
                                inst.then_inc(dsem[e], 16)
                            else:
                                inst.then_inc(csem[e], 1)
                        if e == tail_eng:
                            for x in ENGINES:
                                if dcounts[x]:
                                    eng.wait_ge(dsem[x], 16 * dcounts[x])
                                if counts[x] and x != e:
                                    eng.wait_ge(csem[x], counts[x])

                    getattr(block, e)(section)
        return nc


def _bf(x):
    return np.ascontiguousarray(x).astype(BF)


def _f32(x):
    return np.ascontiguousarray(np.asarray(x, dtype=np.float32))


class Ctx:

    def __init__(self, name):
        self.nc = bass.Bass("TRN2", target_bir_lowering=False)
        self.p = Pgm(self.nc, strict=STRICT)
        self.st = contextlib.ExitStack()
        self._uid = 0
        self._csts = {}

    def cst(self, val):
        key = float(val)
        if key not in self._csts:
            t = self.sb(f"cst{len(self._csts)}", [128, 1], F32)
            k = f"cst_{len(self._csts)}"
            self.p.op("vector", lambda v, t=t, val=key: v.memset(t[:, :], val),
                      writes=[k])
            self._csts[key] = (t, k)
        return self._csts[key]

    def sb(self, name, shape, dt=F32):
        return self.st.enter_context(self.nc.sbuf_tensor(name, list(shape), dt))

    def ps(self, name, shape, dt=F32):
        return self.st.enter_context(self.nc.psum_tensor(name, list(shape), dt))

    def inp(self, name, shape, dt=F32):
        return self.nc.dram_tensor(name, list(shape), dt, kind="ExternalInput").ap()

    def out(self, name, shape, dt=F32):
        return self.nc.dram_tensor(name, list(shape), dt, kind="ExternalOutput").ap()

    def uid(self):
        self._uid += 1
        return self._uid

    def finish(self):
        self.p.emit()
        self.st.close()
        return self.nc


def ln_tile(ctx, x_ap, xn_ap, scratch, eps=1e-5, n=1024):
    p = ctx.p
    s1, negm, mm, var, rstd, sq = scratch["tiles"]
    ks1, knegm, kmm, kvar, krstd, ksq = scratch["keys"]
    kin, kout = scratch["kin"], scratch["kout"]
    p.op("vector", lambda v: v.reduce_sum(s1, x_ap, axis=AX.X), reads=[kin], writes=[ks1])
    p.op("vector", lambda v: v.tensor_scalar(negm, s1, -1.0 / n, None, op0=ALU.mult),
         reads=[ks1], writes=[knegm])
    p.op("scalar", lambda s: s.activation(sq, x_ap, AF.Square), reads=[kin], writes=[ksq])
    p.op("vector", lambda v: v.reduce_sum(s1, sq, axis=AX.X), reads=[ksq], writes=[ks1])
    p.op("vector", lambda v: v.tensor_tensor(mm, negm, negm, op=ALU.mult),
         reads=[knegm], writes=[kmm])
    p.op("vector", lambda v: v.scalar_tensor_tensor(var, s1, 1.0 / n, mm,
                                                    op0=ALU.mult, op1=ALU.subtract),
         reads=[ks1, kmm], writes=[kvar])
    ecol, ekey = ctx.cst(eps)
    p.op("scalar", lambda s: s.activation(var, var, AF.Ln, bias=ecol[:, :]),
         reads=[kvar, ekey], writes=[kvar])
    p.op("scalar", lambda s: s.activation(rstd, var, AF.Exp, scale=-0.5),
         reads=[kvar], writes=[krstd])
    p.op("vector", lambda v: v.tensor_scalar(xn_ap, x_ap, negm, rstd,
                                             op0=ALU.add, op1=ALU.mult),
         reads=[kin, knegm, krstd], writes=[kout])


MOE_NE = 16
MOE_SHARED = True


def build_moe():
    ctx = Ctx("moe")
    nc, p = ctx.nc, ctx.p
    x = ctx.inp("x", [TPC, D])
    wr = ctx.inp("wr", [D, 16])
    wg = ctx.inp("wg", [16, D, 512])
    wu = ctx.inp("wu", [16, D, 512])
    wd = ctx.inp("wd", [16, 512, D])
    sg = ctx.inp("sg", [D, D])
    su = ctx.inp("su", [D, D])
    sd = ctx.inp("sd", [D, D])
    ident = ctx.inp("ident", [128, 128])
    xout = ctx.out("xout", [TPC, D])
    dbg_cw = ctx.out("dbg_cw", [128, (TPC // 128) * 16], F32)
    dbg_xnt = ctx.out("dbg_xnt", [128, 8 * TPC], BF16)

    NT = TPC // 128
    NTC = TPC // 512

    xnT = ctx.sb("xnT", [128, 8 * TPC], BF16)
    xn32t = ctx.sb("xn32t", [128, 1024], F32)
    acc = ctx.sb("acc", [128, NT * D], F32)
    cw = ctx.sb("cw", [128, NT * 16], F32)
    x_sb = ctx.sb("x_sb", [128, D], F32)
    xn_sb = ctx.sb("xn_sb", [128, D], F32)
    iden32 = ctx.sb("iden32", [128, 128], F32)
    lnt = [ctx.sb(f"lnt{i}", [128, 1], F32) for i in range(5)]
    lnsq = ctx.sb("lnsq", [128, D], F32)
    wr_sb = ctx.sb("wr_sb", [128, 8 * 16], F32)
    rprob = ctx.sb("rprob", [128, 16], F32)
    rtmp = ctx.sb("rtmp", [128, 16], F32)
    rm = [ctx.sb(f"rm{i}", [128, 1], F32) for i in range(3)]
    big_g = ctx.sb("big_g", [128, 8 * D], BF16)
    big_u = ctx.sb("big_u", [128, 8 * D], BF16)
    big_d = ctx.sb("big_d", [128, 8 * D], BF16)
    sh = dict(g=big_g, u=big_u, d=big_d)
    wgt = [dict(g=big_g[:, i * 4096:(i + 1) * 4096],
                u=big_u[:, i * 4096:(i + 1) * 4096],
                d=big_d[:, i * 4096:(i + 1) * 4096]) for i in range(2)]
    he = ctx.sb("he", [128, 8 * 512], BF16)
    silu_sb = ctx.sb("silu_sb", [128, 1024], BF16)
    pg = [ctx.ps("pg0", [128, 512]), ctx.ps("pg1", [128, 512])]
    pu = [ctx.ps("pu0", [128, 512]), ctx.ps("pu1", [128, 512])]
    pwd = ctx.ps("pwd", [128, D])
    ptr = ctx.ps("ptr", [128, 128], F32)
    prout = ctx.ps("prout", [128, 16])

    p.dma("sync", iden32[:, :], ident, writes=["iden32"])
    p.dma("sync", wr_sb[:, :].rearrange("p (a b) -> p a b", a=8),
          wr.rearrange("(a p) b -> p a b", p=128), writes=["wr_sb"])

    for ti in range(NT):
        kx = "x_sb"
        kxn = "xn_sb"
        p.dma("sync", x_sb[:, :], x[ti * 128:(ti + 1) * 128, :], writes=[kx])
        scratch = dict(tiles=(lnt[0][:, :], lnt[1][:, :], lnt[2][:, :], lnt[3][:, :],
                              lnt[4][:, :], lnsq[:, :]),
                       keys=("ls1", "lnegm", "lmm", "lvar", "lrstd", "lsq"),
                       kin=kx, kout=kxn)
        ln_tile(ctx, x_sb[:, :], xn_sb[:, :], scratch)
        for fi in range(8):
            p.op("tensor", lambda t, ti=ti, fi=fi: t.transpose(
                ptr[:, :], xn_sb[:, fi * 128:(fi + 1) * 128], iden32[:, :]),
                reads=[kxn, "iden32"], writes=["ptr"])
            p.op("vector", lambda v, fi=fi: v.tensor_copy(
                xn32t[:, fi * 128:(fi + 1) * 128], ptr[:, :]),
                reads=["ptr"], writes=[f"xn32_{fi}"])
        p.op("vector", lambda v, ti=ti: v.tensor_copy(
            xnT[:, :].rearrange("p (a b) -> p a b", a=8)[:, :, ti * 128:(ti + 1) * 128],
            xn32t[:, :].rearrange("p (a b) -> p a b", a=8)),
            reads=[f"xn32_{fi}" for fi in range(8)],
            writes=[f"xnT_{fi}_{ti}" for fi in range(8)])
        for fi in range(8):
            p.op("tensor", lambda t, ti=ti, fi=fi: t.matmul(
                prout[:, :], xn32t[:, fi * 128:(fi + 1) * 128],
                wr_sb[:, fi * 16:(fi + 1) * 16], start=(fi == 0), stop=(fi == 7)),
                reads=[f"xn32_{fi}", "wr_sb"], writes=["prout"])
        p.op("vector", lambda v: v.reduce_max(rm[0][:, :], prout[:, :], axis=AX.X,
                                              negate=True),
             reads=["prout"], writes=["rm0"])
        p.op("scalar", lambda s: s.activation(rprob[:, :], prout[:, :], AF.Exp,
                                              bias=rm[0][:, :]),
             reads=["prout", "rm0"], writes=["rprob"])
        p.op("vector", lambda v: v.reduce_sum(rm[1][:, :], rprob[:, :], axis=AX.X),
             reads=["rprob"], writes=["rm1"])
        p.op("vector", lambda v: v.reciprocal(rm[1][:, :], rm[1][:, :]),
             reads=["rm1"], writes=["rm1"])
        p.op("vector", lambda v: v.tensor_scalar(rprob[:, :], rprob[:, :], rm[1][:, :],
                                                 None, op0=ALU.mult),
             reads=["rprob", "rm1"], writes=["rprob"])
        p.op("vector", lambda v: v.reduce_max(rm[0][:, :], rprob[:, :], axis=AX.X),
             reads=["rprob"], writes=["rm0"])
        p.op("vector", lambda v: v.tensor_scalar(rtmp[:, :], rprob[:, :], rm[0][:, :],
                                                 None, op0=ALU.is_ge),
             reads=["rprob", "rm0"], writes=["rtmp"])
        p.op("vector", lambda v: v.scalar_tensor_tensor(rtmp[:, :], rtmp[:, :], -1e9,
                                                        rprob[:, :], op0=ALU.mult,
                                                        op1=ALU.add),
             reads=["rtmp", "rprob"], writes=["rtmp"])
        p.op("vector", lambda v: v.reduce_max(rm[2][:, :], rtmp[:, :], axis=AX.X),
             reads=["rtmp"], writes=["rm2"])
        p.op("vector", lambda v: v.tensor_scalar(rtmp[:, :], rprob[:, :], rm[2][:, :],
                                                 None, op0=ALU.is_ge),
             reads=["rprob", "rm2"], writes=["rtmp"])
        p.op("vector", lambda v, ti=ti: v.tensor_tensor(
            cw[:, ti * 16:(ti + 1) * 16], rprob[:, :], rtmp[:, :], op=ALU.mult),
            reads=["rprob", "rtmp"], writes=[f"cw{ti}"])

    for ti in range(NT):
        p.op("vector", lambda v, ti=ti: v.memset(acc[:, ti * D:(ti + 1) * D], 0.0),
             writes=[f"acc{ti}"])

    def load_expert(e, buf, kpre):
        ew = [kpre + "g", kpre + "u", kpre + "d"]
        xw = ["shg", "shu", "shd"] if e in (0, 1) else []
        if e < 16:
            p.dma("gpsimd", buf["g"].rearrange("p (a b) -> p a b", a=8),
                  wg[e].rearrange("(a p) b -> p a b", p=128), writes=[ew[0]] + xw)
            p.dma("gpsimd", buf["u"].rearrange("p (a b) -> p a b", a=8),
                  wu[e].rearrange("(a p) b -> p a b", p=128), writes=[ew[1]] + xw)
            p.dma("gpsimd", buf["d"].rearrange("p (a b) -> p a b", a=4),
                  wd[e].rearrange("(a p) b -> p a b", p=128), writes=[ew[2]] + xw)
        else:
            p.dma("gpsimd", buf["g"][:, :].rearrange("p (a b) -> p a b", a=8),
                  sg.rearrange("(a p) b -> p a b", p=128), writes=[ew[0]])
            p.dma("gpsimd", buf["u"][:, :].rearrange("p (a b) -> p a b", a=8),
                  su.rearrange("(a p) b -> p a b", p=128), writes=[ew[1]])
            p.dma("gpsimd", buf["d"][:, :].rearrange("p (a b) -> p a b", a=8),
                  sd.rearrange("(a p) b -> p a b", p=128), writes=[ew[2]])

    def run_expert(e, buf, kpre):
        F = 512 if e < 16 else 1024
        nm = F // 128
        for tc in range(NTC):
            tcol = tc * 512
            for mi in range(nm):
                bk = mi % 2
                pgb, pub = pg[bk], pu[bk]
                kg, ku = f"pg{bk}", f"pu{bk}"
                for fi in range(8):
                    p.op("tensor", lambda t, pgb=pgb, mi=mi, fi=fi, tcol=tcol: t.matmul(
                        pgb[:, :], buf["g"][:, fi * F + mi * 128: fi * F + (mi + 1) * 128],
                        xnT[:, fi * TPC + tcol: fi * TPC + tcol + 512],
                        start=(fi == 0), stop=(fi == 7)),
                        reads=[kpre + "g", "xnT_all"], writes=[kg])
                    p.op("tensor", lambda t, pub=pub, mi=mi, fi=fi, tcol=tcol: t.matmul(
                        pub[:, :], buf["u"][:, fi * F + mi * 128: fi * F + (mi + 1) * 128],
                        xnT[:, fi * TPC + tcol: fi * TPC + tcol + 512],
                        start=(fi == 0), stop=(fi == 7)),
                        reads=[kpre + "u", "xnT_all"], writes=[ku])
                p.op("scalar", lambda s, pgb=pgb, bk=bk: s.activation(
                    silu_sb[:, bk * 512:(bk + 1) * 512], pgb[:, :], AF.Silu),
                    reads=[kg], writes=[f"silu{bk}"])
                p.op("vector", lambda v, pub=pub, mi=mi, bk=bk: v.tensor_tensor(
                    he[:, mi * 512:(mi + 1) * 512], silu_sb[:, bk * 512:(bk + 1) * 512],
                    pub[:, :], op=ALU.mult),
                    reads=[f"silu{bk}", ku], writes=[f"he{mi}"])
            for tj in range(4):
                ti = tc * 4 + tj
                for mc in range(2):
                    for mi in range(nm):
                        p.op("tensor", lambda t, tj=tj, mc=mc, mi=mi: t.matmul(
                            pwd[:, mc * 512:(mc + 1) * 512],
                            he[:, mi * 512 + tj * 128: mi * 512 + (tj + 1) * 128],
                            buf["d"][:, mi * D + mc * 512: mi * D + (mc + 1) * 512],
                            start=(mi == 0), stop=(mi == nm - 1)),
                            reads=[f"he{mi}", kpre + "d"], writes=[f"pwd{mc}"])
                for mc in range(2):
                    if e < 16:
                        p.op("vector", lambda v, ti=ti, mc=mc, e=e: v.scalar_tensor_tensor(
                            acc[:, ti * D + mc * 512: ti * D + (mc + 1) * 512],
                            pwd[:, mc * 512:(mc + 1) * 512],
                            cw[:, ti * 16 + e: ti * 16 + e + 1],
                            acc[:, ti * D + mc * 512: ti * D + (mc + 1) * 512],
                            op0=ALU.mult, op1=ALU.add),
                            reads=[f"pwd{mc}", f"cw{ti}", f"acc{ti}"], writes=[f"acc{ti}"])
                    else:
                        p.op("vector", lambda v, ti=ti, mc=mc: v.tensor_tensor(
                            acc[:, ti * D + mc * 512: ti * D + (mc + 1) * 512],
                            pwd[:, mc * 512:(mc + 1) * 512],
                            acc[:, ti * D + mc * 512: ti * D + (mc + 1) * 512],
                            op=ALU.add),
                            reads=[f"pwd{mc}", f"acc{ti}"], writes=[f"acc{ti}"])

    def _seal(v):
        return v.tensor_copy(lnt[0][:, :], lnt[1][:, :])
    p.op("vector", _seal,
         reads=[f"xnT_{fi}_{ti}" for fi in range(8) for ti in range(NT)],
         writes=["xnT_all", "ls1"])

    if MOE_SHARED:
        load_expert(16, sh, "sh")
        run_expert(16, sh, "sh")
    for e in range(MOE_NE):
        buf = wgt[e % 2]
        kpre = f"e{e % 2}"
        load_expert(e, buf, kpre)
        run_expert(e, buf, kpre)

    p.dma("sync", dbg_cw, cw[:, :], reads=[f"cw{ti}" for ti in range(NT)])
    p.dma("sync", dbg_xnt, xnT[:, :], reads=["xnT_all"])

    for ti in range(NT):
        kx = "x_sb"
        p.dma("sync", x_sb[:, :], x[ti * 128:(ti + 1) * 128, :], writes=[kx])
        p.op("vector", lambda v, ti=ti: v.scalar_tensor_tensor(
            acc[:, ti * D:(ti + 1) * D], acc[:, ti * D:(ti + 1) * D], 0.5,
            x_sb[:, :], op0=ALU.mult, op1=ALU.add),
            reads=[f"acc{ti}", kx], writes=[f"acc{ti}"])
        p.dma("sync", xout[ti * 128:(ti + 1) * 128, :], acc[:, ti * D:(ti + 1) * D],
              reads=[f"acc{ti}"])

    return ctx.finish()


def build_lnT():
    ctx = Ctx("lnT")
    nc, p = ctx.nc, ctx.p
    x = ctx.inp("x", [TPC, D])
    ident = ctx.inp("ident", [128, 128])
    yT = ctx.out("yT", [D, TPC])
    NT = TPC // 128
    x_sb = ctx.sb("x_sb", [128, D], F32)
    xn_sb = ctx.sb("xn_sb", [128, D], F32)
    iden32 = ctx.sb("iden32", [128, 128], F32)
    ysb = ctx.sb("ysb", [128, 8 * TPC], F32)
    lnt = [ctx.sb(f"lnt{i}", [128, 1], F32) for i in range(5)]
    lnsq = ctx.sb("lnsq", [128, D], F32)
    ptr = ctx.ps("ptr", [128, 128], F32)
    p.dma("sync", iden32[:, :], ident, writes=["iden32"])
    for ti in range(NT):
        p.dma("sync", x_sb[:, :], x[ti * 128:(ti + 1) * 128, :], writes=["x_sb"])
        scratch = dict(tiles=(lnt[0][:, :], lnt[1][:, :], lnt[2][:, :], lnt[3][:, :],
                              lnt[4][:, :], lnsq[:, :]),
                       keys=("ls1", "lnegm", "lmm", "lvar", "lrstd", "lsq"),
                       kin="x_sb", kout="xn_sb")
        ln_tile(ctx, x_sb[:, :], xn_sb[:, :], scratch)
        for fi in range(8):
            p.op("tensor", lambda t, fi=fi: t.transpose(
                ptr[:, :], xn_sb[:, fi * 128:(fi + 1) * 128], iden32[:, :]),
                reads=["xn_sb", "iden32"], writes=["ptr"])
            p.op("vector", lambda v, fi=fi, ti=ti: v.tensor_copy(
                ysb[:, fi * TPC + ti * 128: fi * TPC + (ti + 1) * 128], ptr[:, :]),
                reads=["ptr"], writes=[f"y_{fi}_{ti}"])
    for fi in range(8):
        p.dma("sync", yT[fi * 128:(fi + 1) * 128, :], ysb[:, fi * TPC:(fi + 1) * TPC],
              reads=[f"y_{fi}_{ti}" for ti in range(NT)])
    return ctx.finish()


def build_addhalf():
    ctx = Ctx("addhalf")
    nc, p = ctx.nc, ctx.p
    x = ctx.inp("x", [TPC, D])
    yT = ctx.inp("yT", [D, TPC])
    ident = ctx.inp("ident", [128, 128])
    xout = ctx.out("xout", [TPC, D])
    NT = TPC // 128
    iden32 = ctx.sb("iden32", [128, 128], F32)
    x_sb = ctx.sb("x_sb", [128, D], F32)
    y_sb = ctx.sb("y_sb", [128, D], F32)
    o_sb = ctx.sb("o_sb", [128, D], F32)
    ptr = ctx.ps("ptr", [128, 128], F32)
    p.dma("sync", iden32[:, :], ident, writes=["iden32"])
    for ti in range(NT):
        p.dma("sync", x_sb[:, :], x[ti * 128:(ti + 1) * 128, :], writes=["x_sb"])
        p.dma("sync", y_sb[:, :].rearrange("p (a b) -> p a b", a=8),
              yT.rearrange("(a p) t -> p a t", p=128)[:, :, ti * 128:(ti + 1) * 128],
              writes=["y_sb"])
        for fi in range(8):
            p.op("tensor", lambda t, fi=fi: t.transpose(
                ptr[:, :], y_sb[:, fi * 128:(fi + 1) * 128], iden32[:, :]),
                reads=["y_sb", "iden32"], writes=["ptr"])
            p.op("vector", lambda v, fi=fi: v.scalar_tensor_tensor(
                o_sb[:, fi * 128:(fi + 1) * 128], ptr[:, :], 0.5,
                x_sb[:, fi * 128:(fi + 1) * 128], op0=ALU.mult, op1=ALU.add),
                reads=["ptr", "x_sb"], writes=["o_sb"])
        p.dma("sync", xout[ti * 128:(ti + 1) * 128, :], o_sb[:, :], reads=["o_sb"])
    return ctx.finish()


def build_outproj():
    ctx = Ctx("outproj")
    nc, p = ctx.nc, ctx.p
    ot = ctx.inp("ot", [D, TPC], BF16)
    w = ctx.inp("w", [128, 8, D])
    bias = ctx.inp("bias", [1, D])
    x = ctx.inp("x", [TPC, D])
    ones = ctx.inp("ones", [1, 128])
    xout = ctx.out("xout", [TPC, D])
    NT = TPC // 128
    ot_sb = ctx.sb("ot_sb", [128, 8 * TPC], BF16)
    w_sb = ctx.sb("w_sb", [128, 8 * D], BF16)
    b_sb = ctx.sb("b_sb", [1, D], F32)
    bb_sb = ctx.sb("bb_sb", [128, D], F32)
    ones_sb = ctx.sb("ones_sb", [1, 128], F32)
    x_sb = ctx.sb("x_sb", [128, D], F32)
    o_sb = ctx.sb("o_sb", [128, D], F32)
    pmm = ctx.ps("pmm", [128, D], F32)
    pbc = ctx.ps("pbc", [128, 512], F32)
    p.dma("sync", ot_sb[:, :].rearrange("p (a t) -> p a t", a=8),
          ot.rearrange("(a p) t -> p a t", p=128), writes=["ot_sb"])
    p.dma("gpsimd", w_sb[:, :].rearrange("p (a m) -> p a m", a=8), w, writes=["w_sb"])
    p.dma("sync", b_sb[:, :], bias, writes=["b_sb"])
    p.dma("sync", ones_sb[:, :], ones, writes=["ones_sb"])
    for mc in range(2):
        p.op("tensor", lambda t, mc=mc: t.matmul(
            pbc[:, :], ones_sb[:, :], b_sb[:, mc * 512:(mc + 1) * 512],
            start=True, stop=True),
            reads=["ones_sb", "b_sb"], writes=["pbc"])
        p.op("vector", lambda v, mc=mc: v.tensor_copy(
            bb_sb[:, mc * 512:(mc + 1) * 512], pbc[:, :]),
            reads=["pbc"], writes=["bb_sb"])
    for ti in range(NT):
        p.dma("sync", x_sb[:, :], x[ti * 128:(ti + 1) * 128, :], writes=["x_sb"])
        for mc in range(2):
            for fi in range(8):
                p.op("tensor", lambda t, ti=ti, mc=mc, fi=fi: t.matmul(
                    pmm[:, mc * 512:(mc + 1) * 512],
                    ot_sb[:, fi * TPC + ti * 128: fi * TPC + (ti + 1) * 128],
                    w_sb[:, fi * D + mc * 512: fi * D + (mc + 1) * 512],
                    start=(fi == 0), stop=(fi == 7)),
                    reads=["ot_sb", "w_sb"], writes=[f"pmm{mc}"])
            p.op("vector", lambda v, mc=mc: v.tensor_tensor(
                o_sb[:, mc * 512:(mc + 1) * 512], pmm[:, mc * 512:(mc + 1) * 512],
                bb_sb[:, mc * 512:(mc + 1) * 512], op=ALU.add),
                reads=[f"pmm{mc}", "bb_sb"], writes=["o_sb"])
            p.op("vector", lambda v, mc=mc: v.tensor_tensor(
                o_sb[:, mc * 512:(mc + 1) * 512], o_sb[:, mc * 512:(mc + 1) * 512],
                x_sb[:, mc * 512:(mc + 1) * 512], op=ALU.add),
                reads=["o_sb", "x_sb"], writes=["o_sb"])
        p.dma("sync", xout[ti * 128:(ti + 1) * 128, :], o_sb[:, :], reads=["o_sb"])
    return ctx.finish()


def build_qkv():
    ctx = Ctx("qkv")
    nc, p = ctx.nc, ctx.p
    x = ctx.inp("x", [TPC, D])
    wqkv = ctx.inp("wqkv", [128, 8, 3072])
    freqs = ctx.inp("freqs", [1, 32])
    pos = ctx.inp("pos", [1, TPC])
    sgnmask = ctx.inp("sgnmask", [1, D])
    ident = ctx.inp("ident", [128, 128])
    ones = ctx.inp("ones", [1, 128])
    qt_o = ctx.out("qt_o", [16, 64, TPC], BF16)
    kt_o = ctx.out("kt_o", [16, 64, TPC], BF16)
    v_o = ctx.out("v_o", [16, TPC, 64], BF16)
    NT = TPC // 128

    x_sb = ctx.sb("x_sb", [128, D], F32)
    xn_sb = ctx.sb("xn_sb", [128, D], F32)
    xnt_bf = ctx.sb("xnt_bf", [128, D], BF16)
    xn32t = ctx.sb("xn32t", [128, D], F32)
    iden32 = ctx.sb("iden32", [128, 128], F32)
    idenb = ctx.sb("idenb", [128, 128], BF16)
    ones_sb = ctx.sb("ones_sb", [1, 128], F32)
    w_sb = ctx.sb("w_sb", [128, 8 * 3072], BF16)
    fr_sb = ctx.sb("fr_sb", [1, D], F32)
    pos_sb = ctx.sb("pos_sb", [1, TPC], F32)
    sg_row = ctx.sb("sg_row", [1, D], F32)
    sgn_sb = ctx.sb("sgn_sb", [128, D], F32)
    lnt = [ctx.sb(f"lnt{i}", [128, 1], F32) for i in range(5)]
    lnsq = ctx.sb("lnsq", [128, D], F32)
    q_tok = ctx.sb("q_tok", [128, D], F32)
    k_tok = ctx.sb("k_tok", [128, D], F32)
    v_sb = ctx.sb("v_sb", [128, D], BF16)
    ang_sb = ctx.sb("ang_sb", [128, D], F32)
    crep = ctx.sb("crep", [128, D], F32)
    srep = ctx.sb("srep", [128, D], F32)
    u_sb = ctx.sb("u_sb", [128, D], F32)
    n_i32 = ctx.sb("n_i32", [128, D], I32)
    n_f = ctx.sb("n_f", [128, D], F32)
    rot_sb = ctx.sb("rot_sb", [128, D], F32)
    t1_sb = ctx.sb("t1_sb", [128, D], F32)
    qp_bf = ctx.sb("qp_bf", [128, D], BF16)
    kp_bf = ctx.sb("kp_bf", [128, D], BF16)
    qkrow = ctx.sb("qkrow", [64, 2 * 16 * 128], BF16)
    ptr = ctx.ps("ptr", [128, 128], F32)
    pmm = ctx.ps("pmm", [128, 512], F32)
    pang = ctx.ps("pang", [128, D], F32)
    ptb = ctx.ps("ptb", [64, 128], BF16)

    p.dma("sync", iden32[:, :], ident, writes=["iden32"])
    p.op("vector", lambda v: v.tensor_copy(idenb[:, :], iden32[:, :]),
         reads=["iden32"], writes=["idenb"])
    p.dma("sync", ones_sb[:, :], ones, writes=["ones_sb"])
    p.dma("gpsimd", w_sb[:, :].rearrange("p (a j) -> p a j", a=8), wqkv, writes=["w_sb"])
    p.dma("sync", fr_sb[:, 0:32], freqs, writes=["fr_sb"])
    p.dma("sync", pos_sb[:, :], pos, writes=["pos_sb"])
    p.dma("sync", sg_row[:, :], sgnmask, writes=["sg_row"])
    for k in (32, 64, 128, 256, 512):
        p.op("vector", lambda v, k=k: v.tensor_copy(fr_sb[:, k:2 * k], fr_sb[:, 0:k]),
             reads=["fr_sb"], writes=["fr_sb"])
    for mc in range(2):
        p.op("tensor", lambda t, mc=mc: t.matmul(
            pmm[:, :], ones_sb[:, :], sg_row[:, mc * 512:(mc + 1) * 512],
            start=True, stop=True), reads=["ones_sb", "sg_row"], writes=["pmm"])
        p.op("vector", lambda v, mc=mc: v.tensor_copy(
            sgn_sb[:, mc * 512:(mc + 1) * 512], pmm[:, :]), reads=["pmm"],
            writes=["sgn_sb"])

    PI = float(np.pi)
    for ti in range(NT):
        p.dma("sync", x_sb[:, :], x[ti * 128:(ti + 1) * 128, :], writes=["x_sb"])
        scratch = dict(tiles=(lnt[0][:, :], lnt[1][:, :], lnt[2][:, :], lnt[3][:, :],
                              lnt[4][:, :], lnsq[:, :]),
                       keys=("ls1", "lnegm", "lmm", "lvar", "lrstd", "lsq"),
                       kin="x_sb", kout="xn_sb")
        ln_tile(ctx, x_sb[:, :], xn_sb[:, :], scratch)
        for fi in range(8):
            p.op("tensor", lambda t, fi=fi: t.transpose(
                ptr[:, :], xn_sb[:, fi * 128:(fi + 1) * 128], iden32[:, :]),
                reads=["xn_sb", "iden32"], writes=["ptr"])
            p.op("vector", lambda v, fi=fi: v.tensor_copy(
                xn32t[:, fi * 128:(fi + 1) * 128], ptr[:, :]),
                reads=["ptr"], writes=["xn32t"])
        p.op("vector", lambda v: v.tensor_copy(xnt_bf[:, :], xn32t[:, :]),
             reads=["xn32t"], writes=["xnt_bf"])
        for jc in range(6):
            for fi in range(8):
                p.op("tensor", lambda t, jc=jc, fi=fi: t.matmul(
                    pmm[:, :], xnt_bf[:, fi * 128:(fi + 1) * 128],
                    w_sb[:, fi * 3072 + jc * 512: fi * 3072 + (jc + 1) * 512],
                    start=(fi == 0), stop=(fi == 7)),
                    reads=["xnt_bf", "w_sb"], writes=["pmm"])
            if jc < 2:
                p.op("vector", lambda v, jc=jc: v.tensor_copy(
                    q_tok[:, jc * 512:(jc + 1) * 512], pmm[:, :]),
                    reads=["pmm"], writes=["q_tok"])
            elif jc < 4:
                p.op("vector", lambda v, jc=jc: v.tensor_copy(
                    k_tok[:, (jc - 2) * 512:(jc - 1) * 512], pmm[:, :]),
                    reads=["pmm"], writes=["k_tok"])
            else:
                p.op("vector", lambda v, jc=jc: v.tensor_copy(
                    v_sb[:, (jc - 4) * 512:(jc - 3) * 512], pmm[:, :]),
                    reads=["pmm"], writes=["v_sb"])
        p.dma("sync", v_o.rearrange("h t d -> t h d")[ti * 128:(ti + 1) * 128],
              v_sb[:, :].rearrange("p (h d) -> p h d", h=16), reads=["v_sb"])
        for mc in range(2):
            p.op("tensor", lambda t, mc=mc, ti=ti: t.matmul(
                pang[:, mc * 512:(mc + 1) * 512],
                pos_sb[:, ti * 128:(ti + 1) * 128],
                fr_sb[:, mc * 512:(mc + 1) * 512], start=True, stop=True),
                reads=["pos_sb", "fr_sb"], writes=["pang"])
        p.op("vector", lambda v: v.tensor_copy(ang_sb[:, :], pang[:, :]),
             reads=["pang"], writes=["ang_sb"])
        p.op("vector", lambda v: v.tensor_scalar(u_sb[:, :], ang_sb[:, :],
                                                 1.0 / (2 * PI), None, op0=ALU.mult),
             reads=["ang_sb"], writes=["u_sb"])
        p.op("vector", lambda v: v.tensor_copy(n_i32[:, :], u_sb[:, :]),
             reads=["u_sb"], writes=["n_i32"])
        p.op("vector", lambda v: v.tensor_copy(n_f[:, :], n_i32[:, :]),
             reads=["n_i32"], writes=["n_f"])
        p.op("vector", lambda v: v.scalar_tensor_tensor(srep[:, :], n_f[:, :],
                                                        -2 * PI, ang_sb[:, :],
                                                        op0=ALU.mult, op1=ALU.add),
             reads=["n_f", "ang_sb"], writes=["srep"])
        p.op("scalar", lambda s: s.activation(srep[:, :], srep[:, :], AF.Sin),
             reads=["srep"], writes=["srep"])
        p.op("vector", lambda v: v.tensor_scalar(u_sb[:, :], u_sb[:, :], 0.25, None,
                                                 op0=ALU.add),
             reads=["u_sb"], writes=["u_sb"])
        p.op("vector", lambda v: v.tensor_copy(n_i32[:, :], u_sb[:, :]),
             reads=["u_sb"], writes=["n_i32"])
        p.op("vector", lambda v: v.tensor_copy(n_f[:, :], n_i32[:, :]),
             reads=["n_i32"], writes=["n_f"])
        p.op("vector", lambda v: v.scalar_tensor_tensor(crep[:, :], n_f[:, :],
                                                        -2 * PI, ang_sb[:, :],
                                                        op0=ALU.mult, op1=ALU.add),
             reads=["n_f", "ang_sb"], writes=["crep"])
        hp, hpk = ctx.cst(PI / 2)
        p.op("scalar", lambda s, hp=hp: s.activation(crep[:, :], crep[:, :], AF.Sin,
                                                     bias=hp[:, :]),
             reads=["crep", hpk], writes=["crep"])
        p.op("vector", lambda v: v.tensor_tensor(srep[:, :], srep[:, :], sgn_sb[:, :],
                                                 op=ALU.mult),
             reads=["srep", "sgn_sb"], writes=["srep"])
        for (tok, outb, kt_in, kt_out) in ((q_tok, qp_bf, "q_tok", "qp_bf"),
                                           (k_tok, kp_bf, "k_tok", "kp_bf")):
            p.op("vector", lambda v, tok=tok: v.tensor_copy(
                rot_sb[:, :].rearrange("p (h x j) -> p h x j", h=16, x=2)[:, :, 0, :],
                tok[:, :].rearrange("p (h x j) -> p h x j", h=16, x=2)[:, :, 1, :]),
                reads=[kt_in], writes=["rot_sb"])
            p.op("vector", lambda v, tok=tok: v.tensor_copy(
                rot_sb[:, :].rearrange("p (h x j) -> p h x j", h=16, x=2)[:, :, 1, :],
                tok[:, :].rearrange("p (h x j) -> p h x j", h=16, x=2)[:, :, 0, :]),
                reads=[kt_in], writes=["rot_sb"])
            p.op("vector", lambda v, tok=tok: v.tensor_tensor(
                t1_sb[:, :], tok[:, :], crep[:, :], op=ALU.mult),
                reads=[kt_in, "crep"], writes=["t1_sb"])
            p.op("vector", lambda v: v.tensor_tensor(
                rot_sb[:, :], rot_sb[:, :], srep[:, :], op=ALU.mult),
                reads=["rot_sb", "srep"], writes=["rot_sb"])
            p.op("vector", lambda v, outb=outb: v.tensor_tensor(
                outb[:, :], t1_sb[:, :], rot_sb[:, :], op=ALU.add),
                reads=["t1_sb", "rot_sb"], writes=[kt_out])
        for (src, kin, off) in ((qp_bf, "qp_bf", 0), (kp_bf, "kp_bf", 16 * 128)):
            for h in range(16):
                p.op("tensor", lambda t, src=src, h=h: t.transpose(
                    ptb[:, :], src[:, h * 64:(h + 1) * 64], idenb[:, :]),
                    reads=[kin, "idenb"], writes=["ptb"])
                p.op("vector", lambda v, h=h, off=off: v.tensor_copy(
                    qkrow[:, off + h * 128:off + (h + 1) * 128], ptb[:, :]),
                    reads=["ptb"], writes=["qkrow"])
        p.dma("sync", qt_o.rearrange("h d t -> d h t")[:, :, ti * 128:(ti + 1) * 128],
              qkrow[:, 0:16 * 128].rearrange("d (h t) -> d h t", h=16),
              reads=["qkrow"])
        p.dma("sync", kt_o.rearrange("h d t -> d h t")[:, :, ti * 128:(ti + 1) * 128],
              qkrow[:, 16 * 128:].rearrange("d (h t) -> d h t", h=16),
              reads=["qkrow"])
    return ctx.finish()


def build_flash():
    ctx = Ctx("flash")
    nc, p = ctx.nc, ctx.p
    qt = ctx.inp("qt", [2, 64, T], BF16)
    kt = ctx.inp("kt", [2, 64, T], BF16)
    vv = ctx.inp("vv", [2, T, 64], BF16)
    pa = ctx.inp("pa", [1, 4])
    dist = ctx.inp("dist", [128, 16384], mybir.dt.int16)
    ones = ctx.inp("ones", [1, 128])
    ot = ctx.out("ot", [2, 64, T], BF16)

    NQC = S // 512
    NKT = S // 128

    dist_sb = ctx.sb("dist_sb", [128, 16384], mybir.dt.int16)
    esc = ctx.sb("esc", [128, 8192], F32)
    e_sb = ctx.sb("e_sb", [128, 16384], BF16)
    qtx = ctx.sb("qtx", [65, S], BF16)
    ktx = ctx.sb("ktx", [65, S], BF16)
    vext = ctx.sb("vext", [128, NKT * 65], BF16)
    ones_sb = ctx.sb("ones_sb", [1, 128], F32)
    ones_col = ctx.sb("ones_col", [128, 1], F32)
    pa_sb = ctx.sb("pa_sb", [1, 4], F32)
    pcol = ctx.sb("pcol", [128, 2], F32)
    sq_sb = ctx.sb("sq_sb", [64, 512], F32)
    nrm = ctx.sb("nrm", [1, 512], F32)
    mk_sb = ctx.sb("mk_sb", [1, 2], F32)
    pexp = [ctx.sb(f"pexp{i}", [128, 512], BF16) for i in range(3)]
    ptile = [ctx.sb(f"ptile{i}", [128, 512], BF16) for i in range(3)]
    rec_sb = ctx.sb("rec_sb", [1, 512], F32)
    rb_sb = ctx.sb("rb_sb", [64, 512], F32)
    otc = ctx.sb("otc", [64, 512], BF16)
    pS = [ctx.ps(f"pS{i}", [128, 512], F32) for i in range(3)]
    pO = ctx.ps("pO", [65, 512], F32)
    pB = ctx.ps("pB", [64, 512], F32)
    pN = ctx.ps("pN", [1, 512], F32)

    p.dma("sync", dist_sb[:, :], dist, writes=["dist_sb"])
    p.dma("sync", ones_sb[:, :], ones, writes=["ones_sb"])
    p.dma("sync", pa_sb[:, :], pa, writes=["pa_sb"])
    p.op("vector", lambda v: v.memset(ones_col[:, :], 1.0), writes=["ones_col"])
    p.op("vector", lambda v: v.tensor_scalar(pa_sb[:, :], pa_sb[:, :], 0.01, None,
                                             op0=ALU.max),
         reads=["pa_sb"], writes=["pa_sb"])

    for hi in range(2):
        p.op("tensor", lambda t, hi=hi: t.matmul(
            pS[0][:, 0:2], ones_sb[:, :], pa_sb[0:1, hi * 2:hi * 2 + 2],
            start=True, stop=True),
            reads=["ones_sb", "pa_sb"], writes=["pS0"])
        p.op("vector", lambda v: v.tensor_copy(pcol[:, :], pS[0][:, 0:2]),
             reads=["pS0"], writes=["pcol"])
        p.op("vector", lambda v: v.tensor_scalar(pcol[:, 0:1], pcol[:, 0:1], -1.0,
                                                 None, op0=ALU.mult),
             reads=["pcol"], writes=["pcol"])
        for hh in range(2):
            o = hh * 8192
            p.op("vector", lambda v, o=o: v.tensor_scalar(
                esc[:, :], dist_sb[:, o:o + 8192], pcol[:, 1:2], 1.0,
                op0=ALU.mult, op1=ALU.add),
                reads=["dist_sb", "pcol"], writes=["esc"])
            p.op("scalar", lambda s: s.activation(esc[:, :], esc[:, :], AF.Ln),
                 reads=["esc"], writes=["esc"])
            p.op("scalar", lambda s, o=o: s.activation(
                e_sb[:, o:o + 8192], esc[:, :], AF.Exp, scale=pcol[:, 0:1]),
                reads=["esc", "pcol"], writes=["e_sb"])
        for b in range(2):
            tcol = b * S
            p.dma("sync", qtx[0:64, :], qt[hi, :, tcol:tcol + S], writes=["qtx"])
            p.dma("sync", ktx[0:64, :], kt[hi, :, tcol:tcol + S], writes=["ktx"])
            p.op("vector", lambda v: v.memset(ktx[64:65, :], 1.0), writes=["ktx1"])
            p.dma("sync", vext[:, :].rearrange("p (k e) -> p k e", e=65)[:, :, 0:64],
                  vv[hi, tcol:tcol + S, :].rearrange("(k p) d -> p k d", p=128),
                  writes=["vext"])
            p.op("vector", lambda v: v.memset(
                vext[:, :].rearrange("p (k e) -> p k e", e=65)[:, :, 64:65], 1.0),
                writes=["vext1"])
            p.op("vector", lambda v: v.memset(mk_sb[:, :], 0.0), writes=["mk_sb"])
            for qc in range(NQC):
                p.op("scalar", lambda s, qc=qc: s.activation(
                    sq_sb[:, :], ktx[0:64, qc * 512:(qc + 1) * 512], AF.Square),
                    reads=["ktx"], writes=["sq_sb"])
                p.op("tensor", lambda t, qc=qc: t.matmul(
                    pN[:, :], ones_col[0:64, :], sq_sb[:, :], start=True, stop=True),
                    reads=["ones_col", "sq_sb"], writes=["pN"])
                p.op("vector", lambda v: v.reduce_max(mk_sb[:, 1:2], pN[:, :],
                                                      axis=AX.X),
                     reads=["pN"], writes=["mk2"])
                p.op("vector", lambda v: v.tensor_tensor(
                    mk_sb[:, 0:1], mk_sb[:, 0:1], mk_sb[:, 1:2], op=ALU.max),
                    reads=["mk_sb", "mk2"], writes=["mk_sb"])
            p.op("scalar", lambda s: s.activation(mk_sb[:, 0:1], mk_sb[:, 0:1],
                                                  AF.Sqrt),
                 reads=["mk_sb"], writes=["mk_sb"])
            for qc in range(NQC):
                p.op("scalar", lambda s, qc=qc: s.activation(
                    sq_sb[:, :], qtx[0:64, qc * 512:(qc + 1) * 512], AF.Square),
                    reads=["qtx"], writes=["sq_sb"])
                p.op("tensor", lambda t, qc=qc: t.matmul(
                    pN[:, :], ones_col[0:64, :], sq_sb[:, :], start=True, stop=True),
                    reads=["ones_col", "sq_sb"], writes=["pN"])
                p.op("scalar", lambda s: s.activation(nrm[:, :], pN[:, :], AF.Sqrt),
                     reads=["pN"], writes=["nrm"])
                p.op("vector", lambda v, qc=qc: v.tensor_scalar(
                    qtx[64:65, qc * 512:(qc + 1) * 512], nrm[:, :], mk_sb[:, 0:1],
                    -1.0, op0=ALU.mult, op1=ALU.mult),
                    reads=["nrm", "mk_sb"], writes=["qtxc"])
            for qc in range(NQC):
                qcol = qc * 512
                for kt_i in range(NKT + 2):
                    if kt_i < NKT:
                        r = kt_i % 3
                        p.op("tensor", lambda t, r=r, kt_i=kt_i, qcol=qcol: t.matmul(
                            pS[r][:, :], ktx[:, kt_i * 128:(kt_i + 1) * 128],
                            qtx[:, qcol:qcol + 512], start=True, stop=True),
                            reads=["ktx", "ktx1", "qtx", "qtxc"], writes=[f"pS{r}"])
                    if kt_i >= 2:
                        kk = kt_i - 2
                        r = kk % 3
                        base = qcol - kk * 128 + 8064
                        p.op("scalar", lambda s, r=r: s.activation(
                            pexp[r][:, :], pS[r][:, :], AF.Exp, scale=0.125),
                            reads=[f"pS{r}"], writes=[f"pexp{r}"])
                        p.op("vector", lambda v, r=r, base=base: v.tensor_tensor(
                            ptile[r][:, :], pexp[r][:, :], e_sb[:, base:base + 512],
                            op=ALU.mult),
                            reads=[f"pexp{r}", "e_sb"], writes=[f"ptile{r}"])
                        p.op("tensor", lambda t, r=r, kk=kk: t.matmul(
                            pO[:, :], vext[:, kk * 65:(kk + 1) * 65], ptile[r][:, :],
                            start=(kk == 0), stop=(kk == NKT - 1)),
                            reads=["vext", "vext1", f"ptile{r}"], writes=["pO"])
                p.op("vector", lambda v: v.reciprocal(rec_sb[:, :], pO[64:65, :]),
                     reads=["pO"], writes=["rec_sb"])
                p.op("tensor", lambda t: t.matmul(
                    pB[:, :], ones_sb[:, 0:64], rec_sb[:, :], start=True, stop=True),
                    reads=["ones_sb", "rec_sb"], writes=["pB"])
                p.op("vector", lambda v: v.tensor_copy(rb_sb[:, :], pB[:, :]),
                     reads=["pB"], writes=["rb_sb"])
                p.op("vector", lambda v: v.tensor_tensor(
                    otc[:, :], pO[0:64, :], rb_sb[:, :], op=ALU.mult),
                    reads=["pO", "rb_sb"], writes=["otc"])
                p.dma("sync", ot[hi, :, tcol + qcol: tcol + qcol + 512], otc[:, :],
                      reads=["otc"])
    return ctx.finish()


def build_featmap():
    ctx = Ctx("featmap")
    nc, p = ctx.nc, ctx.p
    x = ctx.inp("x", [TPC, D])
    wqkv = ctx.inp("wqkv", [128, 8, 3072])
    bqkv = ctx.inp("bqkv", [1, 3072])
    gb = ctx.inp("gb", [1, 4 * D])
    ident = ctx.inp("ident", [128, 128])
    ones = ctx.inp("ones", [1, 128])
    kqt_o = ctx.out("kqt_o", [8, 128, TPC], BF16)
    kkt_o = ctx.out("kkt_o", [8, 128, TPC], BF16)
    kk_o = ctx.out("kk_o", [TPC, D], BF16)
    v_o = ctx.out("v_o", [TPC, D], BF16)
    NT = TPC // 128

    x_sb = ctx.sb("x_sb", [128, D], F32)
    xn_sb = ctx.sb("xn_sb", [128, D], F32)
    xn32t = ctx.sb("xn32t", [128, D], F32)
    xnt_bf = ctx.sb("xnt_bf", [128, D], BF16)
    iden32 = ctx.sb("iden32", [128, 128], F32)
    idenb = ctx.sb("idenb", [128, 128], BF16)
    ones_sb = ctx.sb("ones_sb", [1, 128], F32)
    w_sb = ctx.sb("w_sb", [128, 8 * 3072], BF16)
    brow = ctx.sb("brow", [1, 3072], F32)
    bb_sb = ctx.sb("bb_sb", [128, 3072], F32)
    gbrow = ctx.sb("gbrow", [1, 4 * D], F32)
    gbb = ctx.sb("gbb", [128, 4 * D], F32)
    lnt = [ctx.sb(f"lnt{i}", [128, 1], F32) for i in range(5)]
    lnsq = ctx.sb("lnsq", [128, D], F32)
    q_tok = ctx.sb("q_tok", [128, D], F32)
    k_tok = ctx.sb("k_tok", [128, D], F32)
    v_sb = ctx.sb("v_sb", [128, D], BF16)
    f1 = [ctx.sb(f"f1_{i}", [128, 1], F32) for i in range(6)]
    sqb = ctx.sb("sqb", [128, D], F32)
    ker_bf = ctx.sb("ker_bf", [128, D], BF16)
    row_sb = ctx.sb("row_sb", [128, 128], BF16)
    ptr = ctx.ps("ptr", [128, 128], F32)
    ptb = ctx.ps("ptb", [128, 128], BF16)
    pmm = ctx.ps("pmm", [128, 512], F32)

    p.dma("sync", iden32[:, :], ident, writes=["iden32"])
    p.op("vector", lambda v: v.tensor_copy(idenb[:, :], iden32[:, :]),
         reads=["iden32"], writes=["idenb"])
    p.dma("sync", ones_sb[:, :], ones, writes=["ones_sb"])
    p.dma("gpsimd", w_sb[:, :].rearrange("p (a j) -> p a j", a=8), wqkv, writes=["w_sb"])
    p.dma("sync", brow[:, :], bqkv, writes=["brow"])
    p.dma("sync", gbrow[:, :], gb, writes=["gbrow"])
    for mc in range(6):
        p.op("tensor", lambda t, mc=mc: t.matmul(
            pmm[:, :], ones_sb[:, :], brow[:, mc * 512:(mc + 1) * 512],
            start=True, stop=True), reads=["ones_sb", "brow"], writes=["pmm"])
        p.op("vector", lambda v, mc=mc: v.tensor_copy(
            bb_sb[:, mc * 512:(mc + 1) * 512], pmm[:, :]), reads=["pmm"],
            writes=["bb_sb"])
    for gi in range(4):
        for mc in range(2):
            p.op("tensor", lambda t, gi=gi, mc=mc: t.matmul(
                pmm[:, :], ones_sb[:, :],
                gbrow[0:1, gi * D + mc * 512: gi * D + (mc + 1) * 512],
                start=True, stop=True), reads=["ones_sb", "gbrow"], writes=["pmm"])
            p.op("vector", lambda v, gi=gi, mc=mc: v.tensor_copy(
                gbb[:, gi * D + mc * 512: gi * D + (mc + 1) * 512], pmm[:, :]),
                reads=["pmm"], writes=["gbb"])

    def featmap(tok, kin, gi, out_bf, kout):
        p.op("vector", lambda v: v.reduce_sum(f1[0][:, :], tok[:, :], axis=AX.X),
             reads=[kin], writes=["f0"])
        p.op("vector", lambda v: v.tensor_scalar(f1[1][:, :], f1[0][:, :],
                                                 -1.0 / D, None, op0=ALU.mult),
             reads=["f0"], writes=["fnegm"])
        p.op("scalar", lambda s: s.activation(sqb[:, :], tok[:, :], AF.Square),
             reads=[kin], writes=["fsq"])
        p.op("vector", lambda v: v.reduce_sum(f1[2][:, :], sqb[:, :], axis=AX.X),
             reads=["fsq"], writes=["fs2"])
        p.op("vector", lambda v: v.tensor_tensor(f1[3][:, :], f1[1][:, :], f1[0][:, :],
                                                 op=ALU.mult),
             reads=["fnegm", "f0"], writes=["fmm"])
        p.op("vector", lambda v: v.tensor_tensor(f1[3][:, :], f1[2][:, :], f1[3][:, :],
                                                 op=ALU.add),
             reads=["fs2", "fmm"], writes=["fmm"])
        p.op("vector", lambda v: v.tensor_scalar(f1[3][:, :], f1[3][:, :],
                                                 1.0 / (D - 1), None, op0=ALU.mult),
             reads=["fmm"], writes=["fmm"])
        p.op("scalar", lambda s: s.activation(f1[4][:, :], f1[3][:, :], AF.Sqrt),
             reads=["fmm"], writes=["fstd"])
        p.op("vector", lambda v: v.tensor_scalar(f1[4][:, :], f1[4][:, :], 1e-6, None,
                                                 op0=ALU.add),
             reads=["fstd"], writes=["fstd"])
        p.op("vector", lambda v: v.reciprocal(f1[4][:, :], f1[4][:, :]),
             reads=["fstd"], writes=["fstd"])
        p.op("vector", lambda v: v.tensor_scalar(sqb[:, :], tok[:, :], f1[1][:, :],
                                                 f1[4][:, :], op0=ALU.add,
                                                 op1=ALU.mult),
             reads=[kin, "fnegm", "fstd"], writes=["fsq"])
        p.op("vector", lambda v, gi=gi: v.tensor_tensor(
            sqb[:, :], sqb[:, :], gbb[:, (2 * gi) * D:(2 * gi + 1) * D], op=ALU.mult),
            reads=["fsq", "gbb"], writes=["fsq"])
        p.op("vector", lambda v, gi=gi: v.tensor_tensor(
            sqb[:, :], sqb[:, :], gbb[:, (2 * gi + 1) * D:(2 * gi + 2) * D],
            op=ALU.add),
            reads=["fsq", "gbb"], writes=["fsq"])
        p.op("scalar", lambda s: s.activation(out_bf[:, :], sqb[:, :], AF.Square),
             reads=["fsq"], writes=[kout])

    for ti in range(NT):
        p.dma("sync", x_sb[:, :], x[ti * 128:(ti + 1) * 128, :], writes=["x_sb"])
        scratch = dict(tiles=(lnt[0][:, :], lnt[1][:, :], lnt[2][:, :], lnt[3][:, :],
                              lnt[4][:, :], lnsq[:, :]),
                       keys=("ls1", "lnegm", "lmm", "lvar", "lrstd", "lsq"),
                       kin="x_sb", kout="xn_sb")
        ln_tile(ctx, x_sb[:, :], xn_sb[:, :], scratch)
        for fi in range(8):
            p.op("tensor", lambda t, fi=fi: t.transpose(
                ptr[:, :], xn_sb[:, fi * 128:(fi + 1) * 128], iden32[:, :]),
                reads=["xn_sb", "iden32"], writes=["ptr"])
            p.op("vector", lambda v, fi=fi: v.tensor_copy(
                xn32t[:, fi * 128:(fi + 1) * 128], ptr[:, :]),
                reads=["ptr"], writes=["xn32t"])
        p.op("vector", lambda v: v.tensor_copy(xnt_bf[:, :], xn32t[:, :]),
             reads=["xn32t"], writes=["xnt_bf"])
        for jc in range(6):
            for fi in range(8):
                p.op("tensor", lambda t, jc=jc, fi=fi: t.matmul(
                    pmm[:, :], xnt_bf[:, fi * 128:(fi + 1) * 128],
                    w_sb[:, fi * 3072 + jc * 512: fi * 3072 + (jc + 1) * 512],
                    start=(fi == 0), stop=(fi == 7)),
                    reads=["xnt_bf", "w_sb"], writes=["pmm"])
            if jc < 2:
                p.op("vector", lambda v, jc=jc: v.tensor_tensor(
                    q_tok[:, jc * 512:(jc + 1) * 512], pmm[:, :],
                    bb_sb[:, jc * 512:(jc + 1) * 512], op=ALU.add),
                    reads=["pmm", "bb_sb"], writes=["q_tok"])
            elif jc < 4:
                p.op("vector", lambda v, jc=jc: v.tensor_tensor(
                    k_tok[:, (jc - 2) * 512:(jc - 1) * 512], pmm[:, :],
                    bb_sb[:, jc * 512:(jc + 1) * 512], op=ALU.add),
                    reads=["pmm", "bb_sb"], writes=["k_tok"])
            else:
                p.op("vector", lambda v, jc=jc: v.tensor_tensor(
                    v_sb[:, (jc - 4) * 512:(jc - 3) * 512], pmm[:, :],
                    bb_sb[:, jc * 512:(jc + 1) * 512], op=ALU.add),
                    reads=["pmm", "bb_sb"], writes=["v_sb"])
        p.dma("sync", v_o[ti * 128:(ti + 1) * 128, :], v_sb[:, :], reads=["v_sb"])
        featmap(q_tok, "q_tok", 0, ker_bf, "ker_bf")
        for fi in range(8):
            p.op("tensor", lambda t, fi=fi: t.transpose(
                ptb[:, :], ker_bf[:, fi * 128:(fi + 1) * 128], idenb[:, :]),
                reads=["ker_bf", "idenb"], writes=["ptb"])
            p.op("vector", lambda v, fi=fi: v.tensor_copy(row_sb[:, :], ptb[:, :]),
                 reads=["ptb"], writes=["row_sb"])
            p.dma("sync", kqt_o[fi, :, ti * 128:(ti + 1) * 128], row_sb[:, :],
                  reads=["row_sb"])
        featmap(k_tok, "k_tok", 1, ker_bf, "ker_bf")
        p.dma("sync", kk_o[ti * 128:(ti + 1) * 128, :], ker_bf[:, :], reads=["ker_bf"])
        for fi in range(8):
            p.op("tensor", lambda t, fi=fi: t.transpose(
                ptb[:, :], ker_bf[:, fi * 128:(fi + 1) * 128], idenb[:, :]),
                reads=["ker_bf", "idenb"], writes=["ptb"])
            p.op("vector", lambda v, fi=fi: v.tensor_copy(row_sb[:, :], ptb[:, :]),
                 reads=["ptb"], writes=["row_sb"])
            p.dma("sync", kkt_o[fi, :, ti * 128:(ti + 1) * 128], row_sb[:, :],
                  reads=["row_sb"])
    return ctx.finish()


def build_linattn():
    ctx = Ctx("linattn")
    nc, p = ctx.nc, ctx.p
    kqt = ctx.inp("kqt", [2, 64, T], BF16)
    kkt = ctx.inp("kkt", [2, 64, T], BF16)
    kk_tok = ctx.inp("kk_tok", [T, 128], BF16)
    v_tok = ctx.inp("v_tok", [T, 128], BF16)
    masks = ctx.inp("masks", [4, 128, 512], BF16)
    ones = ctx.inp("ones", [1, 128])
    ot = ctx.out("ot", [2, 64, T], BF16)
    NCC = S // 512

    kq_sb = ctx.sb("kq_sb", [64, S], BF16)
    kk_sb = ctx.sb("kk_sb", [64, S], BF16)
    kkt_tok = ctx.sb("kkt_tok", [128, 64 * 64], BF16)
    vext = ctx.sb("vext", [128, 64 * 65], BF16)
    m_sb = ctx.sb("m_sb", [128, 4 * 512], BF16)
    ones_sb = ctx.sb("ones_sb", [1, 128], F32)
    s_sb = ctx.sb("s_sb", [64, 65], BF16)
    at_sb = [ctx.sb(f"at{i}", [128, 512], BF16) for i in range(2)]
    zrow = ctx.sb("zrow", [1, 512], F32)
    rb_sb = ctx.sb("rb_sb", [64, 512], F32)
    otc = ctx.sb("otc", [64, 512], BF16)
    pA = [ctx.ps(f"pA{i}", [128, 512], F32) for i in range(2)]
    pO = ctx.ps("pO", [65, 512], F32)
    pSt = ctx.ps("pSt", [64, 65], F32)
    pB = ctx.ps("pB", [64, 512], F32)

    p.dma("sync", ones_sb[:, :], ones, writes=["ones_sb"])
    p.dma("sync", m_sb[:, :].rearrange("p (a c) -> p a c", a=4), masks,
          writes=["m_sb"])
    for hi in range(2):
        for b in range(2):
            tcol = b * S
            p.dma("sync", kq_sb[:, :], kqt[hi, :, tcol:tcol + S], writes=["kq_sb"])
            p.dma("sync", kk_sb[:, :], kkt[hi, :, tcol:tcol + S], writes=["kk_sb"])
            p.dma("sync", kkt_tok[:, :].rearrange("p (k d) -> p k d", d=64),
                  kk_tok[tcol:tcol + S, hi * 64:(hi + 1) * 64].rearrange(
                      "(k p) d -> p k d", p=128),
                  writes=["kkt_tok"])
            p.dma("sync", vext[:, :].rearrange("p (k e) -> p k e", e=65)[:, :, 0:64],
                  v_tok[tcol:tcol + S, hi * 64:(hi + 1) * 64].rearrange(
                      "(k p) d -> p k d", p=128),
                  writes=["vext"])
            p.op("vector", lambda v: v.memset(
                vext[:, :].rearrange("p (k e) -> p k e", e=65)[:, :, 64:65], 1.0),
                writes=["vext1"])
            for cc in range(NCC):
                ccol = cc * 512
                if cc > 0:
                    p.op("vector", lambda v: v.tensor_copy(s_sb[:, :], pSt[:, :]),
                         reads=["pSt"], writes=["s_sb"])
                for kt_i in range(4):
                    r = kt_i % 2
                    p.op("tensor", lambda t, r=r, kt_i=kt_i, ccol=ccol: t.matmul(
                        pA[r][:, :], kk_sb[:, ccol + kt_i * 128: ccol + (kt_i + 1) * 128],
                        kq_sb[:, ccol:ccol + 512], start=True, stop=True),
                        reads=["kk_sb", "kq_sb"], writes=[f"pA{r}"])
                    p.op("vector", lambda v, r=r, kt_i=kt_i: v.tensor_tensor(
                        at_sb[r][:, :], pA[r][:, :],
                        m_sb[:, kt_i * 512:(kt_i + 1) * 512], op=ALU.mult),
                        reads=[f"pA{r}", "m_sb"], writes=[f"at{r}"])
                    p.op("tensor", lambda t, r=r, kt_i=kt_i, cc=cc: t.matmul(
                        pO[:, :], vext[:, (cc * 4 + kt_i) * 65:(cc * 4 + kt_i + 1) * 65],
                        at_sb[r][:, :], start=(kt_i == 0),
                        stop=(cc == 0 and kt_i == 3)),
                        reads=["vext", "vext1", f"at{r}"], writes=["pO"])
                if cc > 0:
                    p.op("tensor", lambda t, ccol=ccol: t.matmul(
                        pO[:, :], s_sb[:, :], kq_sb[:, ccol:ccol + 512],
                        start=False, stop=True),
                        reads=["s_sb", "kq_sb"], writes=["pO"])
                for kt_i in range(4):
                    p.op("tensor", lambda t, cc=cc, kt_i=kt_i: t.matmul(
                        pSt[:, :], kkt_tok[:, (cc * 4 + kt_i) * 64:(cc * 4 + kt_i + 1) * 64],
                        vext[:, (cc * 4 + kt_i) * 65:(cc * 4 + kt_i + 1) * 65],
                        start=(cc == 0 and kt_i == 0), stop=(cc == NCC - 1 and kt_i == 3)),
                        reads=["kkt_tok", "vext", "vext1"], writes=["pSt"])
                p.op("vector", lambda v: v.tensor_scalar(
                    zrow[:, :], pO[64:65, :], 1e-6, None, op0=ALU.add),
                    reads=["pO"], writes=["zrow"])
                p.op("vector", lambda v: v.reciprocal(zrow[:, :], zrow[:, :]),
                     reads=["zrow"], writes=["zrow"])
                p.op("tensor", lambda t: t.matmul(
                    pB[:, :], ones_sb[:, 0:64], zrow[:, :], start=True, stop=True),
                    reads=["ones_sb", "zrow"], writes=["pB"])
                p.op("vector", lambda v: v.tensor_copy(rb_sb[:, :], pB[:, :]),
                     reads=["pB"], writes=["rb_sb"])
                p.op("vector", lambda v: v.tensor_tensor(
                    otc[:, :], pO[0:64, :], rb_sb[:, :], op=ALU.mult),
                    reads=["pO", "rb_sb"], writes=["otc"])
                p.dma("sync", ot[hi, :, tcol + ccol: tcol + ccol + 512], otc[:, :],
                      reads=["otc"])
    return ctx.finish()


def build_l1attn():
    ctx = Ctx("l1attn")
    nc, p = ctx.nc, ctx.p
    KE = TPC + 512
    NKT = KE // 128
    NQT = TPC // 128
    yT = ctx.inp("yT", [D, KE + 1])
    wqkv = ctx.inp("wqkv", [128, 8, 1536])
    wo = ctx.inp("wo", [128, 4, D])
    qs = ctx.inp("qs", [1, 512])
    ks = ctx.inp("ks", [1, 512])
    tabq = ctx.inp("tabq", [2, TPC, 512])
    tabk = ctx.inp("tabk", [2, KE, 512])
    gates = ctx.inp("gates", [2, 2, 4, 128, 512], BF16)
    ident = ctx.inp("ident", [128, 128])
    ones = ctx.inp("ones", [1, 128])
    xmT = ctx.out("xmT", [D, TPC])

    hT = ctx.sb("hT", [128, 8 * KE], BF16)
    w_sb = ctx.sb("w_sb", [128, 8 * 1536], BF16)
    wo_sb = ctx.sb("wo_sb", [128, 4 * D], BF16)
    qp_tok = ctx.sb("qp_tok", [128, NQT * 512], BF16)
    kp_tok = ctx.sb("kp_tok", [128, NKT * 512], BF16)
    v_tok = ctx.sb("v_tok", [128, NKT * 512], BF16)
    otsb = ctx.sb("otsb", [128, 4 * TPC], BF16)
    qtx = ctx.sb("qtx", [65, TPC], BF16)
    ktx = ctx.sb("ktx", [65, KE], BF16)
    vext = ctx.sb("vext", [128, NKT * 65], BF16)
    gat = ctx.sb("gat", [128, 2 * 4 * 512], BF16)
    iden32 = ctx.sb("iden32", [128, 128], F32)
    idenb = ctx.sb("idenb", [128, 128], BF16)
    ones_sb = ctx.sb("ones_sb", [1, 128], F32)
    sc_row = ctx.sb("sc_row", [1, 512], F32)
    qsb = ctx.sb("qsb", [128, 512], F32)
    ksb = ctx.sb("ksb", [128, 512], F32)
    stg = ctx.sb("stg", [128, 512], F32)
    sqf = ctx.sb("sqf", [128, 512], F32)
    nh = ctx.sb("nh", [128, 8], F32)
    tA = ctx.sb("tA", [128, 512], F32)
    tB = ctx.sb("tB", [128, 512], F32)
    rot = ctx.sb("rot", [128, 512], F32)
    t1 = ctx.sb("t1", [128, 512], F32)
    mkd = ctx.sb("mkd", [64, 1], F32)
    mkd_bf = ctx.sb("mkd_bf", [64, 1], BF16)
    qa_bf = ctx.sb("qa_bf", [64, 512], BF16)
    pexp = [ctx.sb(f"pexp{i}", [128, 512], BF16) for i in range(2)]
    ptl = [ctx.sb(f"ptl{i}", [128, 512], BF16) for i in range(2)]
    rec = ctx.sb("rec", [1, 512], F32)
    rb_sb = ctx.sb("rb_sb", [64, 512], F32)
    pq = ctx.ps("pq", [128, 512], F32)
    pS = [ctx.ps(f"pS{i}", [128, 512], F32) for i in range(2)]
    pO = ctx.ps("pO", [65, 512], F32)
    pT = ctx.ps("pT", [128, 128], BF16)
    pN = ctx.ps("pN", [1, 512], F32)
    pB = ctx.ps("pB", [64, 512], F32)

    p.dma("sync", iden32[:, :], ident, writes=["iden32"])
    p.op("vector", lambda v: v.tensor_copy(idenb[:, :], iden32[:, :]),
         reads=["iden32"], writes=["idenb"])
    p.dma("sync", ones_sb[:, :], ones, writes=["ones_sb"])
    p.dma("gpsimd", w_sb[:, :].rearrange("p (a j) -> p a j", a=8), wqkv,
          writes=["w_sb"])
    p.dma("gpsimd", wo_sb[:, :].rearrange("p (a m) -> p a m", a=4), wo,
          writes=["wo_sb"])

    for (row, dst, kk) in ((qs, qsb, "qsb"), (ks, ksb, "ksb")):
        p.dma("sync", sc_row[:, :], row, writes=["sc_row"])
        p.op("tensor", lambda t: t.matmul(
            pq[:, :], ones_sb[:, :], sc_row[:, :], start=True, stop=True),
            reads=["ones_sb", "sc_row"], writes=["pq"])
        p.op("vector", lambda v, dst=dst, kk=kk: v.tensor_copy(dst[:, :], pq[:, :]),
             reads=["pq"], writes=[kk])
    for fi in range(8):
        off = 1 if fi < 4 else 0
        p.dma("gpsimd", hT[:, fi * KE:(fi + 1) * KE],
              yT[fi * 128:(fi + 1) * 128, off:off + KE], writes=["hT"])

    for kti in range(NKT):
        kcol = kti * 128
        is_q = kti >= 4
        qi = kti - 4
        jcs = ([0] if is_q else []) + [1, 2]
        for jc in jcs:
            for fi in range(8):
                p.op("tensor", lambda t, jc=jc, fi=fi, kcol=kcol: t.matmul(
                    pq[:, :], hT[:, fi * KE + kcol: fi * KE + kcol + 128],
                    w_sb[:, fi * 1536 + jc * 512: fi * 1536 + (jc + 1) * 512],
                    start=(fi == 0), stop=(fi == 7)),
                    reads=["hT", "w_sb"], writes=["pq"])
            if jc == 2:
                p.op("vector", lambda v, kti=kti: v.tensor_copy(
                    v_tok[:, kti * 512:(kti + 1) * 512], pq[:, :]),
                    reads=["pq"], writes=["v_tok"])
            elif jc == 0:
                p.op("vector", lambda v: v.tensor_copy(stg[:, :], pq[:, :]),
                     reads=["pq"], writes=["stg"])
                _l2rope(ctx, stg, "stg", qsb, "qsb", tabq, qi * 128, qp_tok,
                        qi * 512, "qp_tok", sqf, nh, tA, tB, rot, t1)
            else:
                p.op("vector", lambda v: v.tensor_copy(stg[:, :], pq[:, :]),
                     reads=["pq"], writes=["stg"])
                _l2rope(ctx, stg, "stg", ksb, "ksb", tabk, kti * 128, kp_tok,
                        kti * 512, "kp_tok", sqf, nh, tA, tB, rot, t1)
    for h in range(8):
        for ti in range(NQT):
            p.op("tensor", lambda t, ti=ti, h=h: t.transpose(
                pT[0:64, :], qp_tok[:, ti * 512 + h * 64: ti * 512 + (h + 1) * 64],
                idenb[:, :]), reads=["qp_tok", "idenb"], writes=["pT"])
            p.op("vector", lambda v, ti=ti: v.tensor_copy(
                qtx[0:64, ti * 128:(ti + 1) * 128], pT[0:64, :]),
                reads=["pT"], writes=["qtx"])
        for ti in range(NKT):
            p.op("tensor", lambda t, ti=ti, h=h: t.transpose(
                pT[0:64, :], kp_tok[:, ti * 512 + h * 64: ti * 512 + (h + 1) * 64],
                idenb[:, :]), reads=["kp_tok", "idenb"], writes=["pT"])
            p.op("vector", lambda v, ti=ti: v.tensor_copy(
                ktx[0:64, ti * 128:(ti + 1) * 128], pT[0:64, :]),
                reads=["pT"], writes=["ktx"])
            p.op("vector", lambda v, ti=ti, h=h: v.tensor_copy(
                vext[:, :].rearrange("p (k e) -> p k e", e=65)[:, ti, 0:64],
                v_tok[:, ti * 512 + h * 64: ti * 512 + (h + 1) * 64]),
                reads=["v_tok"], writes=["vext"])
        p.op("vector", lambda v: v.memset(ktx[64:65, :], 1.0), writes=["ktx1"])
        p.op("vector", lambda v: v.memset(
            vext[:, :].rearrange("p (k e) -> p k e", e=65)[:, :, 64:65], 1.0),
            writes=["vext1"])
        for qb in range(4):
            p.op("vector", lambda v, qb=qb: v.reduce_max(
                mkd[0:64, :], ktx[0:64, qb * 512: qb * 512 + 1024],
                axis=AX.X, apply_absolute_value=True),
                reads=["ktx"], writes=["mkd"])
            p.op("vector", lambda v: v.tensor_copy(mkd_bf[0:64, :], mkd[0:64, :]),
                 reads=["mkd"], writes=["mkd_bf"])
            p.op("scalar", lambda s, qb=qb: s.activation(
                qa_bf[0:64, :], qtx[0:64, qb * 512:(qb + 1) * 512], AF.Abs),
                reads=["qtx"], writes=["qa_bf"])
            p.op("tensor", lambda t: t.matmul(
                pN[0:1, 0:512], mkd_bf[0:64, :], qa_bf[0:64, :],
                start=True, stop=True),
                reads=["mkd_bf", "qa_bf"], writes=["pN"])
            p.op("vector", lambda v, qb=qb: v.tensor_scalar(
                qtx[64:65, qb * 512:(qb + 1) * 512], pN[0:1, 0:512],
                -1.0, None, op0=ALU.mult),
                reads=["pN"], writes=["qtxc"])
            qcls = 0 if qb == 0 else 1
            if qb <= 1:
                p.dma("sync", gat[:, :].rearrange("p (b c d) -> p b c d", b=2, c=4),
                      gates[qcls].rearrange("b c p d -> p b c d"),
                      writes=["gat0", "gat1"])
            for kt_i in range(8):
                r = kt_i % 2
                kt_g = qb * 4 + kt_i
                p.op("tensor", lambda t, r=r, kt_g=kt_g, qb=qb: t.matmul(
                    pS[r][:, :], ktx[:, kt_g * 128:(kt_g + 1) * 128],
                    qtx[:, qb * 512:(qb + 1) * 512], start=True, stop=True),
                    reads=["ktx", "ktx1", "qtx", "qtxc"], writes=[f"pS{r}"])
                p.op("scalar", lambda s, r=r: s.activation(
                    pexp[r][:, :], pS[r][:, :], AF.Exp, scale=8.0),
                    reads=[f"pS{r}"], writes=[f"pexp{r}"])
                goff = (kt_i // 4) * 4 + (kt_i % 4)
                p.op("vector", lambda v, r=r, goff=goff: v.tensor_tensor(
                    ptl[r][:, :], pexp[r][:, :],
                    gat[:, goff * 512:(goff + 1) * 512], op=ALU.mult),
                    reads=[f"pexp{r}", f"gat{qcls}"], writes=[f"ptl{r}"])
                p.op("tensor", lambda t, r=r, kt_g=kt_g, kt_i=kt_i: t.matmul(
                    pO[:, :], vext[:, kt_g * 65:(kt_g + 1) * 65], ptl[r][:, :],
                    start=(kt_i == 0), stop=(kt_i == 7)),
                    reads=["vext", "vext1", f"ptl{r}"], writes=["pO"])
            p.op("vector", lambda v: v.tensor_scalar(rec[:, :], pO[64:65, :], 1e-30,
                                                     None, op0=ALU.add),
                 reads=["pO"], writes=["rec"])
            p.op("vector", lambda v: v.reciprocal(rec[:, :], rec[:, :]),
                 reads=["rec"], writes=["rec"])
            p.op("tensor", lambda t: t.matmul(
                pB[:, :], ones_sb[:, 0:64], rec[:, :], start=True, stop=True),
                reads=["ones_sb", "rec"], writes=["pB"])
            p.op("vector", lambda v: v.tensor_copy(rb_sb[:, :], pB[:, :]),
                 reads=["pB"], writes=["rb_sb"])
            hp = h // 2
            hq = (h % 2) * 64
            p.op("vector", lambda v, hp=hp, hq=hq, qb=qb: v.tensor_tensor(
                otsb[hq:hq + 64, hp * TPC + qb * 512: hp * TPC + (qb + 1) * 512],
                pO[0:64, :], rb_sb[:, :], op=ALU.mult),
                reads=["pO", "rb_sb"], writes=[f"ot{hp}"])
    for tc in range(4):
        for mt in range(8):
            for ki in range(4):
                p.op("tensor", lambda t, tc=tc, mt=mt, ki=ki: t.matmul(
                    pq[:, :], wo_sb[:, ki * D + mt * 128: ki * D + (mt + 1) * 128],
                    otsb[:, ki * TPC + tc * 512: ki * TPC + (tc + 1) * 512],
                    start=(ki == 0), stop=(ki == 3)),
                    reads=["wo_sb", "ot0", "ot1", "ot2", "ot3"], writes=["pq"])
            p.dma("sync", stg[:, :],
                  yT[mt * 128:(mt + 1) * 128, 513 + tc * 512: 513 + (tc + 1) * 512],
                  writes=["stg"])
            p.op("vector", lambda v: v.tensor_tensor(
                sqf[:, :], pq[:, :], stg[:, :], op=ALU.add),
                reads=["pq", "stg"], writes=["sqf"])
            p.dma("sync", xmT[mt * 128:(mt + 1) * 128, tc * 512:(tc + 1) * 512],
                  sqf[:, :], reads=["sqf"])
    return ctx.finish()


def _l2rope(ctx, stg, kstg, scb, kscb, tab, trow, outb, ocol, kout, sqf, nh,
            tA, tB, rot, t1):
    p = ctx.p
    p.op("scalar", lambda s: s.activation(sqf[:, :], stg[:, :], AF.Square),
         reads=[kstg], writes=["sqf"])
    p.op("vector", lambda v: v.reduce_sum(
        nh[:, :], sqf[:, :].rearrange("p (h d) -> p h d", d=64), axis=AX.X),
        reads=["sqf"], writes=["nh"])
    p.op("vector", lambda v: v.tensor_scalar(nh[:, :], nh[:, :], 1e-24, None,
                                             op0=ALU.add),
         reads=["nh"], writes=["nh"])
    p.op("vector", lambda v: v.reciprocal(nh[:, :], nh[:, :]),
         reads=["nh"], writes=["nh"])
    p.op("scalar", lambda s: s.activation(nh[:, :], nh[:, :], AF.Sqrt),
         reads=["nh"], writes=["nh"])
    for h in range(8):
        p.op("vector", lambda v, h=h: v.tensor_scalar(
            stg[:, h * 64:(h + 1) * 64], stg[:, h * 64:(h + 1) * 64],
            nh[:, h:h + 1], None, op0=ALU.mult),
            reads=[kstg, "nh"], writes=[kstg])
    p.op("vector", lambda v: v.tensor_tensor(stg[:, :], stg[:, :], scb[:, :],
                                             op=ALU.mult),
         reads=[kstg, kscb], writes=[kstg])
    p.dma("sync", tA[:, :], tab[0, trow:trow + 128, :], writes=["tA"])
    p.dma("sync", tB[:, :], tab[1, trow:trow + 128, :], writes=["tB"])
    p.op("vector", lambda v: v.tensor_copy(
        rot[:, :].rearrange("p (h x j) -> p h x j", h=8, x=2)[:, :, 0, :],
        stg[:, :].rearrange("p (h x j) -> p h x j", h=8, x=2)[:, :, 1, :]),
        reads=[kstg], writes=["rot"])
    p.op("vector", lambda v: v.tensor_copy(
        rot[:, :].rearrange("p (h x j) -> p h x j", h=8, x=2)[:, :, 1, :],
        stg[:, :].rearrange("p (h x j) -> p h x j", h=8, x=2)[:, :, 0, :]),
        reads=[kstg], writes=["rot"])
    p.op("vector", lambda v: v.tensor_tensor(t1[:, :], stg[:, :], tA[:, :],
                                             op=ALU.mult),
         reads=[kstg, "tA"], writes=["t1"])
    p.op("vector", lambda v: v.tensor_tensor(rot[:, :], rot[:, :], tB[:, :],
                                             op=ALU.mult),
         reads=["rot", "tB"], writes=["rot"])
    p.op("vector", lambda v, ocol=ocol: v.tensor_tensor(
        outb[:, ocol:ocol + 512], t1[:, :], rot[:, :], op=ALU.add),
        reads=["t1", "rot"], writes=[kout])


def build_l1ffn():
    ctx = Ctx("l1ffn")
    nc, p = ctx.nc, ctx.p
    NMID = 22
    xm = ctx.inp("xm", [D, TPC + 1])
    lng = ctx.inp("lng", [8, 128, 1])
    lnb = ctx.inp("lnb", [8, 128, 1])
    w1a = ctx.inp("w1a", [NMID, 128, 8, 128])
    w1g = ctx.inp("w1g", [NMID, 128, 8, 128])
    b1a = ctx.inp("b1a", [NMID, 128, 1])
    b1g = ctx.inp("b1g", [NMID, 128, 1])
    w2 = ctx.inp("w2", [8, 128, NMID, 128])
    b2 = ctx.inp("b2", [8, 128, 1])
    ones = ctx.inp("ones", [1, 128])
    yout = ctx.out("yout", [D, TPC])

    hT = ctx.sb("hT", [128, 8 * TPC], BF16)
    ones_sb = ctx.sb("ones_sb", [1, 128], F32)
    ones_bf = ctx.sb("ones_bf", [128, 1], BF16)
    lng_sb = ctx.sb("lng_sb", [128, 8], F32)
    lnb_sb = ctx.sb("lnb_sb", [128, 8], F32)
    b1a_sb = ctx.sb("b1a_sb", [128, NMID], F32)
    b1g_sb = ctx.sb("b1g_sb", [128, NMID], F32)
    b2_sb = ctx.sb("b2_sb", [128, 8], F32)
    w1abuf = [ctx.sb(f"w1a{i}", [128, 8 * 128], BF16) for i in range(2)]
    w1gbuf = [ctx.sb(f"w1g{i}", [128, 8 * 128], BF16) for i in range(2)]
    w2buf = [ctx.sb(f"w2_{i}", [128, NMID * 128], BF16) for i in range(2)]
    sqt = ctx.sb("sqt", [128, 512], BF16)
    mrow = ctx.sb("mrow", [1, 512], F32)
    vrow = ctx.sb("vrow", [1, 512], F32)
    m2row = ctx.sb("m2row", [1, 512], F32)
    nmb = ctx.sb("nmb", [128, 512], F32)
    rsb = ctx.sb("rsb", [128, 512], F32)
    t1f = ctx.sb("t1f", [128, 512], F32)
    xn_sb = ctx.sb("xn_sb", [128, 8 * 512], BF16)
    a_sb = ctx.sb("a_sb", [128, 512], BF16)
    g_sb = ctx.sb("g_sb", [128, 512], BF16)
    mid_sb = ctx.sb("mid_sb", [128, NMID * 512], BF16)
    xmst = ctx.sb("xmst", [128, 512], F32)
    out_sb = ctx.sb("out_sb", [128, 512], F32)
    pSum = ctx.ps("pSum", [1, 512], F32)
    pSq = ctx.ps("pSq", [1, 512], F32)
    pA = ctx.ps("pA", [128, 512], F32)
    pG = ctx.ps("pG", [128, 512], F32)
    pW = [ctx.ps(f"pW{i}", [128, 512], F32) for i in range(2)]
    pB1 = ctx.ps("pB1", [128, 512], F32)

    p.dma("sync", ones_sb[:, :], ones, writes=["ones_sb"])
    p.op("vector", lambda v: v.memset(ones_bf[:, :], 1.0), writes=["ones_bf"])
    p.dma("sync", lng_sb[:, :].rearrange("p (a o) -> p a o", o=1),
          lng.rearrange("a p o -> p a o"), writes=["lng_sb"],
          allow_slow_non_contiguous=True)
    p.dma("sync", lnb_sb[:, :].rearrange("p (a o) -> p a o", o=1),
          lnb.rearrange("a p o -> p a o"), writes=["lnb_sb"],
          allow_slow_non_contiguous=True)
    p.dma("sync", b1a_sb[:, :].rearrange("p (a o) -> p a o", o=1),
          b1a.rearrange("a p o -> p a o"), writes=["b1a_sb"],
          allow_slow_non_contiguous=True)
    p.dma("sync", b1g_sb[:, :].rearrange("p (a o) -> p a o", o=1),
          b1g.rearrange("a p o -> p a o"), writes=["b1g_sb"],
          allow_slow_non_contiguous=True)
    p.dma("sync", b2_sb[:, :].rearrange("p (a o) -> p a o", o=1),
          b2.rearrange("a p o -> p a o"), writes=["b2_sb"],
          allow_slow_non_contiguous=True)
    for fi in range(8):
        off = 1 if fi < 4 else 0
        p.dma("gpsimd", hT[:, fi * TPC:(fi + 1) * TPC],
              xm[fi * 128:(fi + 1) * 128, off:off + TPC], writes=["hT"])

    for tc in range(4):
        tcol = tc * 512
        for fi in range(8):
            p.op("tensor", lambda t, fi=fi, tcol=tcol: t.matmul(
                pSum[:, :], ones_bf[:, :], hT[:, fi * TPC + tcol: fi * TPC + tcol + 512],
                start=(fi == 0), stop=(fi == 7)),
                reads=["ones_bf", "hT"], writes=["pSum"])
        for fi in range(8):
            p.op("scalar", lambda s, fi=fi, tcol=tcol: s.activation(
                sqt[:, :], hT[:, fi * TPC + tcol: fi * TPC + tcol + 512], AF.Square),
                reads=["hT"], writes=["sqt"])
            p.op("tensor", lambda t, fi=fi: t.matmul(
                pSq[:, :], ones_bf[:, :], sqt[:, :],
                start=(fi == 0), stop=(fi == 7)),
                reads=["ones_bf", "sqt"], writes=["pSq"])
        p.op("vector", lambda v: v.tensor_scalar(mrow[:, :], pSum[:, :], 1.0 / D,
                                                 None, op0=ALU.mult),
             reads=["pSum"], writes=["mrow"])
        p.op("vector", lambda v: v.tensor_tensor(m2row[:, :], mrow[:, :], mrow[:, :],
                                                 op=ALU.mult),
             reads=["mrow"], writes=["m2row"])
        p.op("vector", lambda v: v.scalar_tensor_tensor(
            vrow[:, :], pSq[:, :], 1.0 / D, m2row[:, :], op0=ALU.mult,
            op1=ALU.subtract),
            reads=["pSq", "m2row"], writes=["vrow"])
        ecol, ekey = ctx.cst(1e-5)
        p.op("scalar", lambda s, ecol=ecol: s.activation(vrow[:, :], vrow[:, :],
                                                         AF.Ln, bias=ecol[0:1, :]),
             reads=["vrow", ekey], writes=["vrow"])
        p.op("scalar", lambda s: s.activation(vrow[:, :], vrow[:, :], AF.Exp,
                                              scale=-0.5),
             reads=["vrow"], writes=["vrow"])
        p.op("vector", lambda v: v.tensor_scalar(mrow[:, :], mrow[:, :], -1.0, None,
                                                 op0=ALU.mult),
             reads=["mrow"], writes=["mrow"])
        p.op("tensor", lambda t: t.matmul(pB1[:, :], ones_sb[:, :], mrow[:, :],
                                          start=True, stop=True),
             reads=["ones_sb", "mrow"], writes=["pB1"])
        p.op("vector", lambda v: v.tensor_copy(nmb[:, :], pB1[:, :]),
             reads=["pB1"], writes=["nmb"])
        p.op("tensor", lambda t: t.matmul(pB1[:, :], ones_sb[:, :], vrow[:, :],
                                          start=True, stop=True),
             reads=["ones_sb", "vrow"], writes=["pB1"])
        p.op("vector", lambda v: v.tensor_copy(rsb[:, :], pB1[:, :]),
             reads=["pB1"], writes=["rsb"])
        for fi in range(8):
            p.op("vector", lambda v, fi=fi, tcol=tcol: v.tensor_tensor(
                t1f[:, :], hT[:, fi * TPC + tcol: fi * TPC + tcol + 512], nmb[:, :],
                op=ALU.add),
                reads=["hT", "nmb"], writes=["t1f"])
            p.op("vector", lambda v: v.tensor_tensor(t1f[:, :], t1f[:, :], rsb[:, :],
                                                     op=ALU.mult),
                 reads=["t1f", "rsb"], writes=["t1f"])
            p.op("scalar", lambda s, fi=fi: s.activation(
                xn_sb[:, fi * 512:(fi + 1) * 512], t1f[:, :], AF.Identity,
                bias=lnb_sb[:, fi:fi + 1], scale=lng_sb[:, fi:fi + 1]),
                reads=["t1f", "lng_sb", "lnb_sb"], writes=["xn_sb"])
        for mi in range(NMID):
            ba = w1abuf[mi % 2]
            bg = w1gbuf[mi % 2]
            ka, kg = f"w1a{mi % 2}", f"w1g{mi % 2}"
            p.dma("gpsimd", ba[:, :].rearrange("p (a b) -> p a b", a=8), w1a[mi],
                  writes=[ka])
            p.dma("gpsimd", bg[:, :].rearrange("p (a b) -> p a b", a=8), w1g[mi],
                  writes=[kg])
            for fi in range(8):
                p.op("tensor", lambda t, ba=ba, fi=fi, ka=ka: t.matmul(
                    pA[:, :], ba[:, fi * 128:(fi + 1) * 128],
                    xn_sb[:, fi * 512:(fi + 1) * 512],
                    start=(fi == 0), stop=(fi == 7)),
                    reads=[ka, "xn_sb"], writes=["pA"])
                p.op("tensor", lambda t, bg=bg, fi=fi, kg=kg: t.matmul(
                    pG[:, :], bg[:, fi * 128:(fi + 1) * 128],
                    xn_sb[:, fi * 512:(fi + 1) * 512],
                    start=(fi == 0), stop=(fi == 7)),
                    reads=[kg, "xn_sb"], writes=["pG"])
            p.op("scalar", lambda s, mi=mi: s.activation(
                a_sb[:, :], pA[:, :], AF.Identity, bias=b1a_sb[:, mi:mi + 1]),
                reads=["pA", "b1a_sb"], writes=["a_sb"])
            p.op("scalar", lambda s, mi=mi: s.activation(
                g_sb[:, :], pG[:, :], AF.Gelu, bias=b1g_sb[:, mi:mi + 1]),
                reads=["pG", "b1g_sb"], writes=["g_sb"])
            p.op("vector", lambda v, mi=mi: v.tensor_tensor(
                mid_sb[:, mi * 512:(mi + 1) * 512], a_sb[:, :], g_sb[:, :],
                op=ALU.mult),
                reads=["a_sb", "g_sb"], writes=[f"mid{mi}"])
        for mt in range(8):
            bw = w2buf[mt % 2]
            kw = f"w2_{mt % 2}"
            p.dma("gpsimd", bw[:, :].rearrange("p (a b) -> p a b", a=NMID), w2[mt],
                  writes=[kw])
            r = mt % 2
            for mi in range(NMID):
                p.op("tensor", lambda t, bw=bw, mi=mi, r=r, kw=kw: t.matmul(
                    pW[r][:, :], bw[:, mi * 128:(mi + 1) * 128],
                    mid_sb[:, mi * 512:(mi + 1) * 512],
                    start=(mi == 0), stop=(mi == NMID - 1)),
                    reads=[kw, f"mid{mi}"], writes=[f"pW{r}"])
            p.dma("sync", xmst[:, :],
                  xm[mt * 128:(mt + 1) * 128, 1 + tcol: 1 + tcol + 512],
                  writes=["xmst"])
            p.op("vector", lambda v, mt=mt, r=r: v.scalar_tensor_tensor(
                out_sb[:, :], pW[r][:, :], b2_sb[:, mt:mt + 1], xmst[:, :],
                op0=ALU.add, op1=ALU.add),
                reads=[f"pW{r}", "b2_sb", "xmst"], writes=["out_sb"])
            p.dma("sync", yout[mt * 128:(mt + 1) * 128, tcol:tcol + 512],
                  out_sb[:, :], reads=["out_sb"])
    return ctx.finish()


_PROGS = {}


def _get(name, builder):
    if name not in _PROGS:
        _PROGS[name] = builder()
    return _PROGS[name]


_IDENT = np.eye(128, dtype=np.float32)
_ONES = np.ones((1, 128), np.float32)


def _tile_w(w, part=128):
    K, M = w.shape
    return np.ascontiguousarray(
        np.asarray(w, np.float32).reshape(K // part, part, M).transpose(1, 0, 2))


def _run(name, builder, in_maps, outs):
    nc = _get(name, builder)
    res = run_bass_kernel_spmd(nc, in_maps, core_ids=list(range(NCORES)))
    return [[r[o] for r in res.results] for o in outs]


def run_moe(xfull, wr, wg, wu, wd, sg, su, sd):
    in_maps = []
    for c in range(NCORES):
        in_maps.append(dict(
            x=_f32(xfull[c * TPC:(c + 1) * TPC]),
            wr=_f32(wr), wg=_f32(wg), wu=_f32(wu), wd=_f32(wd),
            sg=_f32(sg), su=_f32(su), sd=_f32(sd), ident=_IDENT))
    (xs,) = _run("moe", build_moe, in_maps, ["xout"])
    return np.concatenate(xs, axis=0)


def run_l0attn(xfull, qkv_w, out_w, bias_p, bias_a, rope_freqs):
    wtile = _tile_w(np.asarray(qkv_w, np.float32).T)
    sgn = np.concatenate([np.concatenate([-np.ones(32), np.ones(32)])] * 16)[None]
    sgn = sgn.astype(np.float32)
    in_maps = []
    for c in range(NCORES):
        posv = (np.arange(c * TPC, (c + 1) * TPC) % S).astype(np.float32)[None]
        in_maps.append(dict(
            x=_f32(xfull[c * TPC:(c + 1) * TPC]), wqkv=wtile,
            freqs=_f32(np.asarray(rope_freqs).reshape(1, 32)), pos=posv,
            sgnmask=sgn, ident=_IDENT, ones=_ONES))
    qs, ks, vs = _run("qkv", build_qkv, in_maps, ["qt_o", "kt_o", "v_o"])
    QT = np.concatenate(qs, axis=2)
    KT = np.concatenate(ks, axis=2)
    V = np.concatenate(vs, axis=1)
    dist_c = np.abs(np.arange(16384)[None, :] - 8064
                    - np.arange(128)[:, None]).astype(np.int16)
    bp = np.asarray(bias_p, np.float32).reshape(16)
    ba = np.asarray(bias_a, np.float32).reshape(16)
    in_maps = []
    for c in range(NCORES):
        h0 = 2 * c
        pa = np.stack([bp[h0:h0 + 2], ba[h0:h0 + 2]], axis=1).reshape(1, 4)
        in_maps.append(dict(qt=QT[h0:h0 + 2], kt=KT[h0:h0 + 2], vv=V[h0:h0 + 2],
                            pa=_f32(pa), dist=dist_c, ones=_ONES))
    (ots,) = _run("flash", build_flash, in_maps, ["ot"])
    OT = np.concatenate(ots, axis=0).reshape(1024, T)
    return run_outproj(OT, np.asarray(out_w, np.float32).T,
                       np.zeros((1, D), np.float32), xfull)


def run_outproj(OT, w_in_out, bias, xfull):
    owt = _tile_w(w_in_out)
    in_maps = []
    for c in range(NCORES):
        in_maps.append(dict(ot=np.ascontiguousarray(OT[:, c * TPC:(c + 1) * TPC]),
                            w=owt, bias=_f32(bias),
                            x=_f32(xfull[c * TPC:(c + 1) * TPC]), ones=_ONES))
    (xs,) = _run("outproj", build_outproj, in_maps, ["xout"])
    return np.concatenate(xs, axis=0)


def run_lnT(xfull):
    in_maps = [dict(x=_f32(xfull[c * TPC:(c + 1) * TPC]), ident=_IDENT)
               for c in range(NCORES)]
    (ys,) = _run("lnT", build_lnT, in_maps, ["yT"])
    return np.concatenate(ys, axis=1)


def run_addhalf(xfull, yT):
    in_maps = []
    for c in range(NCORES):
        in_maps.append(dict(x=_f32(xfull[c * TPC:(c + 1) * TPC]),
                            yT=np.ascontiguousarray(yT[:, c * TPC:(c + 1) * TPC]),
                            ident=_IDENT))
    (xs,) = _run("addhalf", build_addhalf, in_maps, ["xout"])
    return np.concatenate(xs, axis=0)


def _xpos_tables():
    n = np.arange(S, dtype=np.float64)
    half = np.arange(0, 64, 2, dtype=np.float64)
    inv_freq = 1.0 / 10000.0 ** (half / 64.0)
    fr = n[:, None] * inv_freq[None, :]
    fr = np.concatenate([fr, fr], axis=1)
    base = (half + 0.4 * 64) / (1.4 * 64)
    power = (n - S // 2) / 256.0
    sc = base[None, :] ** power[:, None]
    sc = np.concatenate([sc, sc], axis=1)
    cos, sin = np.cos(fr), np.sin(fr)
    sgn = np.concatenate([-np.ones(32), np.ones(32)])[None]
    Aq = cos * sc
    Bq = sgn * sin * sc
    Ak = cos / sc
    Bk = sgn * sin / sc
    def rep(t):
        return np.ascontiguousarray(np.tile(t, (1, 8)).astype(np.float32))
    return rep(Aq), rep(Bq), rep(Ak), rep(Bk)


_XPOS = None


def _l1_gates():
    g = np.zeros((2, 2, 4, 128, 512), np.float32)
    cc = np.arange(512)[None, :]
    pp = np.arange(128)[:, None]
    for kc in range(2):
        for kt in range(4):
            r = (cc + 512) - (kc * 512 + kt * 128 + pp)
            g[1, kc, kt] = ((r >= 0) & (r <= 512)).astype(np.float32)
    g[0] = g[1]
    g0 = g.copy()
    g0[0, 0] = 0.0
    return g.astype(BF), g0.astype(BF)


def run_l1layer(yT, wqkv_l, qs_l, ks_l, wo_l, lng_l, lnb_l, w1_l, b1_l, w2_l, b2_l):
    global _XPOS
    if _XPOS is None:
        _XPOS = _xpos_tables()
    Aq, Bq, Ak, Bk = _XPOS
    gates_n, gates_0 = _l1_gates()
    wq_t = _tile_w(wqkv_l)
    wo_t = _tile_w(np.asarray(wo_l, np.float32))
    qs_rep = np.ascontiguousarray(np.tile(np.asarray(qs_l, np.float32), 8)[None])
    ks_rep = np.ascontiguousarray(np.tile(np.asarray(ks_l, np.float32), 8)[None])
    KE = TPC + 512
    in_maps = []
    for c in range(NCORES):
        t0 = c * TPC
        bs = (t0 // S) * S
        ycore = np.zeros((D, KE + 1), np.float32)
        lo = max(t0 - 513, bs)
        ycore[:, lo - (t0 - 513):] = yT[:, lo:t0 + TPC]
        qpos = (np.arange(t0, t0 + TPC) % S)
        kpos = (np.arange(t0 - 512, t0 + TPC) % S)
        tabq = np.stack([Aq[qpos], Bq[qpos]])
        tabk = np.stack([Ak[kpos], Bk[kpos]])
        in_maps.append(dict(
            yT=ycore, wqkv=wq_t, wo=wo_t, qs=qs_rep, ks=ks_rep,
            tabq=tabq, tabk=tabk,
            gates=(gates_0 if t0 == bs else gates_n),
            ident=_IDENT, ones=_ONES))
    (xms,) = _run("l1attn", build_l1attn, in_maps, ["xmT"])
    xmT = np.concatenate(xms, axis=1)

    HID = 2730
    w1 = np.asarray(w1_l, np.float32)
    w1a = np.zeros((D, 2816), np.float32)
    w1g = np.zeros((D, 2816), np.float32)
    w1a[:, :HID] = w1[:, :HID]
    w1g[:, :HID] = w1[:, HID:]
    b1 = np.asarray(b1_l, np.float32)
    b1a = np.zeros(2816, np.float32)
    b1g = np.zeros(2816, np.float32)
    b1a[:HID] = b1[:HID]
    b1g[:HID] = b1[HID:]
    w2p = np.zeros((2816, D), np.float32)
    w2p[:HID] = np.asarray(w2_l, np.float32)
    w1a_t = np.ascontiguousarray(
        w1a.reshape(8, 128, 22, 128).transpose(2, 1, 0, 3))
    w1g_t = np.ascontiguousarray(
        w1g.reshape(8, 128, 22, 128).transpose(2, 1, 0, 3))
    w2_t = np.ascontiguousarray(
        w2p.reshape(22, 128, 8, 128).transpose(2, 1, 0, 3))
    lng_r = np.ascontiguousarray(np.asarray(lng_l, np.float32).reshape(8, 128, 1))
    lnb_r = np.ascontiguousarray(np.asarray(lnb_l, np.float32).reshape(8, 128, 1))
    b1a_r = np.ascontiguousarray(b1a.reshape(22, 128, 1))
    b1g_r = np.ascontiguousarray(b1g.reshape(22, 128, 1))
    b2_r = np.ascontiguousarray(np.asarray(b2_l, np.float32).reshape(8, 128, 1))
    in_maps = []
    for c in range(NCORES):
        t0 = c * TPC
        bs = (t0 // S) * S
        xcore = np.zeros((D, TPC + 1), np.float32)
        lo = max(t0 - 1, bs)
        xcore[:, lo - (t0 - 1):] = xmT[:, lo:t0 + TPC]
        in_maps.append(dict(xm=xcore, lng=lng_r, lnb=lnb_r, w1a=w1a_t, w1g=w1g_t,
                            b1a=b1a_r, b1g=b1g_r, w2=w2_t, b2=b2_r, ones=_ONES))
    (ys,) = _run("l1ffn", build_l1ffn, in_maps, ["yout"])
    return np.concatenate(ys, axis=1)


def run_l1ffn_sub(xfull, w):
    yT = run_lnT(xfull)
    for l in range(2):
        yT = run_l1layer(yT, w["wqkv"][l], w["q_scale"][l], w["k_scale"][l],
                         w["wo"][l], w["ln_g"][l], w["ln_b"][l], w["w1"][l],
                         w["b1"][l], w["w2"][l], w["b2"][l])
    return run_addhalf(xfull, yT)


def run_l1attn_sub(xfull, Wqkv, bqkv, gq, bq, gk, bk, Wo, bo):
    wtile = _tile_w(np.asarray(Wqkv, np.float32))
    gb = np.concatenate([_f32(gq).ravel(), _f32(bq).ravel(), _f32(gk).ravel(),
                         _f32(bk).ravel()])[None]
    in_maps = []
    for c in range(NCORES):
        in_maps.append(dict(x=_f32(xfull[c * TPC:(c + 1) * TPC]), wqkv=wtile,
                            bqkv=_f32(np.asarray(bqkv).reshape(1, 3072)),
                            gb=gb, ident=_IDENT, ones=_ONES))
    kq, kk, kkt, vt = _run("featmap", build_featmap, in_maps,
                           ["kqt_o", "kkt_o", "kk_o", "v_o"])
    kqT = np.concatenate(kq, axis=2).reshape(1024, T)
    kkT = np.concatenate(kk, axis=2).reshape(1024, T)
    kk_tok = np.concatenate(kkt, axis=0)
    v_tok = np.concatenate(vt, axis=0)
    masks = np.zeros((4, 128, 512), np.float32)
    ccc = np.arange(512)[None, :]
    ppp = np.arange(128)[:, None]
    for kt in range(4):
        masks[kt] = (kt * 128 + ppp <= ccc).astype(np.float32)
    masks = masks.astype(BF)
    in_maps = []
    for c in range(NCORES):
        h0 = 2 * c
        in_maps.append(dict(
            kqt=np.ascontiguousarray(kqT[h0 * 64:(h0 + 2) * 64]).reshape(2, 64, T),
            kkt=np.ascontiguousarray(kkT[h0 * 64:(h0 + 2) * 64]).reshape(2, 64, T),
            kk_tok=np.ascontiguousarray(kk_tok[:, c * 128:(c + 1) * 128]),
            v_tok=np.ascontiguousarray(v_tok[:, c * 128:(c + 1) * 128]),
            masks=masks, ones=_ONES))
    (ots,) = _run("linattn", build_linattn, in_maps, ["ot"])
    OT = np.concatenate(ots, axis=0).reshape(1024, T)
    return run_outproj(OT, np.asarray(Wo, np.float32),
                       np.asarray(bo, np.float32).reshape(1, D), xfull)


def kernel(L0_attn_x, L0_attn_qkv_w, L0_attn_out_w, L0_attn_bias_p, L0_attn_bias_a,
           L0_attn_rope_freqs, L0_ffn_Wr, L0_ffn_Wg, L0_ffn_Wu, L0_ffn_Wd,
           L0_ffn_Sg, L0_ffn_Su, L0_ffn_Sd, L1_attn_Wqkv, L1_attn_bqkv,
           L1_attn_gamma_q, L1_attn_beta_q, L1_attn_gamma_k, L1_attn_beta_k,
           L1_attn_Wo, L1_attn_bo, L1_ffn_wqkv, L1_ffn_q_scale, L1_ffn_k_scale,
           L1_ffn_wo, L1_ffn_ln_g, L1_ffn_ln_b, L1_ffn_w1, L1_ffn_b1, L1_ffn_w2,
           L1_ffn_b2):
    x = np.asarray(L0_attn_x, np.float32).reshape(T, D)
    moe = (L0_ffn_Wr, L0_ffn_Wg, L0_ffn_Wu, L0_ffn_Wd, L0_ffn_Sg, L0_ffn_Su,
           L0_ffn_Sd)
    x = run_moe(x, *moe)
    x = run_l0attn(x, L0_attn_qkv_w, L0_attn_out_w, L0_attn_bias_p, L0_attn_bias_a,
                   L0_attn_rope_freqs)
    x = run_moe(x, *moe)
    l1w = dict(wqkv=L1_ffn_wqkv, q_scale=L1_ffn_q_scale, k_scale=L1_ffn_k_scale,
               wo=L1_ffn_wo, ln_g=L1_ffn_ln_g, ln_b=L1_ffn_ln_b, w1=L1_ffn_w1,
               b1=L1_ffn_b1, w2=L1_ffn_w2, b2=L1_ffn_b2)
    x = run_l1ffn_sub(x, l1w)
    x = run_l1attn_sub(x, L1_attn_Wqkv, L1_attn_bqkv, L1_attn_gamma_q,
                       L1_attn_beta_q, L1_attn_gamma_k, L1_attn_beta_k, L1_attn_Wo,
                       L1_attn_bo)
    x = run_l1ffn_sub(x, l1w)
    return x.reshape(B, S, D).astype(np.float32)


# version 04
# speedup vs baseline: 1.0676x; 1.0676x over previous
import contextlib
import numpy as np
import ml_dtypes

import concourse.bass as bass
import concourse.mybir as mybir
from concourse.bass_utils import run_bass_kernel_spmd

F32 = mybir.dt.float32
BF16 = mybir.dt.bfloat16
I32 = mybir.dt.int32
AF = mybir.ActivationFunctionType
ALU = mybir.AluOpType
AX = mybir.AxisListType

NCORES = 8
B = 2
S = 8192
T = B * S
D = 1024
TPC = T // NCORES
BF = ml_dtypes.bfloat16

ENGINES = ("tensor", "vector", "scalar", "gpsimd", "sync")
STRICT = True


class _Op:
    __slots__ = ("engine", "fn", "reads", "writes", "is_dma", "idx", "deps", "queue")

    def __init__(self, engine, fn, reads, writes, is_dma, queue=None):
        self.engine = engine
        self.fn = fn
        self.reads = reads
        self.writes = writes
        self.is_dma = is_dma
        self.queue = queue if queue is not None else engine
        self.idx = None
        self.deps = None


class Pgm:
    def __init__(self, nc, strict=False):
        self.nc = nc
        self.ops = []
        self.strict = strict
        self.last_writer = {}
        self.readers = {}
        self.prefix = ""
        self.barrier_op = None
        self.last_by_stream = {}

    def _k(self, keys):
        return tuple(k if k.startswith("@") else self.prefix + k for k in keys)

    def op(self, engine, fn, reads=(), writes=()):
        self._add(_Op(engine, fn, self._k(reads), self._k(writes), False))

    def dma(self, engine, out_ap, in_ap, reads=(), writes=(), **kw):
        def fn(eng, out_ap=out_ap, in_ap=in_ap, kw=kw):
            return eng.dma_start(out=out_ap, in_=in_ap, **kw)
        self._add(_Op(engine, fn, self._k(reads), self._k(writes), True))

    def coll(self, fn, reads=(), writes=()):
        self._add(_Op("gpsimd", fn, self._k(reads), self._k(writes), True,
                      queue="coll"))

    def barrier(self):
        bop = _Op("vector", lambda v: v.nop(), (), (), False)
        bop.deps = [(o, "b") for o in self.last_by_stream.values()]
        self.last_writer = {}
        self.readers = {}
        self.ops.append(bop)
        self.last_by_stream[("c", "vector")] = bop
        self.barrier_op = bop

    def _add(self, op):
        deps = []
        if self.barrier_op is not None:
            deps.append((self.barrier_op, "b"))
        for r in op.reads:
            w = self.last_writer.get(r)
            if w is not None:
                deps.append((w, "raw"))
        for w_ in op.writes:
            w = self.last_writer.get(w_)
            if w is not None:
                deps.append((w, "waw"))
            for r in self.readers.get(w_, ()):
                deps.append((r, "war"))
        op.deps = deps
        for r in op.reads:
            self.readers.setdefault(r, []).append(op)
        for w_ in op.writes:
            self.last_writer[w_] = op
            self.readers[w_] = []
        self.ops.append(op)
        self.last_by_stream[("d" if op.is_dma else "c", op.queue)] = op

    def emit(self):
        nc = self.nc
        QUEUES = ENGINES + ("coll",)
        counts = {e: 0 for e in ENGINES}
        dcounts = {q: 0 for q in QUEUES}
        for op in self.ops:
            if op.is_dma:
                op.idx = dcounts[op.queue]
                dcounts[op.queue] += 1
            else:
                op.idx = counts[op.engine]
                counts[op.engine] += 1
        stack = contextlib.ExitStack()
        csem = {}
        dsem = {}
        for e in ENGINES:
            if counts[e]:
                csem[e] = stack.enter_context(nc.semaphore(f"c_{e}"))
        for q in QUEUES:
            if dcounts[q]:
                dsem[q] = stack.enter_context(nc.semaphore(f"d_{q}"))
        ops_by_engine = {e: [op for op in self.ops if op.engine == e] for e in ENGINES}
        tail_eng = "sync" if ops_by_engine["sync"] else (
            "gpsimd" if ops_by_engine["gpsimd"] else "vector")
        with stack:
            with nc.Block() as block:
                for e in ENGINES:
                    if not ops_by_engine[e] and e != tail_eng:
                        continue

                    def section(eng, e=e):
                        hwm_c = {x: 0 for x in ENGINES}
                        hwm_d = {x: 0 for x in QUEUES}
                        for op in ops_by_engine[e]:
                            need_c = {}
                            need_d = {}
                            for dep, kind in op.deps:
                                if dep.is_dma:
                                    v = 16 * (dep.idx + 1)
                                    if v > need_d.get(dep.queue, 0):
                                        need_d[dep.queue] = v
                                else:
                                    if dep.engine == e and not op.is_dma and (
                                            not self.strict):
                                        continue
                                    v = dep.idx + 1
                                    if v > need_c.get(dep.engine, 0):
                                        need_c[dep.engine] = v
                            for x, v in need_c.items():
                                if v > hwm_c[x]:
                                    eng.wait_ge(csem[x], v)
                                    hwm_c[x] = v
                            for x, v in need_d.items():
                                if v > hwm_d[x]:
                                    eng.wait_ge(dsem[x], v)
                                    hwm_d[x] = v
                            inst = op.fn(eng)
                            if op.is_dma:
                                inst.then_inc(dsem[op.queue], 16)
                            else:
                                inst.then_inc(csem[e], 1)
                        if e == tail_eng:
                            for x in QUEUES:
                                if dcounts[x]:
                                    eng.wait_ge(dsem[x], 16 * dcounts[x])
                            for x in ENGINES:
                                if counts[x] and x != e:
                                    eng.wait_ge(csem[x], counts[x])

                    getattr(block, e)(section)
        return nc


def _bf(x):
    return np.ascontiguousarray(x).astype(BF)


def _f32(x):
    return np.ascontiguousarray(np.asarray(x, dtype=np.float32))


class Ctx:

    def __init__(self, name):
        self.nc = bass.Bass("TRN2", target_bir_lowering=False)
        self.p = Pgm(self.nc, strict=STRICT)
        self.gst = contextlib.ExitStack()
        self.st = self.gst
        self._uid = 0
        self._phase = 0
        self._csts = {}

    @contextlib.contextmanager
    def phase(self):
        self._phase += 1
        old_st, old_pre = self.st, self.p.prefix
        self.st = contextlib.ExitStack()
        self.p.prefix = f"p{self._phase}:"
        try:
            yield
        finally:
            self.st.close()
            self.st, self.p.prefix = old_st, old_pre
            self.p.barrier()

    def cst(self, val):
        key = float(val)
        if key not in self._csts:
            t = self.gst.enter_context(self.nc.sbuf_tensor(
                f"cst{len(self._csts)}", [128, 1], F32))
            k = f"@cst_{len(self._csts)}"
            self.p.op("vector", lambda v, t=t, val=key: v.memset(t[:, :], val),
                      writes=[k])
            self._csts[key] = (t, k)
        return self._csts[key]

    def sb(self, name, shape, dt=F32):
        return self.st.enter_context(self.nc.sbuf_tensor(
            f"ph{self._phase}_{name}", list(shape), dt))

    def ps(self, name, shape, dt=F32):
        return self.st.enter_context(self.nc.psum_tensor(
            f"ph{self._phase}_{name}", list(shape), dt))

    def inp(self, name, shape, dt=F32):
        return self.nc.dram_tensor(name, list(shape), dt, kind="ExternalInput").ap()

    def out(self, name, shape, dt=F32):
        return self.nc.dram_tensor(name, list(shape), dt, kind="ExternalOutput").ap()

    def uid(self):
        self._uid += 1
        return self._uid

    def finish(self):
        self.p.emit()
        self.gst.close()
        return self.nc


def ln_tile(ctx, x_ap, xn_ap, scratch, eps=1e-5, n=1024):
    p = ctx.p
    s1, negm, mm, var, rstd, sq = scratch["tiles"]
    ks1, knegm, kmm, kvar, krstd, ksq = scratch["keys"]
    kin, kout = scratch["kin"], scratch["kout"]
    p.op("vector", lambda v: v.reduce_sum(s1, x_ap, axis=AX.X), reads=[kin], writes=[ks1])
    p.op("vector", lambda v: v.tensor_scalar(negm, s1, -1.0 / n, None, op0=ALU.mult),
         reads=[ks1], writes=[knegm])
    p.op("scalar", lambda s: s.activation(sq, x_ap, AF.Square), reads=[kin], writes=[ksq])
    p.op("vector", lambda v: v.reduce_sum(s1, sq, axis=AX.X), reads=[ksq], writes=[ks1])
    p.op("vector", lambda v: v.tensor_tensor(mm, negm, negm, op=ALU.mult),
         reads=[knegm], writes=[kmm])
    p.op("vector", lambda v: v.scalar_tensor_tensor(var, s1, 1.0 / n, mm,
                                                    op0=ALU.mult, op1=ALU.subtract),
         reads=[ks1, kmm], writes=[kvar])
    ecol, ekey = ctx.cst(eps)
    p.op("scalar", lambda s: s.activation(var, var, AF.Ln, bias=ecol[:, :]),
         reads=[kvar, ekey], writes=[kvar])
    p.op("scalar", lambda s: s.activation(rstd, var, AF.Exp, scale=-0.5),
         reads=[kvar], writes=[krstd])
    p.op("vector", lambda v: v.tensor_scalar(xn_ap, x_ap, negm, rstd,
                                             op0=ALU.add, op1=ALU.mult),
         reads=[kin, knegm, krstd], writes=[kout])


MOE_NE = 16
MOE_SHARED = True


def build_moe(fio=None):
    if fio is None:
        ctx = Ctx("moe")
        fio = dict(ctx=ctx, standalone=True)
        fio["x"] = ctx.inp("x", [TPC, D])
        fio["wr"] = ctx.inp("wr", [D, 16])
        fio["wg"] = ctx.inp("wg", [16, D, 512])
        fio["wu"] = ctx.inp("wu", [16, D, 512])
        fio["wd"] = ctx.inp("wd", [16, 512, D])
        fio["sg"] = ctx.inp("sg", [D, D])
        fio["su"] = ctx.inp("su", [D, D])
        fio["sd"] = ctx.inp("sd", [D, D])
        fio["ident"] = ctx.inp("ident", [128, 128])
        fio["xout"] = ctx.out("xout", [TPC, D])
    ctx = fio["ctx"]
    nc, p = ctx.nc, ctx.p
    x, wr, wg, wu, wd = fio["x"], fio["wr"], fio["wg"], fio["wu"], fio["wd"]
    sg, su, sd, ident, xout = fio["sg"], fio["su"], fio["sd"], fio["ident"], fio["xout"]

    NT = TPC // 128
    NTC = TPC // 512

    xnT = ctx.sb("xnT", [128, 8 * TPC], BF16)
    xn32t = ctx.sb("xn32t", [128, 1024], F32)
    acc = ctx.sb("acc", [128, NT * D], F32)
    cw = ctx.sb("cw", [128, NT * 16], F32)
    x_sb = ctx.sb("x_sb", [128, D], F32)
    xn_sb = ctx.sb("xn_sb", [128, D], F32)
    iden32 = ctx.sb("iden32", [128, 128], F32)
    lnt = [ctx.sb(f"lnt{i}", [128, 1], F32) for i in range(5)]
    lnsq = ctx.sb("lnsq", [128, D], F32)
    wr_sb = ctx.sb("wr_sb", [128, 8 * 16], F32)
    rprob = ctx.sb("rprob", [128, 16], F32)
    rtmp = ctx.sb("rtmp", [128, 16], F32)
    rm = [ctx.sb(f"rm{i}", [128, 1], F32) for i in range(3)]
    big_g = ctx.sb("big_g", [128, 8 * D], BF16)
    big_u = ctx.sb("big_u", [128, 8 * D], BF16)
    big_d = ctx.sb("big_d", [128, 8 * D], BF16)
    sh = dict(g=big_g, u=big_u, d=big_d)
    wgt = [dict(g=big_g[:, i * 4096:(i + 1) * 4096],
                u=big_u[:, i * 4096:(i + 1) * 4096],
                d=big_d[:, i * 4096:(i + 1) * 4096]) for i in range(2)]
    he = ctx.sb("he", [128, 8 * 512], BF16)
    silu_sb = ctx.sb("silu_sb", [128, 1024], BF16)
    pg = [ctx.ps("pg0", [128, 512]), ctx.ps("pg1", [128, 512])]
    pu = [ctx.ps("pu0", [128, 512]), ctx.ps("pu1", [128, 512])]
    pwd2 = [ctx.ps("pwdA", [128, 512]), ctx.ps("pwdB", [128, 512])]
    ptrT = ctx.ps("ptrT", [128, 128])
    proutT = ctx.ps("proutT", [128, 16])
    ptr = ptrT[:, :]
    prout = proutT[:, :]

    p.dma("sync", iden32[:, :], ident, writes=["iden32"])
    p.dma("sync", wr_sb[:, :].rearrange("p (a b) -> p a b", a=8),
          wr.rearrange("(a p) b -> p a b", p=128), writes=["wr_sb"])

    for ti in range(NT):
        kx = "x_sb"
        kxn = "xn_sb"
        p.dma("sync", x_sb[:, :], x[ti * 128:(ti + 1) * 128, :], writes=[kx])
        scratch = dict(tiles=(lnt[0][:, :], lnt[1][:, :], lnt[2][:, :], lnt[3][:, :],
                              lnt[4][:, :], lnsq[:, :]),
                       keys=("ls1", "lnegm", "lmm", "lvar", "lrstd", "lsq"),
                       kin=kx, kout=kxn)
        ln_tile(ctx, x_sb[:, :], xn_sb[:, :], scratch)
        for fi in range(8):
            p.op("tensor", lambda t, ti=ti, fi=fi: t.transpose(
                ptr, xn_sb[:, fi * 128:(fi + 1) * 128], iden32[:, :]),
                reads=[kxn, "iden32"], writes=["ptr"])
            p.op("vector", lambda v, fi=fi: v.tensor_copy(
                xn32t[:, fi * 128:(fi + 1) * 128], ptr[:, :]),
                reads=["ptr"], writes=[f"xn32_{fi}"])
        p.op("vector", lambda v, ti=ti: v.tensor_copy(
            xnT[:, :].rearrange("p (a b) -> p a b", a=8)[:, :, ti * 128:(ti + 1) * 128],
            xn32t[:, :].rearrange("p (a b) -> p a b", a=8)),
            reads=[f"xn32_{fi}" for fi in range(8)],
            writes=[f"xnT_{fi}_{ti}" for fi in range(8)])
        for fi in range(8):
            p.op("tensor", lambda t, ti=ti, fi=fi: t.matmul(
                prout, xn32t[:, fi * 128:(fi + 1) * 128],
                wr_sb[:, fi * 16:(fi + 1) * 16], start=(fi == 0), stop=(fi == 7)),
                reads=[f"xn32_{fi}", "wr_sb"], writes=["prout"])
        p.op("vector", lambda v: v.reduce_max(rm[0][:, :], prout, axis=AX.X,
                                              negate=True),
             reads=["prout"], writes=["rm0"])
        p.op("scalar", lambda s: s.activation(rprob[:, :], prout, AF.Exp,
                                              bias=rm[0][:, :]),
             reads=["prout", "rm0"], writes=["rprob"])
        p.op("vector", lambda v: v.reduce_sum(rm[1][:, :], rprob[:, :], axis=AX.X),
             reads=["rprob"], writes=["rm1"])
        p.op("vector", lambda v: v.reciprocal(rm[1][:, :], rm[1][:, :]),
             reads=["rm1"], writes=["rm1"])
        p.op("vector", lambda v: v.tensor_scalar(rprob[:, :], rprob[:, :], rm[1][:, :],
                                                 None, op0=ALU.mult),
             reads=["rprob", "rm1"], writes=["rprob"])
        p.op("vector", lambda v: v.reduce_max(rm[0][:, :], rprob[:, :], axis=AX.X),
             reads=["rprob"], writes=["rm0"])
        p.op("vector", lambda v: v.tensor_scalar(rtmp[:, :], rprob[:, :], rm[0][:, :],
                                                 None, op0=ALU.is_ge),
             reads=["rprob", "rm0"], writes=["rtmp"])
        p.op("vector", lambda v: v.scalar_tensor_tensor(rtmp[:, :], rtmp[:, :], -1e9,
                                                        rprob[:, :], op0=ALU.mult,
                                                        op1=ALU.add),
             reads=["rtmp", "rprob"], writes=["rtmp"])
        p.op("vector", lambda v: v.reduce_max(rm[2][:, :], rtmp[:, :], axis=AX.X),
             reads=["rtmp"], writes=["rm2"])
        p.op("vector", lambda v: v.tensor_scalar(rtmp[:, :], rprob[:, :], rm[2][:, :],
                                                 None, op0=ALU.is_ge),
             reads=["rprob", "rm2"], writes=["rtmp"])
        p.op("vector", lambda v, ti=ti: v.tensor_tensor(
            cw[:, ti * 16:(ti + 1) * 16], rprob[:, :], rtmp[:, :], op=ALU.mult),
            reads=["rprob", "rtmp"], writes=[f"cw{ti}"])

    for ti in range(NT):
        p.op("vector", lambda v, ti=ti: v.memset(acc[:, ti * D:(ti + 1) * D], 0.0),
             writes=[f"acc{ti}"])

    def load_expert(e, buf, kpre):
        ew = [kpre + "g", kpre + "u", kpre + "d"]
        xw = ["shg", "shu", "shd"] if e in (0, 1) else []
        if e < 16:
            p.dma("gpsimd", buf["g"].rearrange("p (a b) -> p a b", a=8),
                  wg[e].rearrange("(a p) b -> p a b", p=128), writes=[ew[0]] + xw)
            p.dma("gpsimd", buf["u"].rearrange("p (a b) -> p a b", a=8),
                  wu[e].rearrange("(a p) b -> p a b", p=128), writes=[ew[1]] + xw)
            p.dma("gpsimd", buf["d"].rearrange("p (a b) -> p a b", a=4),
                  wd[e].rearrange("(a p) b -> p a b", p=128), writes=[ew[2]] + xw)
        else:
            p.dma("gpsimd", buf["g"][:, :].rearrange("p (a b) -> p a b", a=8),
                  sg.rearrange("(a p) b -> p a b", p=128), writes=[ew[0]])
            p.dma("gpsimd", buf["u"][:, :].rearrange("p (a b) -> p a b", a=8),
                  su.rearrange("(a p) b -> p a b", p=128), writes=[ew[1]])
            p.dma("gpsimd", buf["d"][:, :].rearrange("p (a b) -> p a b", a=8),
                  sd.rearrange("(a p) b -> p a b", p=128), writes=[ew[2]])

    def run_expert(e, buf, kpre):
        F = 512 if e < 16 else 1024
        nm = F // 128
        for tc in range(NTC):
            tcol = tc * 512
            for mi in range(nm):
                bk = mi % 2
                pgb, pub = pg[bk], pu[bk]
                kg, ku = f"pg{bk}", f"pu{bk}"
                for fi in range(8):
                    p.op("tensor", lambda t, pgb=pgb, mi=mi, fi=fi, tcol=tcol: t.matmul(
                        pgb[:, :], buf["g"][:, fi * F + mi * 128: fi * F + (mi + 1) * 128],
                        xnT[:, fi * TPC + tcol: fi * TPC + tcol + 512],
                        start=(fi == 0), stop=(fi == 7)),
                        reads=[kpre + "g", "xnT_all"], writes=[kg])
                    p.op("tensor", lambda t, pub=pub, mi=mi, fi=fi, tcol=tcol: t.matmul(
                        pub[:, :], buf["u"][:, fi * F + mi * 128: fi * F + (mi + 1) * 128],
                        xnT[:, fi * TPC + tcol: fi * TPC + tcol + 512],
                        start=(fi == 0), stop=(fi == 7)),
                        reads=[kpre + "u", "xnT_all"], writes=[ku])
                p.op("scalar", lambda s, pgb=pgb, bk=bk: s.activation(
                    silu_sb[:, bk * 512:(bk + 1) * 512], pgb[:, :], AF.Silu),
                    reads=[kg], writes=[f"silu{bk}"])
                p.op("vector", lambda v, pub=pub, mi=mi, bk=bk: v.tensor_tensor(
                    he[:, mi * 512:(mi + 1) * 512], silu_sb[:, bk * 512:(bk + 1) * 512],
                    pub[:, :], op=ALU.mult),
                    reads=[f"silu{bk}", ku], writes=[f"he{mi}"])
            for tj in range(4):
                ti = tc * 4 + tj
                for mc in range(2):
                    pw = pwd2[mc]
                    for mi in range(nm):
                        p.op("tensor", lambda t, pw=pw, tj=tj, mc=mc, mi=mi: t.matmul(
                            pw[:, :],
                            he[:, mi * 512 + tj * 128: mi * 512 + (tj + 1) * 128],
                            buf["d"][:, mi * D + mc * 512: mi * D + (mc + 1) * 512],
                            start=(mi == 0), stop=(mi == nm - 1)),
                            reads=[f"he{mi}", kpre + "d"], writes=[f"pwd{mc}"])
                    if e < 16:
                        p.op("vector", lambda v, pw=pw, ti=ti, mc=mc, e=e:
                             v.scalar_tensor_tensor(
                            acc[:, ti * D + mc * 512: ti * D + (mc + 1) * 512],
                            pw[:, :],
                            cw[:, ti * 16 + e: ti * 16 + e + 1],
                            acc[:, ti * D + mc * 512: ti * D + (mc + 1) * 512],
                            op0=ALU.mult, op1=ALU.add),
                            reads=[f"pwd{mc}", f"cw{ti}", f"acc{ti}"], writes=[f"acc{ti}"])
                    else:
                        p.op("vector", lambda v, pw=pw, ti=ti, mc=mc: v.tensor_tensor(
                            acc[:, ti * D + mc * 512: ti * D + (mc + 1) * 512],
                            pw[:, :],
                            acc[:, ti * D + mc * 512: ti * D + (mc + 1) * 512],
                            op=ALU.add),
                            reads=[f"pwd{mc}", f"acc{ti}"], writes=[f"acc{ti}"])

    def _seal(v):
        return v.tensor_copy(lnt[0][:, :], lnt[1][:, :])
    p.op("vector", _seal,
         reads=[f"xnT_{fi}_{ti}" for fi in range(8) for ti in range(NT)],
         writes=["xnT_all", "ls1"])

    if MOE_SHARED:
        load_expert(16, sh, "sh")
        run_expert(16, sh, "sh")
    for e in range(MOE_NE):
        buf = wgt[e % 2]
        kpre = f"e{e % 2}"
        load_expert(e, buf, kpre)
        run_expert(e, buf, kpre)



    for ti in range(NT):
        kx = "x_sb"
        p.dma("sync", x_sb[:, :], x[ti * 128:(ti + 1) * 128, :], writes=[kx])
        p.op("vector", lambda v, ti=ti: v.scalar_tensor_tensor(
            acc[:, ti * D:(ti + 1) * D], acc[:, ti * D:(ti + 1) * D], 0.5,
            x_sb[:, :], op0=ALU.mult, op1=ALU.add),
            reads=[f"acc{ti}", kx], writes=[f"acc{ti}"])
        p.dma("sync", xout[ti * 128:(ti + 1) * 128, :], acc[:, ti * D:(ti + 1) * D],
              reads=[f"acc{ti}"])

    if fio.get("standalone"):
        return ctx.finish()


def build_lnT(fio=None):
    if fio is None:
        ctx = Ctx("lnT")
        fio = dict(ctx=ctx, standalone=True)
        fio["x"] = ctx.inp("x", [TPC, D])
        fio["ident"] = ctx.inp("ident", [128, 128])
        fio["yT"] = ctx.out("yT", [D, TPC])
    ctx = fio["ctx"]
    nc, p = ctx.nc, ctx.p
    x = fio["x"]
    ident = fio["ident"]
    yT = fio["yT"]
    NT = TPC // 128
    NB = 3
    x_sb = [ctx.sb(f"x_sb{i}", [128, D], F32) for i in range(NB)]
    xn_sb = [ctx.sb(f"xn_sb{i}", [128, D], F32) for i in range(NB)]
    iden32 = ctx.sb("iden32", [128, 128], F32)
    ysb = ctx.sb("ysb", [128, 8 * TPC], F32)
    lnt = [ctx.sb(f"lnt{i}", [128, 1], F32) for i in range(5 * NB)]
    lnsq = [ctx.sb(f"lnsq{i}", [128, D], F32) for i in range(NB)]
    ptr = [ctx.ps(f"ptr{i}", [128, 128], F32) for i in range(2)]
    p.dma("sync", iden32[:, :], ident, writes=["iden32"])
    for ti in range(NT):
        bi = ti % NB
        p.dma("sync", x_sb[bi][:, :], x[ti * 128:(ti + 1) * 128, :],
              writes=[f"x_sb{bi}"])
        scratch = dict(tiles=tuple(lnt[5 * bi + i][:, :] for i in range(5))
                       + (lnsq[bi][:, :],),
                       keys=tuple(f"{k}{bi}" for k in
                                  ("ls1", "lnegm", "lmm", "lvar", "lrstd", "lsq")),
                       kin=f"x_sb{bi}", kout=f"xn_sb{bi}")
        ln_tile(ctx, x_sb[bi][:, :], xn_sb[bi][:, :], scratch)
        for fi in range(8):
            pb = (ti * 8 + fi) % 2
            p.op("tensor", lambda t, fi=fi, bi=bi, pb=pb: t.transpose(
                ptr[pb][:, :], xn_sb[bi][:, fi * 128:(fi + 1) * 128], iden32[:, :]),
                reads=[f"xn_sb{bi}", "iden32"], writes=[f"ptr{pb}"])
            p.op("vector", lambda v, fi=fi, ti=ti, pb=pb: v.tensor_copy(
                ysb[:, fi * TPC + ti * 128: fi * TPC + (ti + 1) * 128], ptr[pb][:, :]),
                reads=[f"ptr{pb}"], writes=[f"y_{fi}_{ti}"])
    for fi in range(8):
        p.dma("sync", yT[fi * 128:(fi + 1) * 128, :], ysb[:, fi * TPC:(fi + 1) * TPC],
              reads=[f"y_{fi}_{ti}" for ti in range(NT)])
    if fio.get("e513") is not None:
        e513 = fio["e513"]
        for fi in range(8):
            p.dma("sync", e513[fi * 128:(fi + 1) * 128, :],
                  ysb[:, (fi + 1) * TPC - 513:(fi + 1) * TPC],
                  reads=[f"y_{fi}_{ti}" for ti in range(NT)])
    if fio.get("standalone"):
        return ctx.finish()


def build_addhalf(fio=None):
    if fio is None:
        ctx = Ctx("addhalf")
        fio = dict(ctx=ctx, standalone=True)
        fio["x"] = ctx.inp("x", [TPC, D])
        fio["yT"] = ctx.inp("yT", [D, TPC])
        fio["ident"] = ctx.inp("ident", [128, 128])
        fio["xout"] = ctx.out("xout", [TPC, D])
    ctx = fio["ctx"]
    nc, p = ctx.nc, ctx.p
    x = fio["x"]
    yT = fio["yT"]
    ident = fio["ident"]
    xout = fio["xout"]
    NT = TPC // 128
    iden32 = ctx.sb("iden32", [128, 128], F32)
    x_sb = ctx.sb("x_sb", [128, D], F32)
    y_sb = ctx.sb("y_sb", [128, D], F32)
    o_sb = ctx.sb("o_sb", [128, D], F32)
    ptr = ctx.ps("ptr", [128, 128], F32)
    p.dma("sync", iden32[:, :], ident, writes=["iden32"])
    for ti in range(NT):
        p.dma("sync", x_sb[:, :], x[ti * 128:(ti + 1) * 128, :], writes=["x_sb"])
        p.dma("sync", y_sb[:, :].rearrange("p (a b) -> p a b", a=8),
              yT.rearrange("(a p) t -> p a t", p=128)[:, :, ti * 128:(ti + 1) * 128],
              writes=["y_sb"])
        for fi in range(8):
            p.op("tensor", lambda t, fi=fi: t.transpose(
                ptr[:, :], y_sb[:, fi * 128:(fi + 1) * 128], iden32[:, :]),
                reads=["y_sb", "iden32"], writes=["ptr"])
            p.op("vector", lambda v, fi=fi: v.scalar_tensor_tensor(
                o_sb[:, fi * 128:(fi + 1) * 128], ptr[:, :], 0.5,
                x_sb[:, fi * 128:(fi + 1) * 128], op0=ALU.mult, op1=ALU.add),
                reads=["ptr", "x_sb"], writes=["o_sb"])
        p.dma("sync", xout[ti * 128:(ti + 1) * 128, :], o_sb[:, :], reads=["o_sb"])
    if fio.get("standalone"):
        return ctx.finish()


def build_outproj(fio=None):
    if fio is None:
        ctx = Ctx("outproj")
        fio = dict(ctx=ctx, standalone=True)
        fio["ot"] = ctx.inp("ot", [D, TPC], BF16)
        fio["w"] = ctx.inp("w", [128, 8, D])
        fio["bias"] = ctx.inp("bias", [1, D])
        fio["x"] = ctx.inp("x", [TPC, D])
        fio["ones"] = ctx.inp("ones", [1, 128])
        fio["xout"] = ctx.out("xout", [TPC, D])
    ctx = fio["ctx"]
    nc, p = ctx.nc, ctx.p
    ot = fio["ot"]
    w = fio["w"]
    bias = fio["bias"]
    x = fio["x"]
    ones = fio["ones"]
    xout = fio["xout"]
    NT = TPC // 128
    ot_sb = ctx.sb("ot_sb", [128, 8 * TPC], BF16)
    w_sb = ctx.sb("w_sb", [128, 8 * D], BF16)
    b_sb = ctx.sb("b_sb", [1, D], F32)
    bb_sb = ctx.sb("bb_sb", [128, D], F32)
    ones_sb = ctx.sb("ones_sb", [1, 128], F32)
    x_sb = ctx.sb("x_sb", [128, D], F32)
    o_sb = ctx.sb("o_sb", [128, D], F32)
    pmm = ctx.ps("pmm", [128, D], F32)
    pbc = ctx.ps("pbc", [128, 512], F32)
    p.dma("sync", ot_sb[:, :].rearrange("p (a t) -> p a t", a=8),
          ot.rearrange("(a p) t -> p a t", p=128), writes=["ot_sb"])
    p.dma("gpsimd", w_sb[:, :].rearrange("p (a m) -> p a m", a=8), w, writes=["w_sb"])
    p.dma("sync", b_sb[:, :], bias, writes=["b_sb"])
    p.dma("sync", ones_sb[:, :], ones, writes=["ones_sb"])
    for mc in range(2):
        p.op("tensor", lambda t, mc=mc: t.matmul(
            pbc[:, :], ones_sb[:, :], b_sb[:, mc * 512:(mc + 1) * 512],
            start=True, stop=True),
            reads=["ones_sb", "b_sb"], writes=["pbc"])
        p.op("vector", lambda v, mc=mc: v.tensor_copy(
            bb_sb[:, mc * 512:(mc + 1) * 512], pbc[:, :]),
            reads=["pbc"], writes=["bb_sb"])
    for ti in range(NT):
        p.dma("sync", x_sb[:, :], x[ti * 128:(ti + 1) * 128, :], writes=["x_sb"])
        for mc in range(2):
            for fi in range(8):
                p.op("tensor", lambda t, ti=ti, mc=mc, fi=fi: t.matmul(
                    pmm[:, mc * 512:(mc + 1) * 512],
                    ot_sb[:, fi * TPC + ti * 128: fi * TPC + (ti + 1) * 128],
                    w_sb[:, fi * D + mc * 512: fi * D + (mc + 1) * 512],
                    start=(fi == 0), stop=(fi == 7)),
                    reads=["ot_sb", "w_sb"], writes=[f"pmm{mc}"])
            p.op("vector", lambda v, mc=mc: v.tensor_tensor(
                o_sb[:, mc * 512:(mc + 1) * 512], pmm[:, mc * 512:(mc + 1) * 512],
                bb_sb[:, mc * 512:(mc + 1) * 512], op=ALU.add),
                reads=[f"pmm{mc}", "bb_sb"], writes=["o_sb"])
            p.op("vector", lambda v, mc=mc: v.tensor_tensor(
                o_sb[:, mc * 512:(mc + 1) * 512], o_sb[:, mc * 512:(mc + 1) * 512],
                x_sb[:, mc * 512:(mc + 1) * 512], op=ALU.add),
                reads=["o_sb", "x_sb"], writes=["o_sb"])
        p.dma("sync", xout[ti * 128:(ti + 1) * 128, :], o_sb[:, :], reads=["o_sb"])
    if fio.get("standalone"):
        return ctx.finish()


def build_qkv(fio=None):
    if fio is None:
        ctx = Ctx("qkv")
        fio = dict(ctx=ctx, standalone=True)
        fio["x"] = ctx.inp("x", [TPC, D])
        fio["wqkv"] = ctx.inp("wqkv", [128, 8, 3072])
        fio["freqs"] = ctx.inp("freqs", [1, 32])
        fio["pos"] = ctx.inp("pos", [1, TPC])
        fio["sgnmask"] = ctx.inp("sgnmask", [1, D])
        fio["ident"] = ctx.inp("ident", [128, 128])
        fio["ones"] = ctx.inp("ones", [1, 128])
        fio["qt_o"] = ctx.out("qt_o", [16, 64, TPC], BF16)
        fio["kt_o"] = ctx.out("kt_o", [16, 64, TPC], BF16)
        fio["v_o"] = ctx.out("v_o", [16, TPC, 64], BF16)
    ctx = fio["ctx"]
    nc, p = ctx.nc, ctx.p
    x = fio["x"]
    wqkv = fio["wqkv"]
    freqs = fio["freqs"]
    pos = fio["pos"]
    sgnmask = fio["sgnmask"]
    ident = fio["ident"]
    ones = fio["ones"]
    qt_o = fio["qt_o"]
    kt_o = fio["kt_o"]
    v_o = fio["v_o"]
    NT = TPC // 128

    x_sb = ctx.sb("x_sb", [128, D], F32)
    xn_sb = ctx.sb("xn_sb", [128, D], F32)
    xnt_bf = ctx.sb("xnt_bf", [128, D], BF16)
    xn32t = ctx.sb("xn32t", [128, D], F32)
    iden32 = ctx.sb("iden32", [128, 128], F32)
    idenb = ctx.sb("idenb", [128, 128], BF16)
    ones_sb = ctx.sb("ones_sb", [1, 128], F32)
    w_sb = ctx.sb("w_sb", [128, 8 * 3072], BF16)
    fr_sb = ctx.sb("fr_sb", [1, D], F32)
    pos_sb = ctx.sb("pos_sb", [1, TPC], F32)
    sg_row = ctx.sb("sg_row", [1, D], F32)
    sgn_sb = ctx.sb("sgn_sb", [128, D], F32)
    lnt = [ctx.sb(f"lnt{i}", [128, 1], F32) for i in range(5)]
    lnsq = ctx.sb("lnsq", [128, D], F32)
    q_tok = ctx.sb("q_tok", [128, D], F32)
    k_tok = ctx.sb("k_tok", [128, D], F32)
    v_sb = ctx.sb("v_sb", [128, D], BF16)
    ang_sb = ctx.sb("ang_sb", [128, D], F32)
    crep = ctx.sb("crep", [128, D], F32)
    srep = ctx.sb("srep", [128, D], F32)
    u_sb = ctx.sb("u_sb", [128, D], F32)
    n_i32 = ctx.sb("n_i32", [128, D], I32)
    n_f = ctx.sb("n_f", [128, D], F32)
    rot_sb = ctx.sb("rot_sb", [128, D], F32)
    t1_sb = ctx.sb("t1_sb", [128, D], F32)
    qp_bf = ctx.sb("qp_bf", [128, D], BF16)
    kp_bf = ctx.sb("kp_bf", [128, D], BF16)
    qkrow = ctx.sb("qkrow", [64, 2 * 16 * 128], BF16)
    ptr = ctx.ps("ptr", [128, 128], F32)
    pmm = ctx.ps("pmm", [128, 512], F32)
    pang = ctx.ps("pang", [128, D], F32)
    ptb = ctx.ps("ptb", [64, 128], BF16)

    p.dma("sync", iden32[:, :], ident, writes=["iden32"])
    p.op("vector", lambda v: v.tensor_copy(idenb[:, :], iden32[:, :]),
         reads=["iden32"], writes=["idenb"])
    p.dma("sync", ones_sb[:, :], ones, writes=["ones_sb"])
    p.dma("gpsimd", w_sb[:, :].rearrange("p (a j) -> p a j", a=8), wqkv, writes=["w_sb"])
    p.dma("sync", fr_sb[:, 0:32], freqs, writes=["fr_sb"])
    p.dma("sync", pos_sb[:, :], pos, writes=["pos_sb"])
    p.dma("sync", sg_row[:, :], sgnmask, writes=["sg_row"])
    for k in (32, 64, 128, 256, 512):
        p.op("vector", lambda v, k=k: v.tensor_copy(fr_sb[:, k:2 * k], fr_sb[:, 0:k]),
             reads=["fr_sb"], writes=["fr_sb"])
    for mc in range(2):
        p.op("tensor", lambda t, mc=mc: t.matmul(
            pmm[:, :], ones_sb[:, :], sg_row[:, mc * 512:(mc + 1) * 512],
            start=True, stop=True), reads=["ones_sb", "sg_row"], writes=["pmm"])
        p.op("vector", lambda v, mc=mc: v.tensor_copy(
            sgn_sb[:, mc * 512:(mc + 1) * 512], pmm[:, :]), reads=["pmm"],
            writes=["sgn_sb"])

    PI = float(np.pi)
    for ti in range(NT):
        p.dma("sync", x_sb[:, :], x[ti * 128:(ti + 1) * 128, :], writes=["x_sb"])
        scratch = dict(tiles=(lnt[0][:, :], lnt[1][:, :], lnt[2][:, :], lnt[3][:, :],
                              lnt[4][:, :], lnsq[:, :]),
                       keys=("ls1", "lnegm", "lmm", "lvar", "lrstd", "lsq"),
                       kin="x_sb", kout="xn_sb")
        ln_tile(ctx, x_sb[:, :], xn_sb[:, :], scratch)
        for fi in range(8):
            p.op("tensor", lambda t, fi=fi: t.transpose(
                ptr[:, :], xn_sb[:, fi * 128:(fi + 1) * 128], iden32[:, :]),
                reads=["xn_sb", "iden32"], writes=["ptr"])
            p.op("vector", lambda v, fi=fi: v.tensor_copy(
                xn32t[:, fi * 128:(fi + 1) * 128], ptr[:, :]),
                reads=["ptr"], writes=["xn32t"])
        p.op("vector", lambda v: v.tensor_copy(xnt_bf[:, :], xn32t[:, :]),
             reads=["xn32t"], writes=["xnt_bf"])
        for jc in range(6):
            for fi in range(8):
                p.op("tensor", lambda t, jc=jc, fi=fi: t.matmul(
                    pmm[:, :], xnt_bf[:, fi * 128:(fi + 1) * 128],
                    w_sb[:, fi * 3072 + jc * 512: fi * 3072 + (jc + 1) * 512],
                    start=(fi == 0), stop=(fi == 7)),
                    reads=["xnt_bf", "w_sb"], writes=["pmm"])
            if jc < 2:
                p.op("vector", lambda v, jc=jc: v.tensor_copy(
                    q_tok[:, jc * 512:(jc + 1) * 512], pmm[:, :]),
                    reads=["pmm"], writes=["q_tok"])
            elif jc < 4:
                p.op("vector", lambda v, jc=jc: v.tensor_copy(
                    k_tok[:, (jc - 2) * 512:(jc - 1) * 512], pmm[:, :]),
                    reads=["pmm"], writes=["k_tok"])
            else:
                p.op("vector", lambda v, jc=jc: v.tensor_copy(
                    v_sb[:, (jc - 4) * 512:(jc - 3) * 512], pmm[:, :]),
                    reads=["pmm"], writes=["v_sb"])
        p.dma("sync", v_o.rearrange("h t d -> t h d")[ti * 128:(ti + 1) * 128],
              v_sb[:, :].rearrange("p (h d) -> p h d", h=16), reads=["v_sb"])
        for mc in range(2):
            p.op("tensor", lambda t, mc=mc, ti=ti: t.matmul(
                pang[:, mc * 512:(mc + 1) * 512],
                pos_sb[:, ti * 128:(ti + 1) * 128],
                fr_sb[:, mc * 512:(mc + 1) * 512], start=True, stop=True),
                reads=["pos_sb", "fr_sb"], writes=["pang"])
        p.op("vector", lambda v: v.tensor_copy(ang_sb[:, :], pang[:, :]),
             reads=["pang"], writes=["ang_sb"])
        p.op("vector", lambda v: v.tensor_scalar(u_sb[:, :], ang_sb[:, :],
                                                 1.0 / (2 * PI), None, op0=ALU.mult),
             reads=["ang_sb"], writes=["u_sb"])
        p.op("vector", lambda v: v.tensor_copy(n_i32[:, :], u_sb[:, :]),
             reads=["u_sb"], writes=["n_i32"])
        p.op("vector", lambda v: v.tensor_copy(n_f[:, :], n_i32[:, :]),
             reads=["n_i32"], writes=["n_f"])
        p.op("vector", lambda v: v.scalar_tensor_tensor(srep[:, :], n_f[:, :],
                                                        -2 * PI, ang_sb[:, :],
                                                        op0=ALU.mult, op1=ALU.add),
             reads=["n_f", "ang_sb"], writes=["srep"])
        p.op("scalar", lambda s: s.activation(srep[:, :], srep[:, :], AF.Sin),
             reads=["srep"], writes=["srep"])
        p.op("vector", lambda v: v.tensor_scalar(u_sb[:, :], u_sb[:, :], 0.25, None,
                                                 op0=ALU.add),
             reads=["u_sb"], writes=["u_sb"])
        p.op("vector", lambda v: v.tensor_copy(n_i32[:, :], u_sb[:, :]),
             reads=["u_sb"], writes=["n_i32"])
        p.op("vector", lambda v: v.tensor_copy(n_f[:, :], n_i32[:, :]),
             reads=["n_i32"], writes=["n_f"])
        p.op("vector", lambda v: v.scalar_tensor_tensor(crep[:, :], n_f[:, :],
                                                        -2 * PI, ang_sb[:, :],
                                                        op0=ALU.mult, op1=ALU.add),
             reads=["n_f", "ang_sb"], writes=["crep"])
        hp, hpk = ctx.cst(PI / 2)
        p.op("scalar", lambda s, hp=hp: s.activation(crep[:, :], crep[:, :], AF.Sin,
                                                     bias=hp[:, :]),
             reads=["crep", hpk], writes=["crep"])
        p.op("vector", lambda v: v.tensor_tensor(srep[:, :], srep[:, :], sgn_sb[:, :],
                                                 op=ALU.mult),
             reads=["srep", "sgn_sb"], writes=["srep"])
        for (tok, outb, kt_in, kt_out) in ((q_tok, qp_bf, "q_tok", "qp_bf"),
                                           (k_tok, kp_bf, "k_tok", "kp_bf")):
            p.op("vector", lambda v, tok=tok: v.tensor_copy(
                rot_sb[:, :].rearrange("p (h x j) -> p h x j", h=16, x=2)[:, :, 0, :],
                tok[:, :].rearrange("p (h x j) -> p h x j", h=16, x=2)[:, :, 1, :]),
                reads=[kt_in], writes=["rot_sb"])
            p.op("vector", lambda v, tok=tok: v.tensor_copy(
                rot_sb[:, :].rearrange("p (h x j) -> p h x j", h=16, x=2)[:, :, 1, :],
                tok[:, :].rearrange("p (h x j) -> p h x j", h=16, x=2)[:, :, 0, :]),
                reads=[kt_in], writes=["rot_sb"])
            p.op("vector", lambda v, tok=tok: v.tensor_tensor(
                t1_sb[:, :], tok[:, :], crep[:, :], op=ALU.mult),
                reads=[kt_in, "crep"], writes=["t1_sb"])
            p.op("vector", lambda v: v.tensor_tensor(
                rot_sb[:, :], rot_sb[:, :], srep[:, :], op=ALU.mult),
                reads=["rot_sb", "srep"], writes=["rot_sb"])
            p.op("vector", lambda v, outb=outb: v.tensor_tensor(
                outb[:, :], t1_sb[:, :], rot_sb[:, :], op=ALU.add),
                reads=["t1_sb", "rot_sb"], writes=[kt_out])
        for (src, kin, off) in ((qp_bf, "qp_bf", 0), (kp_bf, "kp_bf", 16 * 128)):
            for h in range(16):
                p.op("tensor", lambda t, src=src, h=h: t.transpose(
                    ptb[:, :], src[:, h * 64:(h + 1) * 64], idenb[:, :]),
                    reads=[kin, "idenb"], writes=["ptb"])
                p.op("vector", lambda v, h=h, off=off: v.tensor_copy(
                    qkrow[:, off + h * 128:off + (h + 1) * 128], ptb[:, :]),
                    reads=["ptb"], writes=["qkrow"])
        p.dma("sync", qt_o.rearrange("h d t -> d h t")[:, :, ti * 128:(ti + 1) * 128],
              qkrow[:, 0:16 * 128].rearrange("d (h t) -> d h t", h=16),
              reads=["qkrow"])
        p.dma("sync", kt_o.rearrange("h d t -> d h t")[:, :, ti * 128:(ti + 1) * 128],
              qkrow[:, 16 * 128:].rearrange("d (h t) -> d h t", h=16),
              reads=["qkrow"])
    if fio.get("standalone"):
        return ctx.finish()


def build_flash(fio=None):
    if fio is None:
        ctx = Ctx("flash")
        fio = dict(ctx=ctx, standalone=True)
        fio["qt"] = ctx.inp("qt", [2, 64, T], BF16)
        fio["kt"] = ctx.inp("kt", [2, 64, T], BF16)
        fio["vv"] = ctx.inp("vv", [2, T, 64], BF16)
        fio["pa"] = ctx.inp("pa", [1, 4])
        fio["dist"] = ctx.inp("dist", [128, 16384], mybir.dt.int16)
        fio["ones"] = ctx.inp("ones", [1, 128])
        fio["ot"] = ctx.out("ot", [2, 64, T], BF16)
    ctx = fio["ctx"]
    nc, p = ctx.nc, ctx.p
    qt = fio["qt"]
    kt = fio["kt"]
    vv = fio["vv"]
    pa = fio["pa"]
    dist = fio["dist"]
    ones = fio["ones"]
    ot = fio["ot"]

    NQC = S // 512
    NKT = S // 128

    dist_sb = ctx.sb("dist_sb", [128, 16384], mybir.dt.int16)
    esc = ctx.sb("esc", [128, 8192], F32)
    e_sb = ctx.sb("e_sb", [128, 16384], BF16)
    qtx = ctx.sb("qtx", [96, S], BF16)
    ktx = ctx.sb("ktx", [96, S], BF16)
    vext = ctx.sb("vext", [128, NKT * 65], BF16)
    ones_sb = ctx.sb("ones_sb", [1, 128], F32)
    ones_col = ctx.sb("ones_col", [128, 1], F32)
    pa_sb = ctx.sb("pa_sb", [1, 4], F32)
    pcol = ctx.sb("pcol", [128, 2], F32)
    sq_sb = ctx.sb("sq_sb", [64, 512], F32)
    nrm = ctx.sb("nrm", [1, 512], F32)
    mk_sb = ctx.sb("mk_sb", [1, 2], F32)
    pexp = [ctx.sb(f"pexp{i}", [128, 1024], BF16) for i in range(3)]
    ptile = [ctx.sb(f"ptile{i}", [128, 1024], BF16) for i in range(3)]
    rec_sb = ctx.sb("rec_sb", [1, 512], F32)
    rb_sb = ctx.sb("rb_sb", [64, 512], F32)
    otc = ctx.sb("otc", [64, 512], BF16)
    pS = [ctx.ps(f"pS{i}", [128, 1024], F32) for i in range(3)]
    pO = ctx.ps("pO", [65, 512], F32)
    pB = ctx.ps("pB", [64, 512], F32)
    pN = pB[0:1, :]

    p.dma("sync", dist_sb[:, :], dist, writes=["dist_sb"])
    p.dma("sync", ones_sb[:, :], ones, writes=["ones_sb"])
    p.dma("sync", pa_sb[:, :], pa, writes=["pa_sb"])
    p.op("vector", lambda v: v.memset(ones_col[:, :], 1.0), writes=["ones_col"])
    p.op("vector", lambda v: v.memset(qtx[64:96, :], 0.0), writes=["qtxpad"])
    p.op("vector", lambda v: v.memset(ktx[64:96, :], 0.0), writes=["ktxpad"])
    p.op("vector", lambda v: v.tensor_scalar(pa_sb[:, :], pa_sb[:, :], 0.01, None,
                                             op0=ALU.max),
         reads=["pa_sb"], writes=["pa_sb"])

    for hi in range(2):
        p.op("tensor", lambda t, hi=hi: t.matmul(
            pS[2][:, 0:2], ones_sb[:, :], pa_sb[0:1, hi * 2:hi * 2 + 2],
            start=True, stop=True),
            reads=["ones_sb", "pa_sb"], writes=["pS2_0"])
        p.op("vector", lambda v: v.tensor_copy(pcol[:, :], pS[2][:, 0:2]),
             reads=["pS2_0"], writes=["pcol"])
        p.op("vector", lambda v: v.tensor_scalar(pcol[:, 0:1], pcol[:, 0:1], -1.0,
                                                 None, op0=ALU.mult),
             reads=["pcol"], writes=["pcol"])
        for hh in range(2):
            o = hh * 8192
            p.op("vector", lambda v, o=o: v.tensor_scalar(
                esc[:, :], dist_sb[:, o:o + 8192], pcol[:, 1:2], 1.0,
                op0=ALU.mult, op1=ALU.add),
                reads=["dist_sb", "pcol"], writes=["esc"])
            p.op("scalar", lambda s: s.activation(esc[:, :], esc[:, :], AF.Ln),
                 reads=["esc"], writes=["esc"])
            p.op("scalar", lambda s, o=o: s.activation(
                e_sb[:, o:o + 8192], esc[:, :], AF.Exp, scale=pcol[:, 0:1]),
                reads=["esc", "pcol"], writes=["e_sb"])
        for b in range(2):
            tcol = b * S
            if fio.get("fused"):
                for rr in range(4):
                    r = 4 * b + rr
                    p.dma("sync", qtx[0:64, rr * TPC:(rr + 1) * TPC],
                          qt[r, hi, :, :], writes=["qtx"])
                    p.dma("sync", ktx[0:64, rr * TPC:(rr + 1) * TPC],
                          kt[r, hi, :, :], writes=["ktx"])
                    p.dma("sync", vext[:, :].rearrange(
                        "p (k e) -> p k e", e=65)[:, rr * (TPC // 128):(rr + 1) * (TPC // 128), 0:64],
                        vv[r, hi, :, :].rearrange("(k p) d -> p k d", p=128),
                        writes=["vext"])
            else:
                p.dma("sync", qtx[0:64, :], qt[hi, :, tcol:tcol + S], writes=["qtx"])
                p.dma("sync", ktx[0:64, :], kt[hi, :, tcol:tcol + S], writes=["ktx"])
                p.dma("sync", vext[:, :].rearrange("p (k e) -> p k e", e=65)[:, :, 0:64],
                      vv[hi, tcol:tcol + S, :].rearrange("(k p) d -> p k d", p=128),
                      writes=["vext"])
            p.op("vector", lambda v: v.memset(ktx[64:65, :], 1.0), writes=["ktx1"])
            p.op("vector", lambda v: v.memset(
                vext[:, :].rearrange("p (k e) -> p k e", e=65)[:, :, 64:65], 1.0),
                writes=["vext1"])
            p.op("vector", lambda v: v.memset(mk_sb[:, :], 0.0), writes=["mk_sb"])
            for qc in range(NQC):
                p.op("scalar", lambda s, qc=qc: s.activation(
                    sq_sb[:, :], ktx[0:64, qc * 512:(qc + 1) * 512], AF.Square),
                    reads=["ktx"], writes=["sq_sb"])
                p.op("tensor", lambda t, qc=qc: t.matmul(
                    pN, ones_col[0:64, :], sq_sb[:, :], start=True, stop=True),
                    reads=["ones_col", "sq_sb"], writes=["pN"])
                p.op("vector", lambda v: v.reduce_max(mk_sb[:, 1:2], pN,
                                                      axis=AX.X),
                     reads=["pN"], writes=["mk2"])
                p.op("vector", lambda v: v.tensor_tensor(
                    mk_sb[:, 0:1], mk_sb[:, 0:1], mk_sb[:, 1:2], op=ALU.max),
                    reads=["mk_sb", "mk2"], writes=["mk_sb"])
            p.op("scalar", lambda s: s.activation(mk_sb[:, 0:1], mk_sb[:, 0:1],
                                                  AF.Sqrt),
                 reads=["mk_sb"], writes=["mk_sb"])
            for qc in range(NQC):
                p.op("scalar", lambda s, qc=qc: s.activation(
                    sq_sb[:, :], qtx[0:64, qc * 512:(qc + 1) * 512], AF.Square),
                    reads=["qtx"], writes=["sq_sb"])
                p.op("tensor", lambda t, qc=qc: t.matmul(
                    pN, ones_col[0:64, :], sq_sb[:, :], start=True, stop=True),
                    reads=["ones_col", "sq_sb"], writes=["pN"])
                p.op("scalar", lambda s: s.activation(nrm[:, :], pN, AF.Sqrt),
                     reads=["pN"], writes=["nrm"])
                p.op("vector", lambda v, qc=qc: v.tensor_scalar(
                    qtx[64:65, qc * 512:(qc + 1) * 512], nrm[:, :], mk_sb[:, 0:1],
                    -1.0, op0=ALU.mult, op1=ALU.mult),
                    reads=["nrm", "mk_sb"], writes=["qtxc"])
            NG = NKT // 2
            for qc in range(NQC):
                qcol = qc * 512
                for gi in range(NG + 2):
                    if gi < NG:
                        r = gi % 3
                        for half in range(2):
                            kt_i = gi * 2 + half
                            p.op("tensor", lambda t, r=r, kt_i=kt_i, half=half,
                                 qcol=qcol: t.matmul(
                                pS[r][:, half * 512:(half + 1) * 512],
                                ktx[:, kt_i * 128:(kt_i + 1) * 128],
                                qtx[:, qcol:qcol + 512], start=True, stop=True),
                                reads=["ktx", "ktx1", "qtx", "qtxc", "qtxpad",
                                       "ktxpad"],
                                writes=[f"pS{r}_{half}"])
                    if gi >= 2:
                        gg = gi - 2
                        r = gg % 3
                        p.op("scalar", lambda s, r=r: s.activation(
                            pexp[r][:, :], pS[r][:, :], AF.Exp, scale=0.125),
                            reads=[f"pS{r}_0", f"pS{r}_1"], writes=[f"pexp{r}"])
                        for half in range(2):
                            kk = gg * 2 + half
                            base = qcol - kk * 128 + 8064
                            p.op("vector", lambda v, r=r, half=half, base=base:
                                 v.tensor_tensor(
                                ptile[r][:, half * 512:(half + 1) * 512],
                                pexp[r][:, half * 512:(half + 1) * 512],
                                e_sb[:, base:base + 512], op=ALU.mult),
                                reads=[f"pexp{r}", "e_sb"], writes=[f"ptile{r}_{half}"])
                            p.op("tensor", lambda t, r=r, kk=kk, half=half: t.matmul(
                                pO[:, :], vext[:, kk * 65:(kk + 1) * 65],
                                ptile[r][:, half * 512:(half + 1) * 512],
                                start=(kk == 0), stop=(kk == NKT - 1)),
                                reads=["vext", "vext1", f"ptile{r}_{half}"],
                                writes=["pO"])
                p.op("vector", lambda v: v.reciprocal(rec_sb[:, :], pO[64:65, :]),
                     reads=["pO"], writes=["rec_sb"])
                p.op("tensor", lambda t: t.matmul(
                    pB[:, :], ones_sb[:, 0:64], rec_sb[:, :], start=True, stop=True),
                    reads=["ones_sb", "rec_sb"], writes=["pB"])
                p.op("vector", lambda v: v.tensor_copy(rb_sb[:, :], pB[:, :]),
                     reads=["pB"], writes=["rb_sb"])
                p.op("vector", lambda v: v.tensor_tensor(
                    otc[:, :], pO[0:64, :], rb_sb[:, :], op=ALU.mult),
                    reads=["pO", "rb_sb"], writes=["otc"])
                if fio.get("fused"):
                    g = tcol + qcol
                    p.dma("sync", ot[g // TPC, hi, :, (g % TPC):(g % TPC) + 512],
                          otc[:, :], reads=["otc"])
                else:
                    p.dma("sync", ot[hi, :, tcol + qcol: tcol + qcol + 512],
                          otc[:, :], reads=["otc"])
    if fio.get("standalone"):
        return ctx.finish()


def build_featmap(fio=None):
    if fio is None:
        ctx = Ctx("featmap")
        fio = dict(ctx=ctx, standalone=True)
        fio["x"] = ctx.inp("x", [TPC, D])
        fio["wqkv"] = ctx.inp("wqkv", [128, 8, 3072])
        fio["bqkv"] = ctx.inp("bqkv", [1, 3072])
        fio["gb"] = ctx.inp("gb", [1, 4 * D])
        fio["ident"] = ctx.inp("ident", [128, 128])
        fio["ones"] = ctx.inp("ones", [1, 128])
        fio["kqt_o"] = ctx.out("kqt_o", [8, 128, TPC], BF16)
        fio["kkt_o"] = ctx.out("kkt_o", [8, 128, TPC], BF16)
        fio["kk_o"] = ctx.out("kk_o", [TPC, D], BF16)
        fio["v_o"] = ctx.out("v_o", [TPC, D], BF16)
    ctx = fio["ctx"]
    nc, p = ctx.nc, ctx.p
    x = fio["x"]
    wqkv = fio["wqkv"]
    bqkv = fio["bqkv"]
    gb = fio["gb"]
    ident = fio["ident"]
    ones = fio["ones"]
    kqt_o = fio["kqt_o"]
    kkt_o = fio["kkt_o"]
    kk_o = fio["kk_o"]
    v_o = fio["v_o"]
    NT = TPC // 128

    x_sb = ctx.sb("x_sb", [128, D], F32)
    xn_sb = ctx.sb("xn_sb", [128, D], F32)
    xn32t = ctx.sb("xn32t", [128, D], F32)
    xnt_bf = ctx.sb("xnt_bf", [128, D], BF16)
    iden32 = ctx.sb("iden32", [128, 128], F32)
    idenb = ctx.sb("idenb", [128, 128], BF16)
    ones_sb = ctx.sb("ones_sb", [1, 128], F32)
    w_sb = ctx.sb("w_sb", [128, 8 * 3072], BF16)
    brow = ctx.sb("brow", [1, 3072], F32)
    bb_sb = ctx.sb("bb_sb", [128, 3072], F32)
    gbrow = ctx.sb("gbrow", [1, 4 * D], F32)
    gbb = ctx.sb("gbb", [128, 4 * D], F32)
    lnt = [ctx.sb(f"lnt{i}", [128, 1], F32) for i in range(5)]
    lnsq = ctx.sb("lnsq", [128, D], F32)
    q_tok = ctx.sb("q_tok", [128, D], F32)
    k_tok = ctx.sb("k_tok", [128, D], F32)
    v_sb = ctx.sb("v_sb", [128, D], BF16)
    f1 = [ctx.sb(f"f1_{i}", [128, 1], F32) for i in range(6)]
    sqb = ctx.sb("sqb", [128, D], F32)
    ker_bf = ctx.sb("ker_bf", [128, D], BF16)
    row_sb = ctx.sb("row_sb", [128, 128], BF16)
    ptr = ctx.ps("ptr", [128, 128], F32)
    ptb = ctx.ps("ptb", [128, 128], BF16)
    pmm = ctx.ps("pmm", [128, 512], F32)

    p.dma("sync", iden32[:, :], ident, writes=["iden32"])
    p.op("vector", lambda v: v.tensor_copy(idenb[:, :], iden32[:, :]),
         reads=["iden32"], writes=["idenb"])
    p.dma("sync", ones_sb[:, :], ones, writes=["ones_sb"])
    p.dma("gpsimd", w_sb[:, :].rearrange("p (a j) -> p a j", a=8), wqkv, writes=["w_sb"])
    p.dma("sync", brow[:, :], bqkv, writes=["brow"])
    p.dma("sync", gbrow[:, :], gb, writes=["gbrow"])
    for mc in range(6):
        p.op("tensor", lambda t, mc=mc: t.matmul(
            pmm[:, :], ones_sb[:, :], brow[:, mc * 512:(mc + 1) * 512],
            start=True, stop=True), reads=["ones_sb", "brow"], writes=["pmm"])
        p.op("vector", lambda v, mc=mc: v.tensor_copy(
            bb_sb[:, mc * 512:(mc + 1) * 512], pmm[:, :]), reads=["pmm"],
            writes=["bb_sb"])
    for gi in range(4):
        for mc in range(2):
            p.op("tensor", lambda t, gi=gi, mc=mc: t.matmul(
                pmm[:, :], ones_sb[:, :],
                gbrow[0:1, gi * D + mc * 512: gi * D + (mc + 1) * 512],
                start=True, stop=True), reads=["ones_sb", "gbrow"], writes=["pmm"])
            p.op("vector", lambda v, gi=gi, mc=mc: v.tensor_copy(
                gbb[:, gi * D + mc * 512: gi * D + (mc + 1) * 512], pmm[:, :]),
                reads=["pmm"], writes=["gbb"])

    def featmap(tok, kin, gi, out_bf, kout):
        p.op("vector", lambda v: v.reduce_sum(f1[0][:, :], tok[:, :], axis=AX.X),
             reads=[kin], writes=["f0"])
        p.op("vector", lambda v: v.tensor_scalar(f1[1][:, :], f1[0][:, :],
                                                 -1.0 / D, None, op0=ALU.mult),
             reads=["f0"], writes=["fnegm"])
        p.op("scalar", lambda s: s.activation(sqb[:, :], tok[:, :], AF.Square),
             reads=[kin], writes=["fsq"])
        p.op("vector", lambda v: v.reduce_sum(f1[2][:, :], sqb[:, :], axis=AX.X),
             reads=["fsq"], writes=["fs2"])
        p.op("vector", lambda v: v.tensor_tensor(f1[3][:, :], f1[1][:, :], f1[0][:, :],
                                                 op=ALU.mult),
             reads=["fnegm", "f0"], writes=["fmm"])
        p.op("vector", lambda v: v.tensor_tensor(f1[3][:, :], f1[2][:, :], f1[3][:, :],
                                                 op=ALU.add),
             reads=["fs2", "fmm"], writes=["fmm"])
        p.op("vector", lambda v: v.tensor_scalar(f1[3][:, :], f1[3][:, :],
                                                 1.0 / (D - 1), None, op0=ALU.mult),
             reads=["fmm"], writes=["fmm"])
        p.op("scalar", lambda s: s.activation(f1[4][:, :], f1[3][:, :], AF.Sqrt),
             reads=["fmm"], writes=["fstd"])
        p.op("vector", lambda v: v.tensor_scalar(f1[4][:, :], f1[4][:, :], 1e-6, None,
                                                 op0=ALU.add),
             reads=["fstd"], writes=["fstd"])
        p.op("vector", lambda v: v.reciprocal(f1[4][:, :], f1[4][:, :]),
             reads=["fstd"], writes=["fstd"])
        p.op("vector", lambda v: v.tensor_scalar(sqb[:, :], tok[:, :], f1[1][:, :],
                                                 f1[4][:, :], op0=ALU.add,
                                                 op1=ALU.mult),
             reads=[kin, "fnegm", "fstd"], writes=["fsq"])
        p.op("vector", lambda v, gi=gi: v.tensor_tensor(
            sqb[:, :], sqb[:, :], gbb[:, (2 * gi) * D:(2 * gi + 1) * D], op=ALU.mult),
            reads=["fsq", "gbb"], writes=["fsq"])
        p.op("vector", lambda v, gi=gi: v.tensor_tensor(
            sqb[:, :], sqb[:, :], gbb[:, (2 * gi + 1) * D:(2 * gi + 2) * D],
            op=ALU.add),
            reads=["fsq", "gbb"], writes=["fsq"])
        p.op("scalar", lambda s: s.activation(out_bf[:, :], sqb[:, :], AF.Square),
             reads=["fsq"], writes=[kout])

    for ti in range(NT):
        p.dma("sync", x_sb[:, :], x[ti * 128:(ti + 1) * 128, :], writes=["x_sb"])
        scratch = dict(tiles=(lnt[0][:, :], lnt[1][:, :], lnt[2][:, :], lnt[3][:, :],
                              lnt[4][:, :], lnsq[:, :]),
                       keys=("ls1", "lnegm", "lmm", "lvar", "lrstd", "lsq"),
                       kin="x_sb", kout="xn_sb")
        ln_tile(ctx, x_sb[:, :], xn_sb[:, :], scratch)
        for fi in range(8):
            p.op("tensor", lambda t, fi=fi: t.transpose(
                ptr[:, :], xn_sb[:, fi * 128:(fi + 1) * 128], iden32[:, :]),
                reads=["xn_sb", "iden32"], writes=["ptr"])
            p.op("vector", lambda v, fi=fi: v.tensor_copy(
                xn32t[:, fi * 128:(fi + 1) * 128], ptr[:, :]),
                reads=["ptr"], writes=["xn32t"])
        p.op("vector", lambda v: v.tensor_copy(xnt_bf[:, :], xn32t[:, :]),
             reads=["xn32t"], writes=["xnt_bf"])
        for jc in range(6):
            for fi in range(8):
                p.op("tensor", lambda t, jc=jc, fi=fi: t.matmul(
                    pmm[:, :], xnt_bf[:, fi * 128:(fi + 1) * 128],
                    w_sb[:, fi * 3072 + jc * 512: fi * 3072 + (jc + 1) * 512],
                    start=(fi == 0), stop=(fi == 7)),
                    reads=["xnt_bf", "w_sb"], writes=["pmm"])
            if jc < 2:
                p.op("vector", lambda v, jc=jc: v.tensor_tensor(
                    q_tok[:, jc * 512:(jc + 1) * 512], pmm[:, :],
                    bb_sb[:, jc * 512:(jc + 1) * 512], op=ALU.add),
                    reads=["pmm", "bb_sb"], writes=["q_tok"])
            elif jc < 4:
                p.op("vector", lambda v, jc=jc: v.tensor_tensor(
                    k_tok[:, (jc - 2) * 512:(jc - 1) * 512], pmm[:, :],
                    bb_sb[:, jc * 512:(jc + 1) * 512], op=ALU.add),
                    reads=["pmm", "bb_sb"], writes=["k_tok"])
            else:
                p.op("vector", lambda v, jc=jc: v.tensor_tensor(
                    v_sb[:, (jc - 4) * 512:(jc - 3) * 512], pmm[:, :],
                    bb_sb[:, jc * 512:(jc + 1) * 512], op=ALU.add),
                    reads=["pmm", "bb_sb"], writes=["v_sb"])
        if fio.get("fused"):
            p.dma("sync",
                  v_o.rearrange("c t d -> t c d")[ti * 128:(ti + 1) * 128],
                  v_sb[:, :].rearrange("p (c d) -> p c d", c=8), reads=["v_sb"])
        else:
            p.dma("sync", v_o[ti * 128:(ti + 1) * 128, :], v_sb[:, :],
                  reads=["v_sb"])
        featmap(q_tok, "q_tok", 0, ker_bf, "ker_bf")
        for fi in range(8):
            p.op("tensor", lambda t, fi=fi: t.transpose(
                ptb[:, :], ker_bf[:, fi * 128:(fi + 1) * 128], idenb[:, :]),
                reads=["ker_bf", "idenb"], writes=["ptb"])
            p.op("vector", lambda v, fi=fi: v.tensor_copy(row_sb[:, :], ptb[:, :]),
                 reads=["ptb"], writes=["row_sb"])
            p.dma("sync", kqt_o[fi, :, ti * 128:(ti + 1) * 128], row_sb[:, :],
                  reads=["row_sb"])
        featmap(k_tok, "k_tok", 1, ker_bf, "ker_bf")
        if fio.get("fused"):
            p.dma("sync",
                  kk_o.rearrange("c t d -> t c d")[ti * 128:(ti + 1) * 128],
                  ker_bf[:, :].rearrange("p (c d) -> p c d", c=8), reads=["ker_bf"])
        else:
            p.dma("sync", kk_o[ti * 128:(ti + 1) * 128, :], ker_bf[:, :],
                  reads=["ker_bf"])
        for fi in range(8):
            p.op("tensor", lambda t, fi=fi: t.transpose(
                ptb[:, :], ker_bf[:, fi * 128:(fi + 1) * 128], idenb[:, :]),
                reads=["ker_bf", "idenb"], writes=["ptb"])
            p.op("vector", lambda v, fi=fi: v.tensor_copy(row_sb[:, :], ptb[:, :]),
                 reads=["ptb"], writes=["row_sb"])
            p.dma("sync", kkt_o[fi, :, ti * 128:(ti + 1) * 128], row_sb[:, :],
                  reads=["row_sb"])
    if fio.get("standalone"):
        return ctx.finish()


def build_linattn(fio=None):
    if fio is None:
        ctx = Ctx("linattn")
        fio = dict(ctx=ctx, standalone=True)
        fio["kqt"] = ctx.inp("kqt", [2, 64, T], BF16)
        fio["kkt"] = ctx.inp("kkt", [2, 64, T], BF16)
        fio["kk_tok"] = ctx.inp("kk_tok", [T, 128], BF16)
        fio["v_tok"] = ctx.inp("v_tok", [T, 128], BF16)
        fio["masks"] = ctx.inp("masks", [4, 128, 512], BF16)
        fio["ones"] = ctx.inp("ones", [1, 128])
        fio["ot"] = ctx.out("ot", [2, 64, T], BF16)
    ctx = fio["ctx"]
    nc, p = ctx.nc, ctx.p
    kqt = fio["kqt"]
    kkt = fio["kkt"]
    kk_tok = fio["kk_tok"]
    v_tok = fio["v_tok"]
    masks = fio["masks"]
    ones = fio["ones"]
    ot = fio["ot"]
    NCC = S // 512

    kq_sb = ctx.sb("kq_sb", [96, S], BF16)
    kk_sb = ctx.sb("kk_sb", [96, S], BF16)
    kkt_tok = ctx.sb("kkt_tok", [128, 64 * 64], BF16)
    vext = ctx.sb("vext", [128, 64 * 65], BF16)
    m_sb = ctx.sb("m_sb", [128, 4 * 512], BF16)
    ones_sb = ctx.sb("ones_sb", [1, 128], F32)
    s_sb = ctx.sb("s_sb", [64, 65], BF16)
    at_sb = [ctx.sb(f"at{i}", [128, 512], BF16) for i in range(2)]
    zrow = ctx.sb("zrow", [1, 512], F32)
    rb_sb = ctx.sb("rb_sb", [64, 512], F32)
    otc = ctx.sb("otc", [64, 512], BF16)
    pA = [ctx.ps(f"pA{i}", [128, 512], F32) for i in range(2)]
    pO = ctx.ps("pO", [65, 512], F32)
    pSt = ctx.ps("pSt", [64, 65], F32)
    pB = ctx.ps("pB", [64, 512], F32)

    p.dma("sync", ones_sb[:, :], ones, writes=["ones_sb"])
    p.op("vector", lambda v: v.memset(kq_sb[64:96, :], 0.0), writes=["kqpad"])
    p.op("vector", lambda v: v.memset(kk_sb[64:96, :], 0.0), writes=["kkpad"])
    p.dma("sync", m_sb[:, :].rearrange("p (a c) -> p a c", a=4), masks,
          writes=["m_sb"])
    for hi in range(2):
        for b in range(2):
            tcol = b * S
            if fio.get("fused"):
                NTP = TPC // 128
                for rr in range(4):
                    r = 4 * b + rr
                    p.dma("sync", kq_sb[0:64, rr * TPC:(rr + 1) * TPC],
                          kqt.rearrange("(c h) d t -> c h d t", h=2)[r, hi],
                          writes=["kq_sb"])
                    p.dma("sync", kk_sb[0:64, rr * TPC:(rr + 1) * TPC],
                          kkt.rearrange("(c h) d t -> c h d t", h=2)[r, hi],
                          writes=["kk_sb"])
                    p.dma("sync",
                          kkt_tok[:, :].rearrange("p (k d) -> p k d", d=64)[:, rr * NTP:(rr + 1) * NTP],
                          kk_tok[r, :, hi * 64:(hi + 1) * 64].rearrange(
                              "(k p) d -> p k d", p=128),
                          writes=["kkt_tok"])
                    p.dma("sync",
                          vext[:, :].rearrange("p (k e) -> p k e", e=65)[:, rr * NTP:(rr + 1) * NTP, 0:64],
                          v_tok[r, :, hi * 64:(hi + 1) * 64].rearrange(
                              "(k p) d -> p k d", p=128),
                          writes=["vext"])
            else:
                p.dma("sync", kq_sb[0:64, :], kqt[hi, :, tcol:tcol + S],
                      writes=["kq_sb"])
                p.dma("sync", kk_sb[0:64, :], kkt[hi, :, tcol:tcol + S],
                      writes=["kk_sb"])
                p.dma("sync", kkt_tok[:, :].rearrange("p (k d) -> p k d", d=64),
                      kk_tok[tcol:tcol + S, hi * 64:(hi + 1) * 64].rearrange(
                          "(k p) d -> p k d", p=128),
                      writes=["kkt_tok"])
                p.dma("sync", vext[:, :].rearrange("p (k e) -> p k e", e=65)[:, :, 0:64],
                      v_tok[tcol:tcol + S, hi * 64:(hi + 1) * 64].rearrange(
                          "(k p) d -> p k d", p=128),
                      writes=["vext"])
            p.op("vector", lambda v: v.memset(
                vext[:, :].rearrange("p (k e) -> p k e", e=65)[:, :, 64:65], 1.0),
                writes=["vext1"])
            for cc in range(NCC):
                ccol = cc * 512
                if cc > 0:
                    p.op("vector", lambda v: v.tensor_copy(s_sb[:, :], pSt[:, :]),
                         reads=["pSt"], writes=["s_sb"])
                for kt_i in range(4):
                    r = kt_i % 2
                    p.op("tensor", lambda t, r=r, kt_i=kt_i, ccol=ccol: t.matmul(
                        pA[r][:, :], kk_sb[:, ccol + kt_i * 128: ccol + (kt_i + 1) * 128],
                        kq_sb[:, ccol:ccol + 512], start=True, stop=True),
                        reads=["kk_sb", "kq_sb", "kqpad", "kkpad"], writes=[f"pA{r}"])
                    p.op("vector", lambda v, r=r, kt_i=kt_i: v.tensor_tensor(
                        at_sb[r][:, :], pA[r][:, :],
                        m_sb[:, kt_i * 512:(kt_i + 1) * 512], op=ALU.mult),
                        reads=[f"pA{r}", "m_sb"], writes=[f"at{r}"])
                    p.op("tensor", lambda t, r=r, kt_i=kt_i, cc=cc: t.matmul(
                        pO[:, :], vext[:, (cc * 4 + kt_i) * 65:(cc * 4 + kt_i + 1) * 65],
                        at_sb[r][:, :], start=(kt_i == 0),
                        stop=(cc == 0 and kt_i == 3)),
                        reads=["vext", "vext1", f"at{r}"], writes=["pO"])
                if cc > 0:
                    p.op("tensor", lambda t, ccol=ccol: t.matmul(
                        pO[:, :], s_sb[:, :], kq_sb[0:64, ccol:ccol + 512],
                        start=False, stop=True),
                        reads=["s_sb", "kq_sb"], writes=["pO"])
                for kt_i in range(4):
                    p.op("tensor", lambda t, cc=cc, kt_i=kt_i: t.matmul(
                        pSt[:, :], kkt_tok[:, (cc * 4 + kt_i) * 64:(cc * 4 + kt_i + 1) * 64],
                        vext[:, (cc * 4 + kt_i) * 65:(cc * 4 + kt_i + 1) * 65],
                        start=(cc == 0 and kt_i == 0), stop=(cc == NCC - 1 and kt_i == 3)),
                        reads=["kkt_tok", "vext", "vext1"], writes=["pSt"])
                p.op("vector", lambda v: v.tensor_scalar(
                    zrow[:, :], pO[64:65, :], 1e-6, None, op0=ALU.add),
                    reads=["pO"], writes=["zrow"])
                p.op("vector", lambda v: v.reciprocal(zrow[:, :], zrow[:, :]),
                     reads=["zrow"], writes=["zrow"])
                p.op("tensor", lambda t: t.matmul(
                    pB[:, :], ones_sb[:, 0:64], zrow[:, :], start=True, stop=True),
                    reads=["ones_sb", "zrow"], writes=["pB"])
                p.op("vector", lambda v: v.tensor_copy(rb_sb[:, :], pB[:, :]),
                     reads=["pB"], writes=["rb_sb"])
                p.op("vector", lambda v: v.tensor_tensor(
                    otc[:, :], pO[0:64, :], rb_sb[:, :], op=ALU.mult),
                    reads=["pO", "rb_sb"], writes=["otc"])
                if fio.get("fused"):
                    g = tcol + ccol
                    p.dma("sync", ot[g // TPC, hi, :, (g % TPC):(g % TPC) + 512],
                          otc[:, :], reads=["otc"])
                else:
                    p.dma("sync", ot[hi, :, tcol + ccol: tcol + ccol + 512],
                          otc[:, :], reads=["otc"])
    if fio.get("standalone"):
        return ctx.finish()


def build_l1attn(fio=None):
    KE = TPC + 512
    if fio is None:
        ctx = Ctx("l1attn")
        fio = dict(ctx=ctx, standalone=True)
        fio["yT"] = ctx.inp("yT", [D, KE + 1])
        fio["wqkv"] = ctx.inp("wqkv", [128, 8, 1536])
        fio["wo"] = ctx.inp("wo", [128, 4, D])
        fio["qs"] = ctx.inp("qs", [1, 512])
        fio["ks"] = ctx.inp("ks", [1, 512])
        fio["tabq"] = ctx.inp("tabq", [2, TPC, 512])
        fio["tabk"] = ctx.inp("tabk", [2, KE, 512])
        fio["gates"] = ctx.inp("gates", [2, 2, 4, 128, 512], BF16)
        fio["ident"] = ctx.inp("ident", [128, 128])
        fio["ones"] = ctx.inp("ones", [1, 128])
        fio["xmT"] = ctx.out("xmT", [D, TPC])
    ctx = fio["ctx"]
    nc, p = ctx.nc, ctx.p
    yT = fio["yT"]
    wqkv = fio["wqkv"]
    wo = fio["wo"]
    qs = fio["qs"]
    ks = fio["ks"]
    tabq = fio["tabq"]
    tabk = fio["tabk"]
    gates = fio["gates"]
    ident = fio["ident"]
    ones = fio["ones"]
    xmT = fio["xmT"]
    NKT = KE // 128
    NQT = TPC // 128

    hT = ctx.sb("hT", [128, 8 * KE], BF16)
    w_sb = ctx.sb("w_sb", [128, 8 * 1536], BF16)
    wo_sb = ctx.sb("wo_sb", [128, 4 * D], BF16)
    qp_tok = ctx.sb("qp_tok", [128, NQT * 512], BF16)
    kp_tok = ctx.sb("kp_tok", [128, NKT * 512], BF16)
    v_tok = ctx.sb("v_tok", [128, NKT * 512], BF16)
    otsb = ctx.sb("otsb", [128, 4 * TPC], BF16)
    qtx2 = [ctx.sb(f"qtx{i}", [96, TPC], BF16) for i in range(2)]
    ktx2 = [ctx.sb(f"ktx{i}", [96, KE], BF16) for i in range(2)]
    vext2 = [ctx.sb("vextA", [128, NKT * 65], BF16)] * 2
    gat = ctx.sb("gat", [128, 2 * 4 * 512], BF16)
    iden32 = ctx.sb("iden32", [128, 128], F32)
    idenb = ctx.sb("idenb", [128, 128], BF16)
    ones_sb = ctx.sb("ones_sb", [1, 128], F32)
    sc_row = ctx.sb("sc_row", [1, 512], F32)
    qsb = ctx.sb("qsb", [128, 512], F32)
    ksb = ctx.sb("ksb", [128, 512], F32)
    stg2 = [ctx.sb(f"stg{i}", [128, 512], F32) for i in range(2)]
    sqf2 = [ctx.sb(f"sqf{i}", [128, 512], F32) for i in range(2)]
    nh2 = [ctx.sb(f"nh{i}", [128, 8], F32) for i in range(2)]
    tA2 = [ctx.sb("tAA", [128, 512], F32)] * 2
    tB2 = [ctx.sb("tBB", [128, 512], F32)] * 2
    rot2 = [ctx.sb(f"rot{i}", [128, 512], F32) for i in range(2)]
    t12 = [ctx.sb(f"t1{i}", [128, 512], F32) for i in range(2)]
    stg, sqf, nh, tA, tB, rot, t1 = (stg2[0], sqf2[0], nh2[0], tA2[0], tB2[0],
                                     rot2[0], t12[0])
    mkd = ctx.sb("mkd", [64, 1], F32)
    mkd_bf = ctx.sb("mkd_bf", [64, 1], BF16)
    qa_bf = ctx.sb("qa_bf", [64, 512], BF16)
    pexp = [ctx.sb(f"pexp{i}", [128, 512], BF16) for i in range(2)]
    ptl = [ctx.sb(f"ptl{i}", [128, 512], BF16) for i in range(2)]
    rec = sc_row
    rb_sb = ctx.sb("rb_sb", [64, 512], BF16)
    pq2 = [ctx.ps(f"pq{i}", [128, 512], F32) for i in range(2)]
    pq = pq2[0]
    pS = [ctx.ps(f"pS{i}", [128, 512], F32) for i in range(2)]
    pO = ctx.ps("pO", [65, 512], F32)
    pT = ctx.ps("pT", [128, 128], BF16)
    pN = ctx.ps("pN", [1, 512], F32)
    pB = ctx.ps("pB", [64, 512], F32)

    p.dma("sync", iden32[:, :], ident, writes=["iden32"])
    p.op("vector", lambda v: v.tensor_copy(idenb[:, :], iden32[:, :]),
         reads=["iden32"], writes=["idenb"])
    p.dma("sync", ones_sb[:, :], ones, writes=["ones_sb"])
    for i in range(2):
        p.op("vector", lambda v, i=i: v.memset(qtx2[i][64:96, :], 0.0),
             writes=[f"qtx{i}pad"])
        p.op("vector", lambda v, i=i: v.memset(ktx2[i][64:96, :], 0.0),
             writes=[f"ktx{i}pad"])
    p.dma("gpsimd", w_sb[:, :].rearrange("p (a j) -> p a j", a=8), wqkv,
          writes=["w_sb"])
    p.dma("gpsimd", wo_sb[:, :].rearrange("p (a m) -> p a m", a=4), wo,
          writes=["wo_sb"])

    for (row, dst, kk) in ((qs, qsb, "qsb"), (ks, ksb, "ksb")):
        p.dma("sync", sc_row[:, :], row, writes=["sc_row"])
        p.op("tensor", lambda t: t.matmul(
            pq[:, :], ones_sb[:, :], sc_row[:, :], start=True, stop=True),
            reads=["ones_sb", "sc_row"], writes=["pq"])
        p.op("vector", lambda v, dst=dst, kk=kk: v.tensor_copy(dst[:, :], pq[:, :]),
             reads=["pq"], writes=[kk])
    if fio.get("fused"):
        Yown = fio["Y"]
        edges = fio["edges"]
        hmask = fio["hmask"]
        halo_sb = ctx.sb("halo_sb", [128, 8 * 513], F32)
        hstage = ctx.sb("hstage", [128, 513], F32)
        hm_sb = ctx.sb("hm_sb", [128, 8], F32)
        p.dma("sync", hm_sb[:, :], hmask, writes=["hm_sb"])
        for fi in range(8):
            p.op("vector", lambda v, fi=fi: v.memset(
                halo_sb[:, fi * 513:(fi + 1) * 513], 0.0), writes=[f"halo{fi}"])
            for r in range(8):
                p.dma("sync", hstage[:, :],
                      edges[r * 1024 + fi * 128: r * 1024 + (fi + 1) * 128, :],
                      writes=["hstage"])
                p.op("vector", lambda v, fi=fi, r=r: v.scalar_tensor_tensor(
                    halo_sb[:, fi * 513:(fi + 1) * 513], hstage[:, :],
                    hm_sb[:, r:r + 1],
                    halo_sb[:, fi * 513:(fi + 1) * 513],
                    op0=ALU.mult, op1=ALU.add),
                    reads=["hstage", "hm_sb", f"halo{fi}"], writes=[f"halo{fi}"])
        for fi in range(8):
            if fi < 4:
                p.op("vector", lambda v, fi=fi: v.tensor_copy(
                    hT[:, fi * KE: fi * KE + 512],
                    halo_sb[:, fi * 513 + 1: fi * 513 + 513]),
                    reads=[f"halo{fi}"], writes=["hT"])
                p.dma("gpsimd", hT[:, fi * KE + 512:(fi + 1) * KE],
                      Yown[fi * 128:(fi + 1) * 128, 0:KE - 512], writes=["hT"])
            else:
                p.op("vector", lambda v, fi=fi: v.tensor_copy(
                    hT[:, fi * KE: fi * KE + 513],
                    halo_sb[:, fi * 513: fi * 513 + 513]),
                    reads=[f"halo{fi}"], writes=["hT"])
                p.dma("gpsimd", hT[:, fi * KE + 513:(fi + 1) * KE],
                      Yown[fi * 128:(fi + 1) * 128, 0:KE - 513], writes=["hT"])
    else:
        for fi in range(8):
            off = 1 if fi < 4 else 0
            p.dma("gpsimd", hT[:, fi * KE:(fi + 1) * KE],
                  yT[fi * 128:(fi + 1) * 128, off:off + KE], writes=["hT"])

    rix = 0
    for kti in range(NKT):
        kcol = kti * 128
        is_q = kti >= 4
        qi = kti - 4
        jcs = ([0] if is_q else []) + [1, 2]
        for jc in jcs:
            r = rix % 2
            rix += 1
            pqr = pq2[r]
            for fi in range(8):
                p.op("tensor", lambda t, pqr=pqr, jc=jc, fi=fi, kcol=kcol: t.matmul(
                    pqr[:, :], hT[:, fi * KE + kcol: fi * KE + kcol + 128],
                    w_sb[:, fi * 1536 + jc * 512: fi * 1536 + (jc + 1) * 512],
                    start=(fi == 0), stop=(fi == 7)),
                    reads=["hT", "w_sb"], writes=[f"pq{r}"])
            if jc == 2:
                p.op("vector", lambda v, pqr=pqr, kti=kti: v.tensor_copy(
                    v_tok[:, kti * 512:(kti + 1) * 512], pqr[:, :]),
                    reads=[f"pq{r}"], writes=["v_tok"])
            elif jc == 0:
                p.op("vector", lambda v, pqr=pqr, r=r: v.tensor_copy(
                    stg2[r][:, :], pqr[:, :]),
                    reads=[f"pq{r}"], writes=[f"stg{r}"])
                _l2rope(ctx, stg2[r], f"stg{r}", qsb, "qsb", tabq, qi * 128, qp_tok,
                        qi * 512, "qp_tok", sqf2[r], nh2[r], tA2[r], tB2[r],
                        rot2[r], t12[r], r)
            else:
                p.op("vector", lambda v, pqr=pqr, r=r: v.tensor_copy(
                    stg2[r][:, :], pqr[:, :]),
                    reads=[f"pq{r}"], writes=[f"stg{r}"])
                _l2rope(ctx, stg2[r], f"stg{r}", ksb, "ksb", tabk, kti * 128, kp_tok,
                        kti * 512, "kp_tok", sqf2[r], nh2[r], tA2[r], tB2[r],
                        rot2[r], t12[r], r)
    for h in range(8):
        hb = h % 2
        qtx, ktx, vext = qtx2[hb], ktx2[hb], vext2[0]
        for ti in range(NQT):
            p.op("tensor", lambda t, ti=ti, h=h: t.transpose(
                pT[0:64, :], qp_tok[:, ti * 512 + h * 64: ti * 512 + (h + 1) * 64],
                idenb[:, :]), reads=["qp_tok", "idenb"], writes=["pT"])
            p.op("vector", lambda v, ti=ti: v.tensor_copy(
                qtx[0:64, ti * 128:(ti + 1) * 128], pT[0:64, :]),
                reads=["pT"], writes=[f"qtx{hb}"])
        for ti in range(NKT):
            p.op("tensor", lambda t, ti=ti, h=h: t.transpose(
                pT[0:64, :], kp_tok[:, ti * 512 + h * 64: ti * 512 + (h + 1) * 64],
                idenb[:, :]), reads=["kp_tok", "idenb"], writes=["pT"])
            p.op("vector", lambda v, ti=ti: v.tensor_copy(
                ktx[0:64, ti * 128:(ti + 1) * 128], pT[0:64, :]),
                reads=["pT"], writes=[f"ktx{hb}"])
            p.op("vector", lambda v, ti=ti, h=h: v.tensor_copy(
                vext[:, :].rearrange("p (k e) -> p k e", e=65)[:, ti, 0:64],
                v_tok[:, ti * 512 + h * 64: ti * 512 + (h + 1) * 64]),
                reads=["v_tok"], writes=["vext"])
        p.op("vector", lambda v: v.memset(ktx[64:65, :], 1.0), writes=[f"ktx1{hb}"])
        p.op("vector", lambda v: v.memset(
            vext[:, :].rearrange("p (k e) -> p k e", e=65)[:, :, 64:65], 1.0),
            writes=["vext1"])
        for qb in range(4):
            p.op("vector", lambda v, qb=qb: v.reduce_max(
                mkd[0:64, :], ktx[0:64, qb * 512: qb * 512 + 1024],
                axis=AX.X, apply_absolute_value=True),
                reads=[f"ktx{hb}"], writes=["mkd"])
            p.op("vector", lambda v: v.tensor_copy(mkd_bf[0:64, :], mkd[0:64, :]),
                 reads=["mkd"], writes=["mkd_bf"])
            p.op("scalar", lambda s, qb=qb: s.activation(
                qa_bf[0:64, :], qtx[0:64, qb * 512:(qb + 1) * 512], AF.Abs),
                reads=[f"qtx{hb}"], writes=["qa_bf"])
            p.op("tensor", lambda t: t.matmul(
                pN[0:1, 0:512], mkd_bf[0:64, :], qa_bf[0:64, :],
                start=True, stop=True),
                reads=["mkd_bf", "qa_bf"], writes=["pN"])
            p.op("vector", lambda v, qb=qb: v.tensor_scalar(
                qtx[64:65, qb * 512:(qb + 1) * 512], pN[0:1, 0:512],
                -1.0, None, op0=ALU.mult),
                reads=["pN"], writes=[f"qtxc{hb}"])
            qcls = 0 if qb == 0 else 1
            if qb <= 1:
                p.dma("sync", gat[:, :].rearrange("p (b c d) -> p b c d", b=2, c=4),
                      gates[qcls].rearrange("b c p d -> p b c d"),
                      writes=["gat0", "gat1"])
            for kt_i in range(8):
                r = kt_i % 2
                kt_g = qb * 4 + kt_i
                p.op("tensor", lambda t, r=r, kt_g=kt_g, qb=qb: t.matmul(
                    pS[r][:, :], ktx[:, kt_g * 128:(kt_g + 1) * 128],
                    qtx[:, qb * 512:(qb + 1) * 512], start=True, stop=True),
                    reads=[f"ktx{hb}", f"ktx1{hb}", f"qtx{hb}", f"qtxc{hb}",
                           f"qtx{hb}pad", f"ktx{hb}pad"], writes=[f"pS{r}"])
                p.op("scalar", lambda s, r=r: s.activation(
                    pexp[r][:, :], pS[r][:, :], AF.Exp, scale=8.0),
                    reads=[f"pS{r}"], writes=[f"pexp{r}"])
                goff = (kt_i // 4) * 4 + (kt_i % 4)
                p.op("vector", lambda v, r=r, goff=goff: v.tensor_tensor(
                    ptl[r][:, :], pexp[r][:, :],
                    gat[:, goff * 512:(goff + 1) * 512], op=ALU.mult),
                    reads=[f"pexp{r}", f"gat{qcls}"], writes=[f"ptl{r}"])
                p.op("tensor", lambda t, r=r, kt_g=kt_g, kt_i=kt_i: t.matmul(
                    pO[:, :], vext[:, kt_g * 65:(kt_g + 1) * 65], ptl[r][:, :],
                    start=(kt_i == 0), stop=(kt_i == 7)),
                    reads=["vext", "vext1", f"ptl{r}"], writes=["pO"])
            p.op("vector", lambda v: v.tensor_scalar(rec[:, :], pO[64:65, :], 1e-30,
                                                     None, op0=ALU.add),
                 reads=["pO"], writes=["rec"])
            p.op("vector", lambda v: v.reciprocal(rec[:, :], rec[:, :]),
                 reads=["rec"], writes=["rec"])
            p.op("tensor", lambda t: t.matmul(
                pB[:, :], ones_sb[:, 0:64], rec[:, :], start=True, stop=True),
                reads=["ones_sb", "rec"], writes=["pB"])
            p.op("vector", lambda v: v.tensor_copy(rb_sb[:, :], pB[:, :]),
                 reads=["pB"], writes=["rb_sb"])
            hp = h // 2
            hq = (h % 2) * 64
            p.op("vector", lambda v, hp=hp, hq=hq, qb=qb: v.tensor_tensor(
                otsb[hq:hq + 64, hp * TPC + qb * 512: hp * TPC + (qb + 1) * 512],
                pO[0:64, :], rb_sb[:, :], op=ALU.mult),
                reads=["pO", "rb_sb"], writes=[f"ot{hp}"])
    for tc in range(4):
        for mt in range(8):
            for ki in range(4):
                p.op("tensor", lambda t, tc=tc, mt=mt, ki=ki: t.matmul(
                    pq[:, :], wo_sb[:, ki * D + mt * 128: ki * D + (mt + 1) * 128],
                    otsb[:, ki * TPC + tc * 512: ki * TPC + (tc + 1) * 512],
                    start=(ki == 0), stop=(ki == 3)),
                    reads=["wo_sb", "ot0", "ot1", "ot2", "ot3"], writes=["pq"])
            if fio.get("fused"):
                p.dma("sync", stg[:, :],
                      fio["Y"][mt * 128:(mt + 1) * 128, tc * 512:(tc + 1) * 512],
                      writes=["stg"])
            else:
                p.dma("sync", stg[:, :],
                      yT[mt * 128:(mt + 1) * 128,
                         513 + tc * 512: 513 + (tc + 1) * 512],
                      writes=["stg"])
            p.op("vector", lambda v: v.tensor_tensor(
                sqf[:, :], pq[:, :], stg[:, :], op=ALU.add),
                reads=["pq", "stg"], writes=["sqf"])
            p.dma("sync", xmT[mt * 128:(mt + 1) * 128, tc * 512:(tc + 1) * 512],
                  sqf[:, :], reads=["sqf"])
            if fio.get("fused") and tc == 3:
                p.dma("sync", fio["xm1"][mt * 128:(mt + 1) * 128, 0:1],
                      sqf[:, 511:512], reads=["sqf"])
    if fio.get("standalone"):
        return ctx.finish()


def _l2rope(ctx, stg, kstg, scb, kscb, tab, trow, outb, ocol, kout, sqf, nh,
            tA, tB, rot, t1, r=0):
    p = ctx.p
    ksqf, knh, ktA, ktB, krot, kt1 = (f"sqf{r}", f"nh{r}", "tA", "tB",
                                      f"rot{r}", f"t1{r}")
    p.op("scalar", lambda s: s.activation(sqf[:, :], stg[:, :], AF.Square),
         reads=[kstg], writes=[ksqf])
    p.op("vector", lambda v: v.reduce_sum(
        nh[:, :], sqf[:, :].rearrange("p (h d) -> p h d", d=64), axis=AX.X),
        reads=[ksqf], writes=[knh])
    p.op("vector", lambda v: v.tensor_scalar(nh[:, :], nh[:, :], 1e-24, None,
                                             op0=ALU.add),
         reads=[knh], writes=[knh])
    p.op("vector", lambda v: v.reciprocal(nh[:, :], nh[:, :]),
         reads=[knh], writes=[knh])
    p.op("scalar", lambda s: s.activation(nh[:, :], nh[:, :], AF.Sqrt),
         reads=[knh], writes=[knh])
    for h in range(8):
        p.op("vector", lambda v, h=h: v.tensor_scalar(
            stg[:, h * 64:(h + 1) * 64], stg[:, h * 64:(h + 1) * 64],
            nh[:, h:h + 1], None, op0=ALU.mult),
            reads=[kstg, knh], writes=[kstg])
    p.op("vector", lambda v: v.tensor_tensor(stg[:, :], stg[:, :], scb[:, :],
                                             op=ALU.mult),
         reads=[kstg, kscb], writes=[kstg])
    p.dma("sync", tA[:, :], tab[0, trow:trow + 128, :], writes=[ktA])
    p.dma("sync", tB[:, :], tab[1, trow:trow + 128, :], writes=[ktB])
    p.op("vector", lambda v: v.tensor_copy(
        rot[:, :].rearrange("p (h x j) -> p h x j", h=8, x=2)[:, :, 0, :],
        stg[:, :].rearrange("p (h x j) -> p h x j", h=8, x=2)[:, :, 1, :]),
        reads=[kstg], writes=[krot])
    p.op("vector", lambda v: v.tensor_copy(
        rot[:, :].rearrange("p (h x j) -> p h x j", h=8, x=2)[:, :, 1, :],
        stg[:, :].rearrange("p (h x j) -> p h x j", h=8, x=2)[:, :, 0, :]),
        reads=[kstg], writes=[krot])
    p.op("vector", lambda v: v.tensor_tensor(t1[:, :], stg[:, :], tA[:, :],
                                             op=ALU.mult),
         reads=[kstg, ktA], writes=[kt1])
    p.op("vector", lambda v: v.tensor_tensor(rot[:, :], rot[:, :], tB[:, :],
                                             op=ALU.mult),
         reads=[krot, ktB], writes=[krot])
    p.op("vector", lambda v, ocol=ocol: v.tensor_tensor(
        outb[:, ocol:ocol + 512], t1[:, :], rot[:, :], op=ALU.add),
        reads=[kt1, krot], writes=[kout])


def build_l1ffn(fio=None):
    NMID = 22
    if fio is None:
        ctx = Ctx("l1ffn")
        fio = dict(ctx=ctx, standalone=True)
        fio["xm"] = ctx.inp("xm", [D, TPC + 1])
        fio["lng"] = ctx.inp("lng", [8, 128, 1])
        fio["lnb"] = ctx.inp("lnb", [8, 128, 1])
        fio["w1a"] = ctx.inp("w1a", [NMID, 128, 8, 128])
        fio["w1g"] = ctx.inp("w1g", [NMID, 128, 8, 128])
        fio["b1a"] = ctx.inp("b1a", [NMID, 128, 1])
        fio["b1g"] = ctx.inp("b1g", [NMID, 128, 1])
        fio["w2"] = ctx.inp("w2", [8, 128, NMID, 128])
        fio["b2"] = ctx.inp("b2", [8, 128, 1])
        fio["ones"] = ctx.inp("ones", [1, 128])
        fio["yout"] = ctx.out("yout", [D, TPC])
    ctx = fio["ctx"]
    nc, p = ctx.nc, ctx.p
    xm = fio["xm"]
    lng = fio["lng"]
    lnb = fio["lnb"]
    w1a = fio["w1a"]
    w1g = fio["w1g"]
    b1a = fio["b1a"]
    b1g = fio["b1g"]
    w2 = fio["w2"]
    b2 = fio["b2"]
    ones = fio["ones"]
    yout = fio["yout"]

    hT = ctx.sb("hT", [128, 8 * TPC], BF16)
    ones_sb = ctx.sb("ones_sb", [1, 128], F32)
    ones_bf = ctx.sb("ones_bf", [128, 1], BF16)
    lng_sb = ctx.sb("lng_sb", [128, 8], F32)
    lnb_sb = ctx.sb("lnb_sb", [128, 8], F32)
    b1a_sb = ctx.sb("b1a_sb", [128, NMID], F32)
    b1g_sb = ctx.sb("b1g_sb", [128, NMID], F32)
    b2_sb = ctx.sb("b2_sb", [128, 8], F32)
    w1abuf = [ctx.sb(f"w1a{i}", [128, 8 * 128], BF16) for i in range(2)]
    w1gbuf = [ctx.sb(f"w1g{i}", [128, 8 * 128], BF16) for i in range(2)]
    w2buf = [ctx.sb(f"w2_{i}", [128, NMID * 128], BF16) for i in range(2)]
    sqt = ctx.sb("sqt", [128, 512], BF16)
    mrow = ctx.sb("mrow", [1, 512], F32)
    vrow = ctx.sb("vrow", [1, 512], F32)
    m2row = ctx.sb("m2row", [1, 512], F32)
    nmb = ctx.sb("nmb", [128, 512], F32)
    rsb = ctx.sb("rsb", [128, 512], F32)
    t1f = ctx.sb("t1f", [128, 512], F32)
    xn_sb = ctx.sb("xn_sb", [128, 8 * 512], BF16)
    a_sb = ctx.sb("a_sb", [128, 1024], BF16)
    g_sb = ctx.sb("g_sb", [128, 1024], BF16)
    mid_sb = ctx.sb("mid_sb", [128, NMID * 512], BF16)
    xmst = ctx.sb("xmst", [128, 512], F32)
    out_sb = ctx.sb("out_sb", [128, 512], F32)
    pstats = ctx.ps("pstats", [64, 512], F32)
    pSum = pstats[0:1, :]
    pSq = pstats[32:33, :]
    pA2 = [ctx.ps(f"pA{i}", [128, 512], F32) for i in range(2)]
    pG2 = [ctx.ps(f"pG{i}", [128, 512], F32) for i in range(2)]
    pW = [ctx.ps(f"pW{i}", [128, 512], F32) for i in range(2)]
    pB1 = ctx.ps("pB1", [128, 512], F32)

    p.dma("sync", ones_sb[:, :], ones, writes=["ones_sb"])
    p.op("vector", lambda v: v.memset(ones_bf[:, :], 1.0), writes=["ones_bf"])
    p.dma("sync", lng_sb[:, :].rearrange("p (a o) -> p a o", o=1),
          lng.rearrange("a p o -> p a o"), writes=["lng_sb"],
          allow_slow_non_contiguous=True)
    p.dma("sync", lnb_sb[:, :].rearrange("p (a o) -> p a o", o=1),
          lnb.rearrange("a p o -> p a o"), writes=["lnb_sb"],
          allow_slow_non_contiguous=True)
    p.dma("sync", b1a_sb[:, :].rearrange("p (a o) -> p a o", o=1),
          b1a.rearrange("a p o -> p a o"), writes=["b1a_sb"],
          allow_slow_non_contiguous=True)
    p.dma("sync", b1g_sb[:, :].rearrange("p (a o) -> p a o", o=1),
          b1g.rearrange("a p o -> p a o"), writes=["b1g_sb"],
          allow_slow_non_contiguous=True)
    p.dma("sync", b2_sb[:, :].rearrange("p (a o) -> p a o", o=1),
          b2.rearrange("a p o -> p a o"), writes=["b2_sb"],
          allow_slow_non_contiguous=True)
    if fio.get("fused"):
        XMo = fio["XM"]
        edges1 = fio["edges1"]
        halo1 = ctx.sb("halo1", [128, 8], F32)
        h1st = ctx.sb("h1st", [128, 1], F32)
        hm1_sb = ctx.sb("hm1_sb", [128, 8], F32)
        p.dma("sync", hm1_sb[:, :], fio["hmask"], writes=["hm1_sb"])
        p.op("vector", lambda v: v.memset(halo1[:, :], 0.0), writes=["halo1"])
        for fi in range(4, 8):
            for r in range(8):
                p.dma("sync", h1st[:, :],
                      edges1[r * 1024 + fi * 128: r * 1024 + fi * 128 + 128, 0:1],
                      writes=["h1st"])
                p.op("vector", lambda v, fi=fi, r=r: v.scalar_tensor_tensor(
                    halo1[:, fi:fi + 1], h1st[:, :], hm1_sb[:, r:r + 1],
                    halo1[:, fi:fi + 1], op0=ALU.mult, op1=ALU.add),
                    reads=["h1st", "hm1_sb", "halo1"], writes=["halo1"])
        for fi in range(8):
            if fi < 4:
                p.dma("gpsimd", hT[:, fi * TPC:(fi + 1) * TPC],
                      XMo[fi * 128:(fi + 1) * 128, :], writes=["hT"])
            else:
                p.op("vector", lambda v, fi=fi: v.tensor_copy(
                    hT[:, fi * TPC: fi * TPC + 1], halo1[:, fi:fi + 1]),
                    reads=["halo1"], writes=["hT"])
                p.dma("gpsimd", hT[:, fi * TPC + 1:(fi + 1) * TPC],
                      XMo[fi * 128:(fi + 1) * 128, 0:TPC - 1], writes=["hT"])
    else:
        for fi in range(8):
            off = 1 if fi < 4 else 0
            p.dma("gpsimd", hT[:, fi * TPC:(fi + 1) * TPC],
                  xm[fi * 128:(fi + 1) * 128, off:off + TPC], writes=["hT"])

    for tc in range(4):
        tcol = tc * 512
        for fi in range(8):
            p.op("tensor", lambda t, fi=fi, tcol=tcol: t.matmul(
                pSum, ones_bf[:, :], hT[:, fi * TPC + tcol: fi * TPC + tcol + 512],
                start=(fi == 0), stop=(fi == 7)),
                reads=["ones_bf", "hT"], writes=["pSum"])
        for fi in range(8):
            p.op("scalar", lambda s, fi=fi, tcol=tcol: s.activation(
                sqt[:, :], hT[:, fi * TPC + tcol: fi * TPC + tcol + 512], AF.Square),
                reads=["hT"], writes=["sqt"])
            p.op("tensor", lambda t, fi=fi: t.matmul(
                pSq, ones_bf[:, :], sqt[:, :],
                start=(fi == 0), stop=(fi == 7)),
                reads=["ones_bf", "sqt"], writes=["pSq"])
        p.op("vector", lambda v: v.tensor_scalar(mrow[:, :], pSum, 1.0 / D,
                                                 None, op0=ALU.mult),
             reads=["pSum"], writes=["mrow"])
        p.op("vector", lambda v: v.tensor_tensor(m2row[:, :], mrow[:, :], mrow[:, :],
                                                 op=ALU.mult),
             reads=["mrow"], writes=["m2row"])
        p.op("vector", lambda v: v.scalar_tensor_tensor(
            vrow[:, :], pSq, 1.0 / D, m2row[:, :], op0=ALU.mult,
            op1=ALU.subtract),
            reads=["pSq", "m2row"], writes=["vrow"])
        ecol, ekey = ctx.cst(1e-5)
        p.op("scalar", lambda s, ecol=ecol: s.activation(vrow[:, :], vrow[:, :],
                                                         AF.Ln, bias=ecol[0:1, :]),
             reads=["vrow", ekey], writes=["vrow"])
        p.op("scalar", lambda s: s.activation(vrow[:, :], vrow[:, :], AF.Exp,
                                              scale=-0.5),
             reads=["vrow"], writes=["vrow"])
        p.op("vector", lambda v: v.tensor_scalar(mrow[:, :], mrow[:, :], -1.0, None,
                                                 op0=ALU.mult),
             reads=["mrow"], writes=["mrow"])
        p.op("tensor", lambda t: t.matmul(pB1[:, :], ones_sb[:, :], mrow[:, :],
                                          start=True, stop=True),
             reads=["ones_sb", "mrow"], writes=["pB1"])
        p.op("vector", lambda v: v.tensor_copy(nmb[:, :], pB1[:, :]),
             reads=["pB1"], writes=["nmb"])
        p.op("tensor", lambda t: t.matmul(pB1[:, :], ones_sb[:, :], vrow[:, :],
                                          start=True, stop=True),
             reads=["ones_sb", "vrow"], writes=["pB1"])
        p.op("vector", lambda v: v.tensor_copy(rsb[:, :], pB1[:, :]),
             reads=["pB1"], writes=["rsb"])
        for fi in range(8):
            p.op("vector", lambda v, fi=fi, tcol=tcol: v.tensor_tensor(
                t1f[:, :], hT[:, fi * TPC + tcol: fi * TPC + tcol + 512], nmb[:, :],
                op=ALU.add),
                reads=["hT", "nmb"], writes=["t1f"])
            p.op("vector", lambda v: v.tensor_tensor(t1f[:, :], t1f[:, :], rsb[:, :],
                                                     op=ALU.mult),
                 reads=["t1f", "rsb"], writes=["t1f"])
            p.op("scalar", lambda s, fi=fi: s.activation(
                xn_sb[:, fi * 512:(fi + 1) * 512], t1f[:, :], AF.Identity,
                bias=lnb_sb[:, fi:fi + 1], scale=lng_sb[:, fi:fi + 1]),
                reads=["t1f", "lng_sb", "lnb_sb"], writes=["xn_sb"])
        for mi in range(NMID):
            bb = mi % 2
            ba = w1abuf[bb]
            bg = w1gbuf[bb]
            pA = pA2[bb]
            pG = pG2[bb]
            ka, kg = f"w1a{bb}", f"w1g{bb}"
            p.dma("gpsimd", ba[:, :].rearrange("p (a b) -> p a b", a=8), w1a[mi],
                  writes=[ka])
            p.dma("gpsimd", bg[:, :].rearrange("p (a b) -> p a b", a=8), w1g[mi],
                  writes=[kg])
            for fi in range(8):
                p.op("tensor", lambda t, pA=pA, ba=ba, fi=fi, ka=ka, bb=bb: t.matmul(
                    pA[:, :], ba[:, fi * 128:(fi + 1) * 128],
                    xn_sb[:, fi * 512:(fi + 1) * 512],
                    start=(fi == 0), stop=(fi == 7)),
                    reads=[ka, "xn_sb"], writes=[f"pA{bb}"])
                p.op("tensor", lambda t, pG=pG, bg=bg, fi=fi, kg=kg, bb=bb: t.matmul(
                    pG[:, :], bg[:, fi * 128:(fi + 1) * 128],
                    xn_sb[:, fi * 512:(fi + 1) * 512],
                    start=(fi == 0), stop=(fi == 7)),
                    reads=[kg, "xn_sb"], writes=[f"pG{bb}"])
            p.op("scalar", lambda s, pA=pA, mi=mi, bb=bb: s.activation(
                a_sb[:, bb * 512:(bb + 1) * 512], pA[:, :], AF.Identity,
                bias=b1a_sb[:, mi:mi + 1]),
                reads=[f"pA{bb}", "b1a_sb"], writes=[f"a_sb{bb}"])
            p.op("scalar", lambda s, pG=pG, mi=mi, bb=bb: s.activation(
                g_sb[:, bb * 512:(bb + 1) * 512], pG[:, :], AF.Gelu,
                bias=b1g_sb[:, mi:mi + 1]),
                reads=[f"pG{bb}", "b1g_sb"], writes=[f"g_sb{bb}"])
            p.op("vector", lambda v, mi=mi, bb=bb: v.tensor_tensor(
                mid_sb[:, mi * 512:(mi + 1) * 512], a_sb[:, bb * 512:(bb + 1) * 512],
                g_sb[:, bb * 512:(bb + 1) * 512], op=ALU.mult),
                reads=[f"a_sb{bb}", f"g_sb{bb}"], writes=[f"mid{mi}"])
        for mt in range(8):
            bw = w2buf[mt % 2]
            kw = f"w2_{mt % 2}"
            p.dma("gpsimd", bw[:, :].rearrange("p (a b) -> p a b", a=NMID), w2[mt],
                  writes=[kw])
            r = mt % 2
            for mi in range(NMID):
                p.op("tensor", lambda t, bw=bw, mi=mi, r=r, kw=kw: t.matmul(
                    pW[r][:, :], bw[:, mi * 128:(mi + 1) * 128],
                    mid_sb[:, mi * 512:(mi + 1) * 512],
                    start=(mi == 0), stop=(mi == NMID - 1)),
                    reads=[kw, f"mid{mi}"], writes=[f"pW{r}"])
            if fio.get("fused"):
                p.dma("sync", xmst[:, :],
                      fio["XM"][mt * 128:(mt + 1) * 128, tcol: tcol + 512],
                      writes=["xmst"])
            else:
                p.dma("sync", xmst[:, :],
                      xm[mt * 128:(mt + 1) * 128, 1 + tcol: 1 + tcol + 512],
                      writes=["xmst"])
            p.op("vector", lambda v, mt=mt, r=r: v.scalar_tensor_tensor(
                out_sb[:, :], pW[r][:, :], b2_sb[:, mt:mt + 1], xmst[:, :],
                op0=ALU.add, op1=ALU.add),
                reads=[f"pW{r}", "b2_sb", "xmst"], writes=["out_sb"])
            p.dma("sync", yout[mt * 128:(mt + 1) * 128, tcol:tcol + 512],
                  out_sb[:, :], reads=["out_sb"])
            if fio.get("e513") is not None:
                if tc == 2:
                    p.dma("sync", fio["e513"][mt * 128:(mt + 1) * 128, 0:1],
                          out_sb[:, 511:512], reads=["out_sb"])
                elif tc == 3:
                    p.dma("sync", fio["e513"][mt * 128:(mt + 1) * 128, 1:513],
                          out_sb[:, :], reads=["out_sb"])
    if fio.get("standalone"):
        return ctx.finish()


_PROGS = {}


def _get(name, builder):
    if name not in _PROGS:
        _PROGS[name] = builder()
    return _PROGS[name]


_IDENT = np.eye(128, dtype=np.float32)
_ONES = np.ones((1, 128), np.float32)


def _tile_w(w, part=128):
    K, M = w.shape
    return np.ascontiguousarray(
        np.asarray(w, np.float32).reshape(K // part, part, M).transpose(1, 0, 2))


def _run(name, builder, in_maps, outs):
    nc = _get(name, builder)
    res = run_bass_kernel_spmd(nc, in_maps, core_ids=list(range(NCORES)))
    return [[r[o] for r in res.results] for o in outs]


def run_moe(xfull, wr, wg, wu, wd, sg, su, sd):
    in_maps = []
    for c in range(NCORES):
        in_maps.append(dict(
            x=_f32(xfull[c * TPC:(c + 1) * TPC]),
            wr=_f32(wr), wg=_f32(wg), wu=_f32(wu), wd=_f32(wd),
            sg=_f32(sg), su=_f32(su), sd=_f32(sd), ident=_IDENT))
    (xs,) = _run("moe", build_moe, in_maps, ["xout"])
    return np.concatenate(xs, axis=0)


def run_l0attn(xfull, qkv_w, out_w, bias_p, bias_a, rope_freqs):
    wtile = _tile_w(np.asarray(qkv_w, np.float32).T)
    sgn = np.concatenate([np.concatenate([-np.ones(32), np.ones(32)])] * 16)[None]
    sgn = sgn.astype(np.float32)
    in_maps = []
    for c in range(NCORES):
        posv = (np.arange(c * TPC, (c + 1) * TPC) % S).astype(np.float32)[None]
        in_maps.append(dict(
            x=_f32(xfull[c * TPC:(c + 1) * TPC]), wqkv=wtile,
            freqs=_f32(np.asarray(rope_freqs).reshape(1, 32)), pos=posv,
            sgnmask=sgn, ident=_IDENT, ones=_ONES))
    qs, ks, vs = _run("qkv", build_qkv, in_maps, ["qt_o", "kt_o", "v_o"])
    QT = np.concatenate(qs, axis=2)
    KT = np.concatenate(ks, axis=2)
    V = np.concatenate(vs, axis=1)
    dist_c = np.abs(np.arange(16384)[None, :] - 8064
                    - np.arange(128)[:, None]).astype(np.int16)
    bp = np.asarray(bias_p, np.float32).reshape(16)
    ba = np.asarray(bias_a, np.float32).reshape(16)
    in_maps = []
    for c in range(NCORES):
        h0 = 2 * c
        pa = np.stack([bp[h0:h0 + 2], ba[h0:h0 + 2]], axis=1).reshape(1, 4)
        in_maps.append(dict(qt=QT[h0:h0 + 2], kt=KT[h0:h0 + 2], vv=V[h0:h0 + 2],
                            pa=_f32(pa), dist=dist_c, ones=_ONES))
    (ots,) = _run("flash", build_flash, in_maps, ["ot"])
    OT = np.concatenate(ots, axis=0).reshape(1024, T)
    return run_outproj(OT, np.asarray(out_w, np.float32).T,
                       np.zeros((1, D), np.float32), xfull)


def run_outproj(OT, w_in_out, bias, xfull):
    owt = _tile_w(w_in_out)
    in_maps = []
    for c in range(NCORES):
        in_maps.append(dict(ot=np.ascontiguousarray(OT[:, c * TPC:(c + 1) * TPC]),
                            w=owt, bias=_f32(bias),
                            x=_f32(xfull[c * TPC:(c + 1) * TPC]), ones=_ONES))
    (xs,) = _run("outproj", build_outproj, in_maps, ["xout"])
    return np.concatenate(xs, axis=0)


def run_lnT(xfull):
    in_maps = [dict(x=_f32(xfull[c * TPC:(c + 1) * TPC]), ident=_IDENT)
               for c in range(NCORES)]
    (ys,) = _run("lnT", build_lnT, in_maps, ["yT"])
    return np.concatenate(ys, axis=1)


def run_addhalf(xfull, yT):
    in_maps = []
    for c in range(NCORES):
        in_maps.append(dict(x=_f32(xfull[c * TPC:(c + 1) * TPC]),
                            yT=np.ascontiguousarray(yT[:, c * TPC:(c + 1) * TPC]),
                            ident=_IDENT))
    (xs,) = _run("addhalf", build_addhalf, in_maps, ["xout"])
    return np.concatenate(xs, axis=0)


def _xpos_tables():
    n = np.arange(S, dtype=np.float64)
    half = np.arange(0, 64, 2, dtype=np.float64)
    inv_freq = 1.0 / 10000.0 ** (half / 64.0)
    fr = n[:, None] * inv_freq[None, :]
    fr = np.concatenate([fr, fr], axis=1)
    base = (half + 0.4 * 64) / (1.4 * 64)
    power = (n - S // 2) / 256.0
    sc = base[None, :] ** power[:, None]
    sc = np.concatenate([sc, sc], axis=1)
    cos, sin = np.cos(fr), np.sin(fr)
    sgn = np.concatenate([-np.ones(32), np.ones(32)])[None]
    Aq = cos * sc
    Bq = sgn * sin * sc
    Ak = cos / sc
    Bk = sgn * sin / sc
    def rep(t):
        return np.ascontiguousarray(np.tile(t, (1, 8)).astype(np.float32))
    return rep(Aq), rep(Bq), rep(Ak), rep(Bk)


_XPOS = None


def _l1_gates():
    g = np.zeros((2, 2, 4, 128, 512), np.float32)
    cc = np.arange(512)[None, :]
    pp = np.arange(128)[:, None]
    for kc in range(2):
        for kt in range(4):
            r = (cc + 512) - (kc * 512 + kt * 128 + pp)
            g[1, kc, kt] = ((r >= 0) & (r <= 512)).astype(np.float32)
    g[0] = g[1]
    g0 = g.copy()
    g0[0, 0] = 0.0
    return g.astype(BF), g0.astype(BF)


def run_l1layer(yT, wqkv_l, qs_l, ks_l, wo_l, lng_l, lnb_l, w1_l, b1_l, w2_l, b2_l):
    global _XPOS
    if _XPOS is None:
        _XPOS = _xpos_tables()
    Aq, Bq, Ak, Bk = _XPOS
    gates_n, gates_0 = _l1_gates()
    wq_t = _tile_w(wqkv_l)
    wo_t = _tile_w(np.asarray(wo_l, np.float32))
    qs_rep = np.ascontiguousarray(np.tile(np.asarray(qs_l, np.float32), 8)[None])
    ks_rep = np.ascontiguousarray(np.tile(np.asarray(ks_l, np.float32), 8)[None])
    KE = TPC + 512
    in_maps = []
    for c in range(NCORES):
        t0 = c * TPC
        bs = (t0 // S) * S
        ycore = np.zeros((D, KE + 1), np.float32)
        lo = max(t0 - 513, bs)
        ycore[:, lo - (t0 - 513):] = yT[:, lo:t0 + TPC]
        qpos = (np.arange(t0, t0 + TPC) % S)
        kpos = (np.arange(t0 - 512, t0 + TPC) % S)
        tabq = np.stack([Aq[qpos], Bq[qpos]])
        tabk = np.stack([Ak[kpos], Bk[kpos]])
        in_maps.append(dict(
            yT=ycore, wqkv=wq_t, wo=wo_t, qs=qs_rep, ks=ks_rep,
            tabq=tabq, tabk=tabk,
            gates=(gates_0 if t0 == bs else gates_n),
            ident=_IDENT, ones=_ONES))
    (xms,) = _run("l1attn", build_l1attn, in_maps, ["xmT"])
    xmT = np.concatenate(xms, axis=1)

    HID = 2730
    w1 = np.asarray(w1_l, np.float32)
    w1a = np.zeros((D, 2816), np.float32)
    w1g = np.zeros((D, 2816), np.float32)
    w1a[:, :HID] = w1[:, :HID]
    w1g[:, :HID] = w1[:, HID:]
    b1 = np.asarray(b1_l, np.float32)
    b1a = np.zeros(2816, np.float32)
    b1g = np.zeros(2816, np.float32)
    b1a[:HID] = b1[:HID]
    b1g[:HID] = b1[HID:]
    w2p = np.zeros((2816, D), np.float32)
    w2p[:HID] = np.asarray(w2_l, np.float32)
    w1a_t = np.ascontiguousarray(
        w1a.reshape(8, 128, 22, 128).transpose(2, 1, 0, 3))
    w1g_t = np.ascontiguousarray(
        w1g.reshape(8, 128, 22, 128).transpose(2, 1, 0, 3))
    w2_t = np.ascontiguousarray(
        w2p.reshape(22, 128, 8, 128).transpose(2, 1, 0, 3))
    lng_r = np.ascontiguousarray(np.asarray(lng_l, np.float32).reshape(8, 128, 1))
    lnb_r = np.ascontiguousarray(np.asarray(lnb_l, np.float32).reshape(8, 128, 1))
    b1a_r = np.ascontiguousarray(b1a.reshape(22, 128, 1))
    b1g_r = np.ascontiguousarray(b1g.reshape(22, 128, 1))
    b2_r = np.ascontiguousarray(np.asarray(b2_l, np.float32).reshape(8, 128, 1))
    in_maps = []
    for c in range(NCORES):
        t0 = c * TPC
        bs = (t0 // S) * S
        xcore = np.zeros((D, TPC + 1), np.float32)
        lo = max(t0 - 1, bs)
        xcore[:, lo - (t0 - 1):] = xmT[:, lo:t0 + TPC]
        in_maps.append(dict(xm=xcore, lng=lng_r, lnb=lnb_r, w1a=w1a_t, w1g=w1g_t,
                            b1a=b1a_r, b1g=b1g_r, w2=w2_t, b2=b2_r, ones=_ONES))
    (ys,) = _run("l1ffn", build_l1ffn, in_maps, ["yout"])
    return np.concatenate(ys, axis=1)


def run_l1ffn_sub(xfull, w):
    yT = run_lnT(xfull)
    for l in range(2):
        yT = run_l1layer(yT, w["wqkv"][l], w["q_scale"][l], w["k_scale"][l],
                         w["wo"][l], w["ln_g"][l], w["ln_b"][l], w["w1"][l],
                         w["b1"][l], w["w2"][l], w["b2"][l])
    return run_addhalf(xfull, yT)


def run_l1attn_sub(xfull, Wqkv, bqkv, gq, bq, gk, bk, Wo, bo):
    wtile = _tile_w(np.asarray(Wqkv, np.float32))
    gb = np.concatenate([_f32(gq).ravel(), _f32(bq).ravel(), _f32(gk).ravel(),
                         _f32(bk).ravel()])[None]
    in_maps = []
    for c in range(NCORES):
        in_maps.append(dict(x=_f32(xfull[c * TPC:(c + 1) * TPC]), wqkv=wtile,
                            bqkv=_f32(np.asarray(bqkv).reshape(1, 3072)),
                            gb=gb, ident=_IDENT, ones=_ONES))
    kq, kk, kkt, vt = _run("featmap", build_featmap, in_maps,
                           ["kqt_o", "kkt_o", "kk_o", "v_o"])
    kqT = np.concatenate(kq, axis=2).reshape(1024, T)
    kkT = np.concatenate(kk, axis=2).reshape(1024, T)
    kk_tok = np.concatenate(kkt, axis=0)
    v_tok = np.concatenate(vt, axis=0)
    masks = np.zeros((4, 128, 512), np.float32)
    ccc = np.arange(512)[None, :]
    ppp = np.arange(128)[:, None]
    for kt in range(4):
        masks[kt] = (kt * 128 + ppp <= ccc).astype(np.float32)
    masks = masks.astype(BF)
    in_maps = []
    for c in range(NCORES):
        h0 = 2 * c
        in_maps.append(dict(
            kqt=np.ascontiguousarray(kqT[h0 * 64:(h0 + 2) * 64]).reshape(2, 64, T),
            kkt=np.ascontiguousarray(kkT[h0 * 64:(h0 + 2) * 64]).reshape(2, 64, T),
            kk_tok=np.ascontiguousarray(kk_tok[:, c * 128:(c + 1) * 128]),
            v_tok=np.ascontiguousarray(v_tok[:, c * 128:(c + 1) * 128]),
            masks=masks, ones=_ONES))
    (ots,) = _run("linattn", build_linattn, in_maps, ["ot"])
    OT = np.concatenate(ots, axis=0).reshape(1024, T)
    return run_outproj(OT, np.asarray(Wo, np.float32),
                       np.asarray(bo, np.float32).reshape(1, D), xfull)


def kernel(L0_attn_x, L0_attn_qkv_w, L0_attn_out_w, L0_attn_bias_p, L0_attn_bias_a,
           L0_attn_rope_freqs, L0_ffn_Wr, L0_ffn_Wg, L0_ffn_Wu, L0_ffn_Wd,
           L0_ffn_Sg, L0_ffn_Su, L0_ffn_Sd, L1_attn_Wqkv, L1_attn_bqkv,
           L1_attn_gamma_q, L1_attn_beta_q, L1_attn_gamma_k, L1_attn_beta_k,
           L1_attn_Wo, L1_attn_bo, L1_ffn_wqkv, L1_ffn_q_scale, L1_ffn_k_scale,
           L1_ffn_wo, L1_ffn_ln_g, L1_ffn_ln_b, L1_ffn_w1, L1_ffn_b1, L1_ffn_w2,
           L1_ffn_b2):
    x = np.asarray(L0_attn_x, np.float32).reshape(T, D)
    moe = (L0_ffn_Wr, L0_ffn_Wg, L0_ffn_Wu, L0_ffn_Wd, L0_ffn_Sg, L0_ffn_Su,
           L0_ffn_Sd)
    x = run_moe(x, *moe)
    x = run_l0attn(x, L0_attn_qkv_w, L0_attn_out_w, L0_attn_bias_p, L0_attn_bias_a,
                   L0_attn_rope_freqs)
    x = run_moe(x, *moe)
    l1w = dict(wqkv=L1_ffn_wqkv, q_scale=L1_ffn_q_scale, k_scale=L1_ffn_k_scale,
               wo=L1_ffn_wo, ln_g=L1_ffn_ln_g, ln_b=L1_ffn_ln_b, w1=L1_ffn_w1,
               b1=L1_ffn_b1, w2=L1_ffn_w2, b2=L1_ffn_b2)
    x = run_l1ffn_sub(x, l1w)
    x = run_l1attn_sub(x, L1_attn_Wqkv, L1_attn_bqkv, L1_attn_gamma_q,
                       L1_attn_beta_q, L1_attn_gamma_k, L1_attn_beta_k, L1_attn_Wo,
                       L1_attn_bo)
    x = run_l1ffn_sub(x, l1w)
    return x.reshape(B, S, D).astype(np.float32)
